# version 05 (final)
# baseline (speedup 1.0000x reference)
import contextlib
import math
import numpy as np
import ml_dtypes
import concourse.bass as bass
import concourse.mybir as mybir
from concourse.bass_utils import run_bass_kernel_spmd

F32 = mybir.dt.float32
BF16 = mybir.dt.bfloat16
AF = mybir.ActivationFunctionType
ALU = mybir.AluOpType
AX = mybir.AxisListType
BF = ml_dtypes.bfloat16

NCORES = 8
B, T, D = 4, 8192, 1024
TS = T // NCORES
ROWS = B * TS
NTILES = ROWS // 128
EPS = 1e-5

ENGS = ("sync", "scalar", "vector", "tensor", "gpsimd")


class Plan:

    def __init__(self, nc, es):
        self.nc = nc
        self.es = es
        self.steps = {e: [] for e in ENGS}
        self.sems = {}
        self.counts = {}
        self.pending = {}
        self._last_fence = {}

    def sem(self, name):
        if name not in self.sems:
            self.sems[name] = self.es.enter_context(self.nc.semaphore(name))
            self.counts[name] = 0
        return self.sems[name]

    def val(self, name):
        self.sem(name)
        return self.counts[name]

    def fence(self):
        snap = [(n, c) for n, c in self.counts.items()
                if c > 0 and self._last_fence.get(n) != c]
        for n, c in snap:
            self._last_fence[n] = c
        for e in ENGS:
            self.pending[e] = list(self.pending.get(e, [])) + snap

    def step(self, eng, fn, waits=(), inc=None, inc_n=1):
        pf = self.pending.pop(eng, None)
        if pf:
            waits = pf + list(waits)
        w = [(self.sem(s), v) for (s, v) in waits]
        tgt = None
        if inc is not None:
            self.sem(inc)
            self.counts[inc] += inc_n
            tgt = (self.sems[inc], inc_n)
        self.steps[eng].append((fn, w, tgt))

    def dma(self, eng, out, in_, waits=(), inc="ld", **kw):
        self.step(eng, lambda e, o=out, i=in_: e.dma_start(out=o, in_=i, **kw),
                  waits, inc, 16)

    def wait(self, eng, waits):
        w = [(self.sem(s), v) for (s, v) in waits]
        self.steps[eng].append((None, w, None))

    def emit(self):
        with self.nc.Block() as block:
            for e in ENGS:
                steps = self.steps[e]
                if not steps:
                    continue

                def body(eng, steps=steps):
                    for fn, waits, tgt in steps:
                        for (s, v) in waits:
                            eng.wait_ge(s, v)
                        if fn is None:
                            continue
                        inst = fn(eng)
                        if tgt is not None:
                            inst.then_inc(*tgt)

                getattr(block, e)(body)


def emit_ln_tile(p, x_tile, xn_bf, scr, mu, sumsq, rstd, nmr, epsb=None,
                 waits=(), inc=None, D_=D):
    st = p.step
    st("vector", lambda e: e.reduce_sum(mu[:, 0:1], x_tile, axis=AX.X),
       waits=list(waits) + [("lnr", p.val("lnr"))])
    st("vector", lambda e: e.tensor_mul(scr[:, :], x_tile, x_tile), inc="lnm")
    p.wait("vector", [("lnm", p.val("lnm"))])
    st("vector", lambda e: e.reduce_sum(sumsq[:, 0:1], scr[:, :], axis=AX.X),
       inc="lnv")
    st("scalar", lambda e: e.activation(out=mu[:, 1:2], in_=mu[:, 0:1],
                                        func=AF.Copy, scale=-1.0 / D_),
       waits=[("lnv", p.val("lnv"))])
    st("scalar", lambda e: e.activation(out=sumsq[:, 1:2], in_=mu[:, 0:1],
                                        func=AF.Square, scale=1.0 / D_))
    st("scalar", lambda e: e.activation(out=rstd[:, 0:1], in_=sumsq[:, 0:1],
                                        func=AF.Copy, scale=1.0 / D_), inc="lnq")
    st("vector", lambda e: e.tensor_sub(sumsq[:, 0:1], rstd[:, 0:1], sumsq[:, 1:2]),
       waits=[("lnq", p.val("lnq"))], inc="lnw")
    st("scalar", lambda e: e.activation(out=rstd[:, 0:1], in_=sumsq[:, 0:1],
                                        func=AF.Ln, bias=epsb[:, 0:1]),
       waits=[("lnw", p.val("lnw"))], inc="lng")
    p.wait("scalar", [("lng", p.val("lng"))])
    st("scalar", lambda e: e.activation(out=rstd[:, 1:2], in_=rstd[:, 0:1],
                                        func=AF.Exp, scale=-0.5), inc="lnr")
    st("vector", lambda e: e.tensor_scalar(out=xn_bf, in0=x_tile,
                                           scalar1=mu[:, 1:2], scalar2=rstd[:, 1:2],
                                           op0=ALU.add, op1=ALU.mult),
       waits=[("lnr", p.val("lnr"))], inc=inc)


def np_split_rows(a):
    out = []
    for c in range(NCORES):
        out.append(np.ascontiguousarray(
            a[:, c * TS:(c + 1) * TS].reshape(ROWS, *a.shape[2:])))
    return out


def build_l1ffn(debug=False, ntiles=NTILES):
    nc = bass.Bass("TRN2", target_bir_lowering=False, debug=False)
    x = nc.dram_tensor("x", [ROWS, D], F32, kind="ExternalInput").ap()
    g1r = nc.dram_tensor("g1r", [1, 32, 4, 32], F32, kind="ExternalInput").ap()
    g2r = nc.dram_tensor("g2r", [4, 16, 1, 16], F32, kind="ExternalInput").ap()
    g1i = nc.dram_tensor("g1i", [1, 32, 4, 32], F32, kind="ExternalInput").ap()
    g2i = nc.dram_tensor("g2i", [4, 16, 1, 16], F32, kind="ExternalInput").ap()
    brl = nc.dram_tensor("brl", [1, 512], F32, kind="ExternalInput").ap()
    bim = nc.dram_tensor("bim", [1, 512], F32, kind="ExternalInput").ap()
    ident = nc.dram_tensor("ident", [128, 128], BF16, kind="ExternalInput").ap()
    xo = nc.dram_tensor("xo", [ROWS, D], F32, kind="ExternalOutput").ap()
    if debug:
        dbg_xn = nc.dram_tensor("dbg_xn", [128, D], F32, kind="ExternalOutput").ap()
        dbg_fm = nc.dram_tensor("dbg_fm", [128, D], F32, kind="ExternalOutput").ap()
        dbg_wc = nc.dram_tensor("dbg_wc", [8 * 128, D], F32, kind="ExternalOutput").ap()
        dbg_bb = nc.dram_tensor("dbg_bb", [128, D], F32, kind="ExternalOutput").ap()
        dbg_wt = nc.dram_tensor("dbg_wt", [1024, 256], F32, kind="ExternalOutput").ap()
        dbg_st = nc.dram_tensor("dbg_st", [128, 4], F32, kind="ExternalOutput").ap()

    wtmp = [nc.dram_tensor(f"wtmp{i}", [1024, 256], F32).ap() for i in range(2)]
    w4 = [nc.dram_tensor(f"w4_{i}", [32, 16, 32, 16], F32).ap() for i in range(2)]

    es = contextlib.ExitStack()
    with es:
        sb = lambda n, sh, dt=F32: es.enter_context(nc.sbuf_tensor(n, sh, dt))
        psum = lambda n, sh, dt=F32: es.enter_context(nc.psum_tensor(n, sh, dt))
        p = Plan(nc, es)

        sg1 = sb("sg1", [4, 1024])
        sg2 = sb("sg2", [4, 256])
        sg1b = sb("sg1b", [4, 1024], BF16)
        sg2b = sb("sg2b", [4, 256], BF16)
        wt_ps_full = psum("wt_ps", [128, 512])
        wt_ps = wt_ps_full[:, 0:256]
        wt_sb = sb("wt_sb", [128, 256])
        wc = [sb(f"wc{k}", [128, 1024], BF16) for k in range(8)]
        sident = sb("sident", [128, 128], BF16)
        bias_bc = sb("bias_bc", [128, 1024])

        p.dma("sync", sident[:, :], ident)
        for i, (g1, g2) in enumerate(((g1r, g2r), (g1i, g2i))):
            p.dma("sync", sg1.rearrange("c (b d) -> c b d", b=32),
                  g1.rearrange("one b c d -> c b d"),
                  waits=[("wcast", p.val("wcast"))], inc="ldw")
            p.dma("sync", sg2[:, :], g2.rearrange("c e one g -> c (e g)"), inc="ldw")
            p.step("vector", lambda e, i=i: e.tensor_copy(sg1b[:, :], sg1[:, :]),
                   waits=[("ldw", p.val("ldw")), ("wtmm", p.val("wtmm"))])
            p.step("vector", lambda e: e.tensor_copy(sg2b[:, :], sg2[:, :]),
                   inc="wcast")
            for m in range(8):
                p.step("tensor",
                       lambda e, m=m: e.matmul(wt_ps, sg1b[:, m * 128:(m + 1) * 128],
                                               sg2b[:, :], start=True, stop=True),
                       waits=[("wcast", p.val("wcast")), ("wtev", p.val("wtev"))],
                       inc="wtmm")
                p.step("vector",
                       lambda e, m=m: e.tensor_copy(
                           wt_sb.rearrange("p (g e) -> p g e", g=16),
                           wt_ps.rearrange("p (e g) -> p g e", g=16)),
                       waits=[("wtmm", p.val("wtmm")), ("wtst", p.val("wtst"))],
                       inc="wtev")
                p.dma("scalar", wtmp[i][m * 128:(m + 1) * 128, :], wt_sb[:, :],
                      waits=[("wtev", p.val("wtev"))], inc="wtst")
            w4src = wtmp[i].rearrange("(b d) (g e) -> d g b e", b=32, g=16)
            for q in range(32):
                eng = ("scalar", "sync")[q % 2]
                p.dma(eng, w4[i][q:q + 1], w4src[q:q + 1],
                      waits=[("wtst", p.val("wtst"))], inc="w4st")
        wcf = sb("wcf", [128, 512])
        for k in range(8):
            half = k // 4
            dk = (k % 4) * 8
            srcs = (0, 1) if half == 0 else (1, 0)
            for j, si in enumerate(srcs):
                src = w4[si].rearrange("d g b e -> (d g) (b e)")[
                    dk * 16:(dk + 8) * 16, :]
                p.dma("sync", wcf[:, :], src,
                      waits=[("w4st", p.val("w4st")),
                             ("ldwc", p.val("ldwc"))], inc="ldwcf")
                p.step("vector",
                       lambda e, k=k, j=j: e.tensor_copy(
                           wc[k][:, j * 512:(j + 1) * 512], wcf[:, :]),
                       waits=[("ldwcf", p.val("ldwcf"))], inc="ldwc")
        for k in range(4, 8):
            p.step("vector",
                   lambda e, k=k: e.tensor_scalar_mul(wc[k][:, 0:512],
                                                      wc[k][:, 0:512], -1.0),
                   waits=[("ldwc", p.val("ldwc"))], inc="wneg")
        p.dma("scalar", bias_bc[:, 0:512], brl.partition_broadcast(128), inc="ldb")
        p.dma("scalar", bias_bc[:, 512:1024], bim.partition_broadcast(128), inc="ldb")
        p.step("vector", lambda e: e.tensor_scalar_mul(bias_bc[:, :], bias_bc[:, :], 0.5),
               waits=[("ldb", p.val("ldb"))], inc="bprep")

        xt = [sb(f"xt{i}", [128, D]) for i in range(2)]
        scr = sb("scr", [128, D])
        xn_bf = sb("xn_bf", [128, D], BF16)
        xn_fm = sb("xn_fm", [128, D], BF16)
        mu = sb("mu", [128, 2]); sumsq = sb("sumsq", [128, 2])
        rstd = sb("rstd", [128, 2]); nmr = sb("nmr", [128, 2])
        epsb = sb("epsb", [128, 1])
        p.step("vector", lambda e: e.memset(epsb[:, :], EPS))
        tr_ps_full = [psum(f"tr_ps{i}", [128, 1024], BF16) for i in range(2)]
        tr_ps = [t[:, 0:128] for t in tr_ps_full]
        mm_ps = [psum(f"mm_ps{i}", [128, 512]) for i in range(2)]
        out_t = [sb(f"out_t{i}", [128, D]) for i in range(2)]

        wc_ready = [("wneg", p.val("wneg")), ("bprep", p.val("bprep")),
                    ("ldwc", p.val("ldwc")), ("ld", p.val("ld"))]
        if debug:
            dbg_sb = sb("dbg_sb", [128, D])
            for k in range(8):
                p.step("vector", lambda e, k=k: e.tensor_copy(
                    dbg_sb[:, :], wc[k][:, :]), waits=wc_ready, inc="dbgv")
                p.dma("scalar", dbg_wc[k * 128:(k + 1) * 128, :], dbg_sb[:, :],
                      waits=[("dbgv", p.val("dbgv"))], inc="dbgst")
            p.dma("scalar", dbg_bb, bias_bc[:, :], waits=wc_ready, inc="dbgst")
            p.dma("scalar", dbg_wt, wtmp[0], waits=wc_ready, inc="dbgst")
        for it in range(ntiles):
            buf = it % 2
            p.dma("sync", xt[buf][:, :], x[it * 128:(it + 1) * 128, :],
                  waits=[("oev", max(0, p.val("oev") - 1))], inc=f"ldx{buf}")
            emit_ln_tile(p, xt[buf][:, :], xn_bf[:, :], scr, mu, sumsq, rstd, nmr,
                         epsb, waits=[(f"ldx{buf}", (it // 2 + 1) * 16)], inc="lnd")
            if debug and it == 0:
                p.step("vector", lambda e: e.tensor_copy(dbg_sb[:, :], xn_bf[:, :]),
                       waits=[("lnd", p.val("lnd"))], inc="dbgv2")
                p.dma("scalar", dbg_xn, dbg_sb[:, :],
                      waits=[("dbgv2", p.val("dbgv2"))], inc="dbgst")
                p.step("vector", lambda e: e.tensor_copy(dbg_sb[:, 0:2], mu[:, :]),
                       waits=[("lnd", p.val("lnd"))])
                p.step("vector", lambda e: e.tensor_copy(dbg_sb[:, 2:4], rstd[:, :]),
                       inc="dbgv2")
                p.dma("scalar", dbg_st, dbg_sb[:, 0:4],
                      waits=[("dbgv2", p.val("dbgv2"))], inc="dbgst")
            for k in range(8):
                p.step("tensor",
                       lambda e, k=k, buf=buf: e.transpose(
                           tr_ps[k % 2], xn_bf[:, k * 128:(k + 1) * 128],
                           sident[:, :]),
                       waits=([("lnd", p.val("lnd"))] if k == 0 else
                              [("trev", p.val("trev") - 1)]) + (wc_ready if it == 0 and k == 0 else []),
                       inc="trmm")
                p.step("vector",
                       lambda e, k=k: e.tensor_copy(xn_fm[:, k * 128:(k + 1) * 128],
                                                    tr_ps[k % 2]),
                       waits=[("trmm", p.val("trmm"))], inc="trev")
            if debug and it == 0:
                p.step("vector", lambda e: e.tensor_copy(dbg_sb[:, :], xn_fm[:, :]),
                       waits=[("trev", p.val("trev"))], inc="dbgv3")
                p.dma("scalar", dbg_fm, dbg_sb[:, :],
                      waits=[("dbgv3", p.val("dbgv3"))], inc="dbgst")
            for h in range(2):
                for k in range(8):
                    p.step("tensor",
                           lambda e, h=h, k=k, buf=buf: e.matmul(
                               mm_ps[h][:, :], xn_fm[:, k * 128:(k + 1) * 128],
                               wc[k][:, h * 512:(h + 1) * 512],
                               start=(k == 0), stop=(k == 7)),
                           waits=([("trev", p.val("trev"))] if k == 0 else []) +
                                 ([("oev", p.val("oev"))] if (k == 0 and it > 0) else []),
                           inc=("mmd" if k == 7 else None))
            for h in range(2):
                p.step("vector",
                       lambda e, h=h, buf=buf: e.scalar_tensor_tensor(
                           out=out_t[buf][:, h * 512:(h + 1) * 512],
                           in0=mm_ps[h][:, :], scalar=0.5,
                           in1=xt[buf][:, h * 512:(h + 1) * 512],
                           op0=ALU.mult, op1=ALU.add),
                       waits=[("mmd", p.val("mmd"))] +
                             ([(f"stx{buf}", (it // 2) * 16)] if h == 0 else []),
                       inc=("sttg" if h == 1 else None))
            p.wait("vector", [("sttg", p.val("sttg"))])
            p.step("vector",
                   lambda e, buf=buf: e.tensor_add(out_t[buf][:, :], out_t[buf][:, :],
                                                   bias_bc[:, :]),
                   inc="oev")
            p.dma("scalar", xo[it * 128:(it + 1) * 128, :], out_t[buf][:, :],
                  waits=[("oev", p.val("oev"))], inc=f"stx{buf}")
        final = [("stx0", p.val("stx0")), ("stx1", p.val("stx1"))]
        if debug:
            final.append(("dbgst", p.val("dbgst")))
        p.wait("sync", final)
        p.emit()
    return nc


_L1FFN_CACHE = {}


def run_l1ffn(x_full, G1_R, G2_R, G1_I, G2_I, b_r, b_i):
    if "nc" not in _L1FFN_CACHE:
        _L1FFN_CACHE["nc"] = build_l1ffn()
    nc = _L1FFN_CACHE["nc"]
    ident = np.eye(128, dtype=BF)
    shards = np_split_rows(x_full)
    in_maps = []
    for c in range(NCORES):
        in_maps.append(dict(x=shards[c], g1r=G1_R, g2r=G2_R, g1i=G1_I, g2i=G2_I,
                            brl=b_r.reshape(1, 512), bim=b_i.reshape(1, 512),
                            ident=ident))
    res = run_bass_kernel_spmd(nc, in_maps, core_ids=list(range(NCORES)))
    out = np.empty((B, T, D), np.float32)
    for c in range(NCORES):
        out[:, c * TS:(c + 1) * TS] = res.results[c]["xo"].reshape(B, TS, D)
    return out




def emit_wT(p, nc, es, name, w_dram, wt_sb, sident, n=D, m=D, row_scale=None,
            eng_dma="scalar", ps=None, extra_waits=(), sp=None):
    if sp is None:
        sp = name
    sb = lambda nm, sh, dt=F32: es.enter_context(nc.sbuf_tensor(nm, sh, dt))
    wld = sb(f"{name}_ld", [128, m])
    wbf = sb(f"{name}_bf", [128, m], BF16)
    if ps is None:
        ps = [es.enter_context(nc.psum_tensor(f"{name}_ps{i}", [128, 1024], BF16))
              for i in range(2)]
    nb = n // 128
    mb = m // 128
    for i in range(nb):
        p.dma(eng_dma, wld[:, :], w_dram[i * 128:(i + 1) * 128, :],
              waits=[(f"{sp}cast", p.val(f"{sp}cast"))], inc=f"{sp}ld")
        p.step("vector", lambda e: e.tensor_copy(wbf[:, :], wld[:, :]),
               waits=[(f"{sp}ld", p.val(f"{sp}ld")),
                      (f"{sp}tr", p.val(f"{sp}tr"))], inc=f"{sp}cast")
        for j in range(mb):
            sl = p.val(f"{sp}tr") % 2
            p.step("tensor",
                   lambda e, i=i, j=j, sl=sl: e.transpose(
                       ps[sl][:, 0:128],
                       wbf[:, j * 128:(j + 1) * 128], sident[:, :]),
                   waits=[(f"{sp}cast", p.val(f"{sp}cast")),
                          (f"{sp}ev", max(0, p.val(f"{sp}ev") - 1)),
                          ("ld", 16)] + (list(extra_waits) if (i == 0 and j == 0) else []),
                   inc=f"{sp}tr")
            p.step("vector",
                   lambda e, i=i, j=j, sl=sl: e.tensor_copy(
                       wt_sb[j][:, i * 128:(i + 1) * 128],
                       ps[sl][:, 0:128]),
                   waits=[(f"{sp}tr", p.val(f"{sp}tr"))], inc=f"{sp}ev")
    if row_scale is not None:
        for j in range(mb):
            p.step("vector",
                   lambda e, j=j: e.tensor_scalar_mul(wt_sb[j][:, :], wt_sb[j][:, :],
                                                      row_scale[j]),
                   waits=[(f"{sp}ev", p.val(f"{sp}ev"))], inc=f"{sp}sc")


def build_l1attn_proj():
    nc = bass.Bass("TRN2", target_bir_lowering=False, debug=False)
    x = nc.dram_tensor("x", [ROWS, D], F32, kind="ExternalInput").ap()
    wq = nc.dram_tensor("wq", [D, D], F32, kind="ExternalInput").ap()
    wk = nc.dram_tensor("wk", [D, D], F32, kind="ExternalInput").ap()
    wv = nc.dram_tensor("wv", [D, D], F32, kind="ExternalInput").ap()
    ident = nc.dram_tensor("ident", [128, 128], BF16, kind="ExternalInput").ap()
    qf = nc.dram_tensor("qf", [D, ROWS], BF16, kind="ExternalOutput").ap()
    kf = nc.dram_tensor("kf", [D, ROWS], BF16, kind="ExternalOutput").ap()
    vt = nc.dram_tensor("vt", [ROWS, D], BF16, kind="ExternalOutput").ap()

    es = contextlib.ExitStack()
    with es:
        sb = lambda n_, sh, dt=F32: es.enter_context(nc.sbuf_tensor(n_, sh, dt))
        psum = lambda n_, sh, dt=F32: es.enter_context(nc.psum_tensor(n_, sh, dt))
        p = Plan(nc, es)
        sident = sb("sident", [128, 128], BF16)
        p.dma("sync", sident[:, :], ident)
        wq_sb = [sb(f"wq{i}", [128, D], BF16) for i in range(8)]
        wk_sb = [sb(f"wk{i}", [128, D], BF16) for i in range(8)]
        wv_sb = [sb(f"wv{i}", [128, D], BF16) for i in range(8)]
        wld = sb("wld", [128, D])
        for nm, dram, tiles in (("q", wq, wq_sb), ("k", wk, wk_sb), ("v", wv, wv_sb)):
            for i in range(8):
                p.dma("scalar", wld[:, :], dram[i * 128:(i + 1) * 128, :],
                      waits=[("wcst", p.val("wcst"))], inc="wld")
                p.step("vector",
                       lambda e, t=tiles[i]: e.tensor_copy(t[:, :], wld[:, :]),
                       waits=[("wld", p.val("wld"))], inc="wcst")
        xt = [sb(f"xt{i}", [128, D]) for i in range(2)]
        scr = sb("scr", [128, D])
        xn_bf = sb("xn_bf", [128, D], BF16)
        xn_fm = [sb(f"xnfm{j}", [128, ROWS], BF16) for j in range(8)]
        mu = sb("mu", [128, 2]); sumsq = sb("sumsq", [128, 2])
        rstd = sb("rstd", [128, 2]); epsb = sb("epsb", [128, 1])
        p.step("vector", lambda e: e.memset(epsb[:, :], EPS))
        tr_ps = [es.enter_context(nc.psum_tensor(f"trp{i}", [128, 1024], BF16))
                 for i in range(2)]
        for it in range(NTILES):
            buf = it % 2
            p.dma("sync", xt[buf][:, :], x[it * 128:(it + 1) * 128, :],
                  waits=[("lnd", max(0, p.val("lnd") - 1))], inc=f"ldx{buf}")
            emit_ln_tile(p, xt[buf][:, :], xn_bf[:, :], scr, mu, sumsq, rstd, None,
                         epsb, waits=[(f"ldx{buf}", (it // 2 + 1) * 16)], inc="lnd")
            for k in range(8):
                p.step("tensor",
                       lambda e, k=k: e.transpose(
                           tr_ps[k % 2][:, 0:128], xn_bf[:, k * 128:(k + 1) * 128],
                           sident[:, :]),
                       waits=([("lnd", p.val("lnd")), ("ld", 16)] if k == 0 else
                              [("trev", p.val("trev") - 1)]),
                       inc="trmm")
                p.step("vector",
                       lambda e, k=k, it=it: e.tensor_copy(
                           xn_fm[k][:, it * 128:(it + 1) * 128], tr_ps[k % 2][:, 0:128]),
                       waits=[("trmm", p.val("trmm"))], inc="trev")
        prq = [("wcst", p.val("wcst")), ("trev", p.val("trev"))]
        mm_ps = [psum(f"mmp{i}", [128, 512]) for i in range(4)]
        ev = sb("ev", [128, 512], BF16)
        evv = sb("evv", [128, 512], BF16)
        for nm, tiles, out in (("q", wq_sb, qf), ("k", wk_sb, kf)):
            for mtile in range(8):
                for nchunk in range(ROWS // 512):
                    pb = mm_ps[p.val("pmm") % 4]
                    for kk in range(8):
                        p.step("tensor",
                               lambda e, pb=pb, kk=kk, mtile=mtile, nchunk=nchunk,
                                      tiles=tiles: e.matmul(
                                   pb[:, :], tiles[kk][:, mtile * 128:(mtile + 1) * 128],
                                   xn_fm[kk][:, nchunk * 512:(nchunk + 1) * 512],
                                   start=(kk == 0), stop=(kk == 7)),
                               waits=(prq if p.val("pmm") == 0 and kk == 0 else
                                      ([("pev", max(0, p.val("pev") - 2))] if kk == 0 else [])),
                               inc=("pmm" if kk == 7 else None))
                    g = p.val("pev")
                    slot = g % 2
                    buf_ev = ev if slot else evv
                    p.step("vector",
                           lambda e, pb=pb, buf_ev=buf_ev: e.tensor_copy(buf_ev[:, :], pb[:, :]),
                           waits=[("pmm", p.val("pmm")),
                                  (f"pst{slot}", (g // 2) * 16)], inc="pev")
                    p.dma("sync",
                          out[mtile * 128:(mtile + 1) * 128,
                              nchunk * 512:(nchunk + 1) * 512],
                          buf_ev[:, :], waits=[("pev", p.val("pev"))], inc=f"pst{slot}")
        for ttile in range(NTILES):
            for half in range(2):
                pb = mm_ps[p.val("pmm") % 4]
                for kk in range(8):
                    p.step("tensor",
                           lambda e, pb=pb, kk=kk, ttile=ttile, half=half: e.matmul(
                               pb[:, :], xn_fm[kk][:, ttile * 128:(ttile + 1) * 128],
                               wv_sb[kk][:, half * 512:(half + 1) * 512],
                               start=(kk == 0), stop=(kk == 7)),
                           waits=([("pev", max(0, p.val("pev") - 2))] if kk == 0 else []),
                           inc=("pmm" if kk == 7 else None))
                g = p.val("pev")
                slot = g % 2
                buf_ev = ev if slot else evv
                p.step("vector",
                       lambda e, pb=pb, buf_ev=buf_ev: e.tensor_copy(buf_ev[:, :], pb[:, :]),
                       waits=[("pmm", p.val("pmm")),
                              (f"pst{slot}", (g // 2) * 16)], inc="pev")
                p.dma("sync",
                      vt[ttile * 128:(ttile + 1) * 128, half * 512:(half + 1) * 512],
                      buf_ev[:, :], waits=[("pev", p.val("pev"))], inc=f"pst{slot}")
        p.wait("sync", [("pst0", p.val("pst0")), ("pst1", p.val("pst1"))])
        p.emit()
    return nc


def build_l1attn_attn():
    QR = 4096
    NQB, NKC = 4, 8
    nc = bass.Bass("TRN2", target_bir_lowering=False, debug=False)
    qf = nc.dram_tensor("qf", [D, QR], BF16, kind="ExternalInput").ap()
    kf = nc.dram_tensor("kf", [D, T], BF16, kind="ExternalInput").ap()
    vt = nc.dram_tensor("vt", [T, D], BF16, kind="ExternalInput").ap()
    x = nc.dram_tensor("x", [QR, D], F32, kind="ExternalInput").ap()
    xo = nc.dram_tensor("xo", [QR, D], F32, kind="ExternalOutput").ap()

    es = contextlib.ExitStack()
    with es:
        sb = lambda n_, sh, dt=F32: es.enter_context(nc.sbuf_tensor(n_, sh, dt))
        psum = lambda n_, sh, dt=F32: es.enter_context(nc.psum_tensor(n_, sh, dt))
        p = Plan(nc, es)
        ones_bf = sb("ones_bf", [128, 1], BF16)
        p.step("vector", lambda e: e.memset(ones_bf[:, :], 1.0), inc="ones")
        qf_sb = [[sb(f"qf{s}_{dk}", [128, 1024], BF16) for dk in range(8)]
                 for s in range(2)]
        kf_sb = [[sb(f"kf{s}_{dk}", [128, 1024], BF16) for dk in range(8)]
                 for s in range(2)]
        vt_sb = [[sb(f"vt{s}_{kt}", [128, 1024], BF16) for kt in range(8)]
                 for s in range(2)]
        st_sb = [sb(f"st{kt}", [128, 1024], BF16) for kt in range(8)]
        o_sb = [sb(f"osb{qt}", [128, 1024]) for qt in range(8)]
        xt_sb = [sb(f"xts{qt}", [128, D]) for qt in range(8)]
        rec = sb("rec", [128, 8])
        outb = [sb(f"outb{i}", [128, D]) for i in range(2)]
        st_ps = [psum(f"stps{i}", [128, 512]) for i in range(2)]
        av_ps = [psum(f"avps{i}", [128, 512]) for i in range(4)]
        dn_ps = psum("dnps", [128, 8])

        for qb in range(NQB):
            qs = qb % 2
            for dk in range(8):
                p.dma("sync", qf_sb[qs][dk][:, :],
                      qf[dk * 128:(dk + 1) * 128, qb * 1024:(qb + 1) * 1024],
                      waits=([("qk", 16 * NKC * (qb - 1))] if qb >= 2 else []),
                      inc=f"ldq{qs}")
            for qt in range(8):
                p.dma("scalar", xt_sb[qt][:, :],
                      x[qb * 1024 + qt * 128:qb * 1024 + (qt + 1) * 128, :],
                      waits=([("resv", 8 * qb)] if qb >= 1 else []), inc="ldxt")
            for kc in range(NKC):
                g = qb * NKC + kc
                ks = g % 2
                for dk in range(8):
                    p.dma("sync", kf_sb[ks][dk][:, :],
                          kf[dk * 128:(dk + 1) * 128, kc * 1024:(kc + 1) * 1024],
                          waits=([("qk", 16 * (g - 1))] if g >= 2 else []),
                          inc=f"ldk{ks}")
                for kt in range(8):
                    p.dma("scalar", vt_sb[ks][kt][:, :],
                          vt[kc * 1024 + kt * 128:kc * 1024 + (kt + 1) * 128, :],
                          waits=([("av", 16 * (g - 1))] if g >= 2 else []),
                          inc=f"ldv{ks}")
                for kt in range(8):
                    for qn in range(2):
                        pb = st_ps[p.val("qk") % 2]
                        for dk in range(8):
                            w = []
                            if dk == 0:
                                w = [(f"ldk{ks}", ((g // 2) + 1) * 128),
                                     (f"ldq{qs}", ((qb // 2) + 1) * 128),
                                     ("expd", max(0, p.val("qk") - 1))]
                            p.step("tensor",
                                   lambda e, pb=pb, dk=dk, kt=kt, qn=qn, ks=ks, qs=qs:
                                   e.matmul(pb[:, :],
                                            kf_sb[ks][dk][:, kt * 128:(kt + 1) * 128],
                                            qf_sb[qs][dk][:, qn * 512:(qn + 1) * 512],
                                            start=(dk == 0), stop=(dk == 7)),
                                   waits=w, inc=("qk" if dk == 7 else None))
                        p.step("scalar",
                               lambda e, pb=pb, kt=kt, qn=qn: e.activation(
                                   out=st_sb[kt][:, qn * 512:(qn + 1) * 512],
                                   in_=pb[:, :], func=AF.Exp, scale=1.0 / 32.0),
                               waits=[("qk", p.val("qk"))] +
                                     ([("avch", g)] if g >= 1 and kt == 0 and qn == 0 else []),
                               inc="expd")
                for qt in range(8):
                    for kt in range(8):
                        first = (kc == 0 and qt == 0 and kt == 0)
                        last = (kc == NKC - 1 and qt == 7 and kt == 7)
                        w = []
                        if qt == 0 and kt == 0:
                            w = [("expd", p.val("expd")), ("ones", 1)]
                            if first and qb >= 1:
                                w.append(("recd", qb))
                        p.step("tensor",
                               lambda e, qt=qt, kt=kt, first=first, last=last:
                               e.matmul(dn_ps[:, qt:qt + 1],
                                        st_sb[kt][:, qt * 128:(qt + 1) * 128],
                                        ones_bf[:, :], start=first, stop=last,
                                        skip_group_check=True),
                               waits=w, inc=None)
                for qt in range(8):
                    for dv in range(2):
                        pb = av_ps[p.val("av") % 4]
                        for kt in range(8):
                            w = []
                            if kt == 0:
                                w = [(f"ldv{ks}", ((g // 2) + 1) * 128),
                                     ("ave", max(0, p.val("av") - 3))]
                                if qt == 0 and dv == 0:
                                    w.append(("expd", p.val("expd")))
                                    if kc == 0 and qb >= 1:
                                        w.append(("resv", 8 * qb))
                            p.step("tensor",
                                   lambda e, pb=pb, kt=kt, qt=qt, dv=dv, ks=ks:
                                   e.matmul(pb[:, :],
                                            st_sb[kt][:, qt * 128:(qt + 1) * 128],
                                            vt_sb[ks][kt][:, dv * 512:(dv + 1) * 512],
                                            start=(kt == 0), stop=(kt == 7)),
                                   waits=w, inc=("av" if kt == 7 else None))
                        if kc == 0:
                            p.step("vector",
                                   lambda e, pb=pb, qt=qt, dv=dv: e.tensor_copy(
                                       o_sb[qt][:, dv * 512:(dv + 1) * 512], pb[:, :]),
                                   waits=[("av", p.val("av"))], inc="ave")
                        else:
                            p.step("vector",
                                   lambda e, pb=pb, qt=qt, dv=dv: e.tensor_add(
                                       o_sb[qt][:, dv * 512:(dv + 1) * 512],
                                       o_sb[qt][:, dv * 512:(dv + 1) * 512], pb[:, :]),
                                   waits=[("av", p.val("av"))], inc="ave")
                p.step("vector", lambda e: e.tensor_copy(ones_bf[:, :], ones_bf[:, :]),
                       waits=[], inc="avch")
            p.step("vector", lambda e: e.reciprocal(rec[:, :], dn_ps[:, :]),
                   waits=[("qk", p.val("qk")), ("expd", p.val("expd")),
                          ("ave", p.val("ave"))], inc="recd")
            for qt in range(8):
                idx = qb * 8 + qt
                slot = idx % 2
                ob = outb[slot]
                p.step("vector",
                       lambda e, qt=qt, ob=ob: e.scalar_tensor_tensor(
                           out=ob[:, :], in0=o_sb[qt][:, :], scalar=rec[:, qt:qt + 1],
                           in1=xt_sb[qt][:, :], op0=ALU.mult, op1=ALU.add),
                       waits=[("recd", p.val("recd")), ("ldxt", (qb + 1) * 128),
                              (f"stres{slot}", 16 * (idx // 2))],
                       inc="resv")
                p.dma("sync", xo[qb * 1024 + qt * 128:qb * 1024 + (qt + 1) * 128, :],
                      ob[:, :], waits=[("resv", p.val("resv"))], inc=f"stres{slot}")
        p.wait("sync", [("stres0", p.val("stres0")), ("stres1", p.val("stres1"))])
        p.emit()
    return nc


def build_l0ffn_proj():
    nc = bass.Bass("TRN2", target_bir_lowering=False, debug=False)
    x = nc.dram_tensor("x", [ROWS, D], F32, kind="ExternalInput").ap()
    nw = nc.dram_tensor("nw", [1, D], F32, kind="ExternalInput").ap()
    nb = nc.dram_tensor("nb", [1, D], F32, kind="ExternalInput").ap()
    wq = nc.dram_tensor("wq", [D, D], F32, kind="ExternalInput").ap()
    wk = nc.dram_tensor("wk", [D, D], F32, kind="ExternalInput").ap()
    wv = nc.dram_tensor("wv", [D, D], F32, kind="ExternalInput").ap()
    bq = nc.dram_tensor("bq", [1, D], F32, kind="ExternalInput").ap()
    bk = nc.dram_tensor("bk", [1, D], F32, kind="ExternalInput").ap()
    bv = nc.dram_tensor("bv", [1, D], F32, kind="ExternalInput").ap()
    ident = nc.dram_tensor("ident", [128, 128], BF16, kind="ExternalInput").ap()
    qf = nc.dram_tensor("qf", [D, ROWS], BF16, kind="ExternalOutput").ap()
    kf = nc.dram_tensor("kf", [D, ROWS], BF16, kind="ExternalOutput").ap()
    vt = nc.dram_tensor("vt", [ROWS, D], BF16, kind="ExternalOutput").ap()

    es = contextlib.ExitStack()
    with es:
        sb = lambda n_, sh, dt=F32: es.enter_context(nc.sbuf_tensor(n_, sh, dt))
        psum = lambda n_, sh, dt=F32: es.enter_context(nc.psum_tensor(n_, sh, dt))
        p = Plan(nc, es)
        sident = sb("sident", [128, 128], BF16)
        p.dma("sync", sident[:, :], ident)
        wcol = sb("wcol", [128, 8])
        nbcol = sb("nbcol", [128, 8], BF16)
        nbf = sb("nbf", [128, 8])
        for j in range(8):
            p.dma("sync", wcol[:, j:j + 1],
                  nw[0:1, j * 128:(j + 1) * 128].rearrange("one b -> b one"),
                  inc="ldnw")
            p.dma("sync", nbf[:, j:j + 1],
                  nb[0:1, j * 128:(j + 1) * 128].rearrange("one b -> b one"),
                  inc="ldnw")
        p.step("vector", lambda e: e.tensor_copy(nbcol[:, :], nbf[:, :]),
               waits=[("ldnw", p.val("ldnw"))], inc="nbc")
        weff = {}
        wt_ps_sh = [es.enter_context(nc.psum_tensor(f"wt_ps_sh{i}", [128, 1024], BF16))
                    for i in range(2)]
        prev = None
        for nm, wd in (("q", wq), ("k", wk), ("v", wv)):
            weff[nm] = [sb(f"weff_{nm}{j}", [128, D], BF16) for j in range(8)]
            ew = [(f"w{prev}ev", p.val(f"w{prev}ev"))] if prev else []
            emit_wT(p, nc, es, f"w{nm}", wd, weff[nm], sident, ps=wt_ps_sh,
                    extra_waits=ew)
            prev = nm
        be_ps = psum("be_ps", [128, 512])
        brow = sb("brow", [1, D])
        be_sb = {}
        for nm, bd in (("q", bq), ("k", bk), ("v", bv)):
            be_sb[nm] = sb(f"be_{nm}", [1, D], BF16)
            p.dma("scalar", brow[0:1, :], bd, waits=[("bev", p.val("bev"))], inc="ldb")
            for half in range(2):
                for kk in range(8):
                    p.step("tensor",
                           lambda e, kk=kk, half=half, nm=nm: e.matmul(
                               be_ps[0:1, :], nbcol[:, kk:kk + 1],
                               weff[nm][kk][:, half * 512:(half + 1) * 512],
                               start=(kk == 0), stop=(kk == 7)),
                           waits=([("nbc", p.val("nbc")),
                                   (f"w{nm}ev", p.val(f"w{nm}ev")),
                                   ("bevv", p.val("bevv"))] if kk == 0 else []),
                           inc=("bemm" if kk == 7 else None))
                p.step("vector",
                       lambda e, half=half, nm=nm: e.tensor_add(
                           be_sb[nm][0:1, half * 512:(half + 1) * 512],
                           be_ps[0:1, :], brow[0:1, half * 512:(half + 1) * 512]),
                       waits=[("bemm", p.val("bemm")), ("ldb", p.val("ldb"))],
                       inc="bevv")
            p.step("vector", lambda e: e.tensor_copy(brow[0:1, 0:1], brow[0:1, 0:1]),
                   waits=[("bevv", p.val("bevv"))], inc="bev")
            for j in range(8):
                p.step("vector",
                       lambda e, j=j, nm=nm: e.tensor_scalar_mul(
                           weff[nm][j][:, :], weff[nm][j][:, :], wcol[:, j:j + 1]),
                       waits=[("bemm", p.val("bemm")), ("ldnw", p.val("ldnw"))],
                       inc=f"w{nm}sc")
        ones_bf = sb("ones_row", [1, 128], BF16)
        ones_512 = sb("ones_512", [1, 512], BF16)
        p.step("vector", lambda e: e.memset(ones_bf[:, :], 1.0))
        p.step("vector", lambda e: e.memset(ones_512[:, :], 1.0), inc="ones", inc_n=1)

        xt = [sb(f"xt{i}", [128, D]) for i in range(2)]
        z1 = sb("z1", [128, D])
        scr = sb("scr", [128, D])
        xn_bf = sb("xn_bf", [128, D], BF16)
        xn_fm = [sb(f"xnfm{j}", [128, ROWS], BF16) for j in range(8)]
        mu = sb("mu", [128, 2]); sumsq = sb("sumsq", [128, 2])
        rstd = sb("rstd", [128, 2]); epsb = sb("epsb", [128, 1])
        mu2 = sb("mu2", [128, 2]); sumsq2 = sb("sumsq2", [128, 2])
        rstd2 = sb("rstd2", [128, 2])
        p.step("vector", lambda e: e.memset(epsb[:, :], EPS))
        tr_ps = [es.enter_context(nc.psum_tensor(f"trp{i}", [128, 1024], BF16))
                 for i in range(2)]
        for it in range(NTILES):
            buf = it % 2
            p.dma("sync", xt[buf][:, :], x[it * 128:(it + 1) * 128, :],
                  waits=[("lnd", max(0, p.val("lnd") - 1))], inc=f"ldx{buf}")
            emit_ln_tile(p, xt[buf][:, :], z1[:, :], scr, mu, sumsq, rstd, None,
                         epsb, waits=[(f"ldx{buf}", (it // 2 + 1) * 16)], inc="ln1")
            emit_ln_tile(p, z1[:, :], xn_bf[:, :], scr, mu2, sumsq2, rstd2, None,
                         epsb, waits=[("ln1", p.val("ln1"))], inc="lnd")
            for k in range(8):
                p.step("tensor",
                       lambda e, k=k: e.transpose(
                           tr_ps[k % 2][:, 0:128], xn_bf[:, k * 128:(k + 1) * 128],
                           sident[:, :]),
                       waits=([("lnd", p.val("lnd")), ("ld", 16)] if k == 0 else
                              [("trev", p.val("trev") - 1)]),
                       inc="trmm")
                p.step("vector",
                       lambda e, k=k, it=it: e.tensor_copy(
                           xn_fm[k][:, it * 128:(it + 1) * 128], tr_ps[k % 2][:, 0:128]),
                       waits=[("trmm", p.val("trmm"))], inc="trev")
        prq = [("bev", p.val("bev")), ("trev", p.val("trev")), ("ones", 1),
               ("wqsc", p.val("wqsc")), ("wksc", p.val("wksc")),
               ("wvsc", p.val("wvsc"))]
        mm_ps = [psum(f"mmp{i}", [128, 512]) for i in range(3)]
        ev = sb("ev", [128, 512], BF16)
        evv = sb("evv", [128, 512], BF16)
        for nm, out, tm in (("q", qf, False), ("k", kf, False), ("v", vt, True)):
            n_out = NTILES if tm else 8
            for mtile in range(n_out):
                for nchunk in range(2 if tm else ROWS // 512):
                    pb = mm_ps[p.val("pmm") % 3]
                    for kk in range(8):
                        if tm:
                            fn = (lambda e, pb=pb, kk=kk, mtile=mtile, nchunk=nchunk, nm=nm:
                                  e.matmul(pb[:, :],
                                           xn_fm[kk][:, mtile * 128:(mtile + 1) * 128],
                                           weff[nm][kk][:, nchunk * 512:(nchunk + 1) * 512],
                                           start=(kk == 0), stop=False))
                        else:
                            fn = (lambda e, pb=pb, kk=kk, mtile=mtile, nchunk=nchunk, nm=nm:
                                  e.matmul(pb[:, :],
                                           weff[nm][kk][:, mtile * 128:(mtile + 1) * 128],
                                           xn_fm[kk][:, nchunk * 512:(nchunk + 1) * 512],
                                           start=(kk == 0), stop=False))
                        p.step("tensor", fn,
                               waits=(prq if p.val("pmm") == 0 and kk == 0 else
                                      ([("pev", max(0, p.val("pev") - 1))] if kk == 0 else [])),
                               inc=None)
                    if tm:
                        bfn = (lambda e, pb=pb, mtile=mtile, nchunk=nchunk, nm=nm:
                               e.matmul(pb[:, :], ones_bf[0:1, :],
                                        be_sb[nm][0:1, nchunk * 512:(nchunk + 1) * 512],
                                        start=False, stop=True))
                    else:
                        bfn = (lambda e, pb=pb, mtile=mtile, nm=nm:
                               e.matmul(pb[:, :],
                                        be_sb[nm][0:1, mtile * 128:(mtile + 1) * 128],
                                        ones_512[0:1, :], start=False, stop=True))
                    p.step("tensor", bfn, inc="pmm")
                    g = p.val("pev")
                    slot = g % 2
                    buf_ev = ev if slot else evv
                    p.step("vector",
                           lambda e, pb=pb, buf_ev=buf_ev: e.tensor_copy(buf_ev[:, :], pb[:, :]),
                           waits=[("pmm", p.val("pmm")), (f"pst{slot}", (g // 2) * 16)],
                           inc="pev")
                    dst = (out[mtile * 128:(mtile + 1) * 128,
                               nchunk * 512:(nchunk + 1) * 512])
                    p.dma("sync", dst, buf_ev[:, :],
                          waits=[("pev", p.val("pev"))], inc=f"pst{slot}")
        p.wait("sync", [("pst0", p.val("pst0")), ("pst1", p.val("pst1"))])
        p.emit()
    return nc


def build_l0ffn_attn():
    NP_, NQB, NKC = 4, 8, 8
    QBS = 1024
    nc = bass.Bass("TRN2", target_bir_lowering=False, debug=False)
    qf4 = nc.dram_tensor("qf4", [NP_, 128, T], BF16, kind="ExternalInput").ap()
    kf4 = nc.dram_tensor("kf4", [NP_, 128, T], BF16, kind="ExternalInput").ap()
    vt4 = nc.dram_tensor("vt4", [NP_, T, 128], BF16, kind="ExternalInput").ap()
    ident = nc.dram_tensor("ident", [128, 128], BF16, kind="ExternalInput").ap()
    af = nc.dram_tensor("af", [NP_, 128, T], BF16, kind="ExternalOutput").ap()

    es = contextlib.ExitStack()
    with es:
        sb = lambda n_, sh, dt=F32: es.enter_context(nc.sbuf_tensor(n_, sh, dt))
        psum = lambda n_, sh, dt=F32: es.enter_context(nc.psum_tensor(n_, sh, dt))
        p = Plan(nc, es)
        sident = sb("sident", [128, 128], BF16)
        p.dma("sync", sident[:, :], ident)
        qf_sb = [sb(f"qfs{s}", [128, T], BF16) for s in range(2)]
        kf_sb = [sb(f"kfs{s}", [128, 1024], BF16) for s in range(2)]
        vt_sb = [[sb(f"vts{s}_{kt}", [128, 132], BF16) for kt in range(8)]
                 for s in range(2)]
        for s in range(2):
            for kt in range(8):
                p.step("vector",
                       lambda e, s=s, kt=kt: e.memset(vt_sb[s][kt][:, 128:132], 1.0),
                       inc="vones")
        st_sb = [sb(f"st{kt}", [128, QBS], BF16) for kt in range(8)]
        ob_sb = sb("ob_sb", [128, 132], BF16)
        mkr = sb("mkr", [1, 4])
        p.step("vector", lambda e: e.memset(mkr[:, :], 0.0), inc="mkri")
        af_sb = [sb(f"afs{i}", [128, QBS], BF16) for i in range(2)]
        rec_sb = sb("rec_sb", [128, 16])
        st_ps = [psum(f"stps{i}", [128, 512]) for i in range(2)]
        av_big = es.enter_context(nc.psum_tensor("av_big", [128, 1536]))
        av_ps = [av_big[:, (s // 3) * 512 + (s % 3) * 160:
                        (s // 3) * 512 + (s % 3) * 160 + 132] for s in range(8)]
        tr_ps = [es.enter_context(nc.psum_tensor(f"trp{i}", [128, 1024], BF16))
                 for i in range(2)]

        for pr in range(NP_):
            ps_ = pr % 2
            p.dma("sync", qf_sb[ps_][:, :], qf4[pr],
                  waits=([("qkv", 16 * NKC * NQB * (pr - 1))] if pr >= 2 else []),
                  inc=f"ldq{ps_}")
            for qb in range(NQB):
                for kc in range(NKC):
                    g = (pr * NQB + qb) * NKC + kc
                    ks = g % 2
                    p.dma("sync", kf_sb[ks][:, :],
                          kf4[pr][:, kc * 1024:(kc + 1) * 1024],
                          waits=([("qkv", 16 * (g - 1))] if g >= 2 else []),
                          inc=f"ldk{ks}")
                    for kt in range(8):
                        p.dma("scalar", vt_sb[ks][kt][:, 0:128],
                              vt4[pr][kc * 1024 + kt * 128:kc * 1024 + (kt + 1) * 128, :],
                              waits=([("avv", 16 * (g - 1))] if g >= 2 else
                                     ([("vones", p.val("vones"))] if g == 0 else [])),
                              inc=f"ldv{ks}")
                    for kt in range(8):
                        for qn in range(QBS // 512):
                            pb = st_ps[p.val("qkv") % 2]
                            p.step("tensor",
                                   lambda e, pb=pb, kt=kt, qn=qn, ks=ks, ps_=ps_, qb=qb:
                                   e.matmul(pb[:, :],
                                            kf_sb[ks][:, kt * 128:(kt + 1) * 128],
                                            qf_sb[ps_][:, qb * QBS + qn * 512:
                                                       qb * QBS + (qn + 1) * 512],
                                            start=True, stop=True),
                                   waits=[(f"ldk{ks}", ((g // 2) + 1) * 16),
                                          (f"ldq{ps_}", ((pr // 2) + 1) * 16),
                                          ("expd", max(0, p.val("qkv") - 1))] +
                                         ([("avch", g)] if (kt == 0 and qn == 0 and g >= 1) else []),
                                   inc="qkv")
                            p.step("scalar",
                                   lambda e, pb=pb, kt=kt, qn=qn: e.activation(
                                       out=st_sb[kt][:, qn * 512:(qn + 1) * 512],
                                       in_=pb[:, :], func=AF.Exp, scale=1.0 / 32.0),
                                   waits=[("qkv", p.val("qkv"))], inc="expd")
                    for qs2 in range(8):
                        for kt in range(8):
                            w = []
                            if kt == 0:
                                w = [(f"ldv{ks}", ((g // 2) + 1) * 128)]
                                if qs2 == 0:
                                    w.append(("expd", p.val("expd")))
                                    if kc == 0:
                                        w.append(("nrm", p.val("nrm")))
                            p.step("tensor",
                                   lambda e, qs2=qs2, kt=kt, ks=ks, kc=kc: e.matmul(
                                       av_ps[qs2],
                                       st_sb[kt][:, qs2 * 128:(qs2 + 1) * 128],
                                       vt_sb[ks][kt][:, :],
                                       start=(kc == 0 and kt == 0 and qs2 % 3 == 0),
                                       stop=(kc == NKC - 1 and kt == 7),
                                       skip_group_check=True),
                                   waits=w, inc=("avv" if kt == 7 and qs2 == 7 else None),
                                   inc_n=16)
                    p.step("scalar", lambda e: e.copy(mkr[0:1, 0:1], mkr[0:1, 2:3]),
                           waits=[("avv", p.val("avv")), ("mkri", 1)], inc="avch")
                for qs2 in range(8):
                    p.step("vector",
                           lambda e, qs2=qs2: e.reciprocal(rec_sb[:, qs2:qs2 + 1],
                                                           av_ps[qs2][:, 128:129]),
                           waits=[("avv", p.val("avv")), ("trd", max(0, p.val("trd") - 1))],
                           inc="recq")
                    p.wait("vector", [("recq", p.val("recq"))])
                    p.step("vector",
                           lambda e, qs2=qs2: e.tensor_scalar_mul(
                               ob_sb[:, 0:128], av_ps[qs2][:, 0:128],
                               rec_sb[:, qs2:qs2 + 1]),
                           waits=[("trq", p.val("trq"))], inc="nrmv")
                    p.step("tensor",
                           lambda e, qs2=qs2: e.transpose(
                               tr_ps[qs2 % 2][:, 0:128],
                               ob_sb[:, 0:128], sident[:, :]),
                           waits=[("nrmv", p.val("nrmv")), ("ld", 16),
                                  ("trd", max(0, p.val("trd") - 1))], inc="trq")
                    gq = pr * NQB + qb
                    p.step("vector",
                           lambda e, qs2=qs2, qb=qb: e.tensor_copy(
                               af_sb[qb % 2][:, qs2 * 128:(qs2 + 1) * 128],
                               tr_ps[qs2 % 2][:, 0:128]),
                           waits=[("trq", p.val("trq"))] +
                                 ([(f"staf{qb % 2}", 16 * (gq // 2))] if qs2 == 0 else []),
                           inc="trd")
                p.step("vector", lambda e: e.tensor_copy(rec_sb[:, 0:1], rec_sb[:, 0:1]),
                       waits=[("trd", p.val("trd"))], inc="nrm")
                p.dma("sync", af[pr][:, qb * QBS:(qb + 1) * QBS], af_sb[qb % 2][:, :],
                      waits=[("nrm", p.val("nrm")),
                             (f"staf{qb % 2}", 16 * ((pr * NQB + qb) // 2))],
                      inc=f"staf{qb % 2}")
        p.wait("sync", [("staf0", p.val("staf0")), ("staf1", p.val("staf1"))])
        p.emit()
    return nc


def build_l0ffn_oproj():
    nc = bass.Bass("TRN2", target_bir_lowering=False, debug=False)
    x = nc.dram_tensor("x", [ROWS, D], F32, kind="ExternalInput").ap()
    af = nc.dram_tensor("af", [D, ROWS], BF16, kind="ExternalInput").ap()
    ow = nc.dram_tensor("ow", [D, D], F32, kind="ExternalInput").ap()
    ob = nc.dram_tensor("ob", [1, D], F32, kind="ExternalInput").ap()
    ident = nc.dram_tensor("ident", [128, 128], BF16, kind="ExternalInput").ap()
    xo = nc.dram_tensor("xo", [ROWS, D], F32, kind="ExternalOutput").ap()

    es = contextlib.ExitStack()
    with es:
        sb = lambda n_, sh, dt=F32: es.enter_context(nc.sbuf_tensor(n_, sh, dt))
        psum = lambda n_, sh, dt=F32: es.enter_context(nc.psum_tensor(n_, sh, dt))
        p = Plan(nc, es)
        sident = sb("sident", [128, 128], BF16)
        p.dma("sync", sident[:, :], ident)
        owt = [sb(f"owt{j}", [128, D], BF16) for j in range(8)]
        emit_wT(p, nc, es, "wo", ow, owt, sident)
        obf = sb("obf", [1, D])
        ob_bf = sb("ob_bf", [1, D], BF16)
        p.dma("scalar", obf[0:1, :], ob, inc="ldob")
        p.step("vector", lambda e: e.tensor_copy(ob_bf[0:1, :], obf[0:1, :]),
               waits=[("ldob", p.val("ldob"))], inc="obc")
        ones_bf = sb("ones_row", [1, 128], BF16)
        p.step("vector", lambda e: e.memset(ones_bf[:, :], 1.0), inc="ones")
        af_sb = [sb(f"afsb{k}", [128, ROWS], BF16) for k in range(8)]
        for k in range(8):
            p.dma("sync", af_sb[k][:, :], af[k * 128:(k + 1) * 128, :], inc="ldaf")

        xt = [sb(f"xt{i}", [128, D]) for i in range(2)]
        z1 = sb("z1", [128, D])
        scr = sb("scr", [128, D])
        base = [sb(f"base{i}", [128, D]) for i in range(2)]
        onew = [sb(f"onew{i}", [128, D]) for i in range(2)]
        mu = sb("mu", [128, 2]); sumsq = sb("sumsq", [128, 2])
        rstd = sb("rstd", [128, 2]); epsb = sb("epsb", [128, 1])
        p.step("vector", lambda e: e.memset(epsb[:, :], EPS))
        mm_ps = [psum(f"mmp{i}", [128, 512]) for i in range(4)]
        mmrdy = [("woev", p.val("woev")), ("obc", p.val("obc")), ("ones", 1),
                 ("ldaf", p.val("ldaf"))]
        for it in range(NTILES):
            buf = it % 2
            p.dma("sync", xt[buf][:, :], x[it * 128:(it + 1) * 128, :],
                  waits=[("bsg", max(0, p.val("bsg") - 1))], inc=f"ldx{buf}")
            emit_ln_tile(p, xt[buf][:, :], z1[:, :], scr, mu, sumsq, rstd, None,
                         epsb, waits=[(f"ldx{buf}", (it // 2 + 1) * 16)], inc="lnd")
            p.step("vector",
                   lambda e, buf=buf: e.scalar_tensor_tensor(
                       out=base[buf][:, :], in0=z1[:, :], scalar=0.5,
                       in1=xt[buf][:, :], op0=ALU.mult, op1=ALU.add),
                   waits=[("lnd", p.val("lnd"))], inc="bsg")
            for h in range(2):
                pb = mm_ps[p.val("pmm") % 4]
                for kk in range(8):
                    p.step("tensor",
                           lambda e, pb=pb, kk=kk, it=it, h=h: e.matmul(
                               pb[:, :], af_sb[kk][:, it * 128:(it + 1) * 128],
                               owt[kk][:, h * 512:(h + 1) * 512],
                               start=(kk == 0), stop=False),
                           waits=(mmrdy if p.val("pmm") == 0 and kk == 0 else
                                  ([("pev", max(0, p.val("pev") - 2))] if kk == 0 else [])),
                           inc=None)
                p.step("tensor",
                       lambda e, pb=pb, h=h: e.matmul(
                           pb[:, :], ones_bf[0:1, :],
                           ob_bf[0:1, h * 512:(h + 1) * 512], start=False, stop=True),
                       inc="pmm")
                p.step("vector",
                       lambda e, pb=pb, buf=buf, h=h: e.scalar_tensor_tensor(
                           out=onew[buf][:, h * 512:(h + 1) * 512], in0=pb[:, :],
                           scalar=0.5, in1=base[buf][:, h * 512:(h + 1) * 512],
                           op0=ALU.mult, op1=ALU.add),
                       waits=[("pmm", p.val("pmm")), ("bsg", p.val("bsg"))] +
                             ([(f"stx{buf}", 16 * (it // 2))] if h == 0 else []),
                       inc="pev")
            p.dma("scalar", xo[it * 128:(it + 1) * 128, :], onew[buf][:, :],
                  waits=[("pev", p.val("pev"))], inc=f"stx{buf}")
        p.wait("sync", [("stx0", p.val("stx0")), ("stx1", p.val("stx1"))])
        p.emit()
    return nc


DI = 2048
DTR = 64
DST = 16
SEG = TS + 2


def build_mamba_pre():
    nc = bass.Bass("TRN2", target_bir_lowering=False, debug=False)
    x = nc.dram_tensor("x", [ROWS, D], F32, kind="ExternalInput").ap()
    xh = nc.dram_tensor("xh", [8, D], F32, kind="ExternalInput").ap()
    win = nc.dram_tensor("win", [D, 2 * DI], F32, kind="ExternalInput").ap()
    wcv = nc.dram_tensor("wcv", [DI, 3], F32, kind="ExternalInput").ap()
    wx = nc.dram_tensor("wx", [DI, 96], F32, kind="ExternalInput").ap()
    ident = nc.dram_tensor("ident", [128, 128], BF16, kind="ExternalInput").ap()
    u_o = nc.dram_tensor("u_o", [DI, ROWS], BF16, kind="ExternalOutput").ap()
    res_o = nc.dram_tensor("res_o", [DI, ROWS], BF16, kind="ExternalOutput").ap()
    xdbl_o = nc.dram_tensor("xdbl_o", [ROWS, 96], F32, kind="ExternalOutput").ap()

    es = contextlib.ExitStack()
    with es:
        sb = lambda n_, sh, dt=F32: es.enter_context(nc.sbuf_tensor(n_, sh, dt))
        psum = lambda n_, sh, dt=F32: es.enter_context(nc.psum_tensor(n_, sh, dt))
        p = Plan(nc, es)
        sident = sb("sident", [128, 128], BF16)
        p.dma("sync", sident[:, :], ident)
        xt = [sb(f"xt{i}", [128, D]) for i in range(2)]
        z1 = sb("z1", [128, D])
        scr = sb("scr", [128, D])
        xn_bf = sb("xn_bf", [128, D], BF16)
        xn_fm = [sb(f"xnfm{j}", [128, 4 * SEG], BF16) for j in range(8)]
        mu = sb("mu", [128, 2]); sumsq = sb("sumsq", [128, 2])
        rstd = sb("rstd", [128, 2]); epsb = sb("epsb", [128, 1])
        p.step("vector", lambda e: e.memset(epsb[:, :], EPS))
        tr_ps = [es.enter_context(nc.psum_tensor(f"trp{i}", [128, 1024], BF16))
                 for i in range(2)]
        xth = sb("xth", [8, D])
        p.dma("sync", xth[:, :], xh, inc="ldxh")
        emit_ln_tile(p, xth[:, :], xn_bf[0:8, :], scr[0:8, :], mu[0:8, :],
                     sumsq[0:8, :], rstd[0:8, :], None, epsb[0:8, :],
                     waits=[("ldxh", 16)], inc="lnd", D_=D)
        for k in range(8):
            p.step("tensor",
                   lambda e, k=k: e.transpose(tr_ps[k % 2][:, 0:8],
                                              xn_bf[0:8, k * 128:(k + 1) * 128],
                                              sident[0:8, 0:8]),
                   waits=([("lnd", p.val("lnd")), ("ld", 16)] if k == 0 else
                          [("trev", p.val("trev") - 1)]),
                   inc="trmm")
            p.step("vector",
                   lambda e, k=k: e.tensor_copy(
                       xn_fm[k].rearrange("p (b s) -> p b s", b=4)[:, :, 0:2],
                       tr_ps[k % 2][:, 0:8].rearrange("p (b t) -> p b t", b=4)),
                   waits=[("trmm", p.val("trmm"))], inc="trev")
        for it in range(NTILES):
            buf = it % 2
            b_, lt = it // 8, it % 8
            p.dma("sync", xt[buf][:, :], x[it * 128:(it + 1) * 128, :],
                  waits=[("lnd", max(0, p.val("lnd") - 1))], inc=f"ldx{buf}")
            emit_ln_tile(p, xt[buf][:, :], xn_bf[:, :], scr, mu, sumsq, rstd, None,
                         epsb, waits=[(f"ldx{buf}", (it // 2 + 1) * 16)], inc="lnd")
            for k in range(8):
                col = b_ * SEG + 2 + lt * 128
                p.step("tensor",
                       lambda e, k=k: e.transpose(tr_ps[k % 2][:, 0:128],
                                                  xn_bf[:, k * 128:(k + 1) * 128],
                                                  sident[:, :]),
                       waits=([("lnd", p.val("lnd"))] if k == 0 else
                              [("trev", p.val("trev") - 1)]),
                       inc="trmm")
                p.step("vector",
                       lambda e, k=k, col=col: e.tensor_copy(
                           xn_fm[k][:, col:col + 128], tr_ps[k % 2][:, 0:128]),
                       waits=[("trmm", p.val("trmm"))], inc="trev")
        winb = [sb(f"winb{k}", [128, DI], BF16) for k in range(8)]
        wld = sb("wld", [128, DI])
        ccol = sb("ccol", [128, 3])
        wxb = sb("wxb", [128, 96], BF16)
        wxl = sb("wxl", [128, 96])
        u_raw = sb("u_raw", [128, 4 * SEG])
        u_cv = sb("u_cv", [128, ROWS])
        u_out = [sb(f"uout{i}", [128, ROWS], BF16) for i in range(2)]
        mm_ps = [psum(f"mmp{i}", [128, 456]) for i in range(4)]
        xd_ps = [psum(f"xdp{i}", [128, 96]) for i in range(2)]
        for half, dst in ((0, u_o), (1, res_o)):
            for k in range(8):
                p.dma("scalar", wld[:, :],
                      win[k * 128:(k + 1) * 128, half * DI:(half + 1) * DI],
                      waits=[("wcst", p.val("wcst"))], inc="wld")
                p.step("vector",
                       lambda e, k=k: e.tensor_copy(winb[k][:, :], wld[:, :]),
                       waits=[("wld", p.val("wld")), ("umm", p.val("umm"))],
                       inc="wcst")
            for dt_ in range(16):
                ub = dt_ % 2
                if half == 0:
                    p.dma("scalar", ccol[:, :], wcv[dt_ * 128:(dt_ + 1) * 128, :],
                          waits=[("cvd", p.val("cvd"))], inc="ldcc")
                for nch in range(9):
                    cw = min(456, 4 * SEG - nch * 456)
                    pb = mm_ps[p.val("umm") % 4]
                    for kk in range(8):
                        p.step("tensor",
                               lambda e, pb=pb, kk=kk, dt_=dt_, nch=nch, cw=cw, half=half:
                               e.matmul(pb[:, 0:cw],
                                        winb[kk][:, dt_ * 128:(dt_ + 1) * 128],
                                        xn_fm[kk][:, nch * 456:nch * 456 + cw],
                                        start=(kk == 0), stop=(kk == 7)),
                               waits=([("wcst", p.val("wcst")), ("trev", p.val("trev")),
                                       ("uev", max(0, p.val("uev") - 2))] if kk == 0 else []),
                               inc=("umm" if kk == 7 else None))
                    warw = ([("cv3", p.val("cv3"))] if half == 0 else
                            [("cvd", p.val("cvd"))]) if nch == 0 else []
                    p.step("vector",
                           lambda e, pb=pb, nch=nch, cw=cw: e.tensor_copy(
                               u_raw[:, nch * 456:nch * 456 + cw], pb[:, 0:cw]),
                           waits=[("umm", p.val("umm"))] + warw, inc="uev")
                if half == 1:
                    p.step("vector",
                           lambda e, ub=ub: e.tensor_copy(
                               u_out[ub].rearrange("p (b t) -> p b t", b=4),
                               u_raw.rearrange("p (b s) -> p b s", b=4)[:, :, 2:SEG]),
                           waits=[("uev", p.val("uev")),
                                  (f"stu{ub}", 16 * (((half * 16 + dt_) // 2)))],
                           inc="cvd")
                    p.dma("sync", dst[dt_ * 128:(dt_ + 1) * 128, :], u_out[ub][:, :],
                          waits=[("cvd", p.val("cvd"))], inc=f"stu{ub}")
                    continue
                for b_ in range(4):
                    s0 = b_ * SEG
                    o0 = b_ * TS
                    p.step("vector",
                           lambda e, s0=s0, o0=o0: e.tensor_scalar_mul(
                               u_cv[:, o0:o0 + TS], u_raw[:, s0 + 2:s0 + 2 + TS],
                               ccol[:, 2:3]),
                           waits=[("uev", p.val("uev")), ("ldcc", p.val("ldcc"))],
                           inc="cv1")
                    p.wait("vector", [("cv1", p.val("cv1"))])
                    p.step("vector",
                           lambda e, s0=s0, o0=o0: e.scalar_tensor_tensor(
                               out=u_cv[:, o0:o0 + TS], in0=u_raw[:, s0 + 1:s0 + 1 + TS],
                               scalar=ccol[:, 1:2], in1=u_cv[:, o0:o0 + TS],
                               op0=ALU.mult, op1=ALU.add), inc="cv2")
                    p.wait("vector", [("cv2", p.val("cv2"))])
                    p.step("vector",
                           lambda e, s0=s0, o0=o0: e.scalar_tensor_tensor(
                               out=u_cv[:, o0:o0 + TS], in0=u_raw[:, s0:s0 + TS],
                               scalar=ccol[:, 0:1], in1=u_cv[:, o0:o0 + TS],
                               op0=ALU.mult, op1=ALU.add), inc="cv3")
                p.step("scalar",
                       lambda e, ub=ub: e.activation(out=u_out[ub][:, :], in_=u_cv[:, :],
                                                     func=AF.Silu),
                       waits=[("cv3", p.val("cv3")),
                              (f"stu{ub}", 16 * ((dt_ // 2)))], inc="cvd")
                p.dma("sync", dst[dt_ * 128:(dt_ + 1) * 128, :], u_out[ub][:, :],
                      waits=[("cvd", p.val("cvd"))], inc=f"stu{ub}")
        wxb2 = [sb(f"wxb2_{k}", [128, 96], BF16) for k in range(16)]
        for k in range(16):
            p.dma("scalar", wxl[:, :], wx[k * 128:(k + 1) * 128, :],
                  waits=[("wxc", p.val("wxc"))], inc="wxld")
            p.step("vector", lambda e, k=k: e.tensor_copy(wxb2[k][:, :], wxl[:, :]),
                   waits=[("wxld", p.val("wxld"))], inc="wxc")
        uk = [sb(f"uk{i}", [128, 128], BF16) for i in range(4)]
        xde = [sb(f"xde{i}", [128, 96]) for i in range(2)]
        for it in range(NTILES):
            pb = xd_ps[it % 2]
            for kk in range(16):
                gi = it * 16 + kk
                ub = gi % 4
                p.dma("sync", uk[ub][:, :],
                      u_o[kk * 128:(kk + 1) * 128, it * 128:(it + 1) * 128],
                      waits=[("stu0", p.val("stu0")), ("stu1", p.val("stu1")),
                             ("xmm", max(0, p.val("xmm") - 3))], inc=f"lduk{ub}")
                p.step("tensor",
                       lambda e, pb=pb, kk=kk, ub=ub: e.matmul(
                           pb[:, :], uk[ub][:, :], wxb2[kk][:, :],
                           start=(kk == 0), stop=(kk == 15)),
                       waits=[(f"lduk{ub}", (gi // 4 + 1) * 16)] +
                             ([("wxc", p.val("wxc")),
                               ("xde_", max(0, p.val("xde_") - 1))] if kk == 0 else []),
                       inc="xmm")
            p.step("vector",
                   lambda e, pb=pb, it=it: e.tensor_copy(xde[it % 2][:, :], pb[:, :]),
                   waits=[("xmm", p.val("xmm")),
                          (f"stxd{it % 2}", 16 * (it // 2))], inc="xde_")
            p.dma("scalar", xdbl_o[it * 128:(it + 1) * 128, :], xde[it % 2][:, :],
                  waits=[("xde_", p.val("xde_"))], inc=f"stxd{it % 2}")
        p.wait("sync", [("stxd0", p.val("stxd0")), ("stxd1", p.val("stxd1")),
                        ("stu0", p.val("stu0")), ("stu1", p.val("stu1"))])
        p.emit()
    return nc


def build_mamba_scan():
    BT = B * T
    L2 = 4096
    nc = bass.Bass("TRN2", target_bir_lowering=False, debug=False)
    u_i = nc.dram_tensor("u_i", [256, BT], BF16, kind="ExternalInput").ap()
    res_i = nc.dram_tensor("res_i", [256, BT], BF16, kind="ExternalInput").ap()
    xdbl = nc.dram_tensor("xdbl", [BT, 96], F32, kind="ExternalInput").ap()
    wdt = nc.dram_tensor("wdt", [DTR, 256], F32, kind="ExternalInput").ap()
    bdt = nc.dram_tensor("bdt", [1, 256], F32, kind="ExternalInput").ap()
    alog = nc.dram_tensor("alog", [256, DST], F32, kind="ExternalInput").ap()
    dpar = nc.dram_tensor("dpar", [1, 256], F32, kind="ExternalInput").ap()
    ident = nc.dram_tensor("ident", [128, 128], BF16, kind="ExternalInput").ap()
    y_o = nc.dram_tensor("y_o", [256, BT], BF16, kind="ExternalOutput").ap()
    bc_scr = nc.dram_tensor("bc_scr", [32, BT], BF16).ap()

    es = contextlib.ExitStack()
    with es:
        sb = lambda n_, sh, dt=F32: es.enter_context(nc.sbuf_tensor(n_, sh, dt))
        psum = lambda n_, sh, dt=F32: es.enter_context(nc.psum_tensor(n_, sh, dt))
        p = Plan(nc, es)
        sident = sb("sident", [128, 128], BF16)
        p.dma("sync", sident[:, :], ident)
        A_sb = [sb(f"asb{i}", [128, DST]) for i in range(2)]
        ald = sb("ald", [128, DST])
        bdtc = sb("bdtc", [128, 2])
        dpc = sb("dpc", [128, 2])
        wdtl = sb("wdtl", [DTR, 256])
        wdtb = sb("wdtb", [DTR, 256], BF16)
        for i in range(2):
            p.dma("scalar", ald[:, :], alog[i * 128:(i + 1) * 128, :],
                  waits=[("acst", p.val("acst"))], inc="lda")
            p.step("scalar",
                   lambda e, i=i: e.activation(out=A_sb[i][:, :], in_=ald[:, :],
                                               func=AF.Exp),
                   waits=[("lda", p.val("lda"))], inc="aex")
            p.step("vector",
                   lambda e, i=i: e.tensor_scalar_mul(A_sb[i][:, :], A_sb[i][:, :],
                                                      -1.0),
                   waits=[("aex", p.val("aex"))], inc="acst")
            p.dma("scalar", bdtc[:, i:i + 1],
                  bdt[0:1, i * 128:(i + 1) * 128].rearrange("one b -> b one"),
                  inc="ldc")
            p.dma("scalar", dpc[:, i:i + 1],
                  dpar[0:1, i * 128:(i + 1) * 128].rearrange("one b -> b one"),
                  inc="ldc")
        p.dma("scalar", wdtl[:, :], wdt, inc="ldw")
        p.step("vector", lambda e: e.tensor_copy(wdtb[:, :], wdtl[:, :]),
               waits=[("ldw", p.val("ldw"))], inc="wdtc")
        dtbc = sb("dtbc", [96, BT], BF16)
        dt_fm = dtbc[0:64, :]
        bc_sb = dtbc[64:96, :]
        xdt = [sb(f"xdt{i}", [128, 96]) for i in range(2)]
        xdb = sb("xdb", [128, 96], BF16)
        tp = [es.enter_context(nc.psum_tensor(f"tp{i}", [128, 1024], BF16))
              for i in range(2)]
        NBLK = BT // 128
        for blk in range(NBLK):
            bb = blk % 2
            p.dma("sync", xdt[bb][:, :], xdbl[blk * 128:(blk + 1) * 128, :],
                  waits=[("xdc", max(0, p.val("xdc") - 1))], inc=f"ldxd{bb}")
            p.step("vector",
                   lambda e, bb=bb: e.tensor_copy(xdb[:, :], xdt[bb][:, :]),
                   waits=[(f"ldxd{bb}", (blk // 2 + 1) * 16),
                          ("trq", p.val("trq"))], inc="xdc")
            p.step("tensor",
                   lambda e: e.transpose(tp[0][0:64, 0:128],
                                         xdb[:, 0:64], sident[:, :]),
                   waits=[("xdc", p.val("xdc")), ("ld", 16),
                          ("trd", max(0, p.val("trd") - 1))], inc="trq")
            p.step("tensor",
                   lambda e: e.transpose(tp[1][64:96, 0:128], xdb[:, 64:96],
                                         sident[:, :], tile_position=(0, 64)),
                   inc="trq")
            p.step("vector",
                   lambda e, blk=blk: e.tensor_copy(
                       dt_fm[:, blk * 128:(blk + 1) * 128], tp[0][0:64, 0:128]),
                   waits=[("trq", p.val("trq"))], inc="trd2")
            p.step("vector",
                   lambda e, blk=blk: e.tensor_copy(
                       bc_sb[:, blk * 128:(blk + 1) * 128], tp[1][64:96, 0:128]),
                   inc="trd")
        p.dma("sync", bc_scr, bc_sb[:, :], waits=[("trd", p.val("trd"))], inc="stbc")
        delta = sb("delta", [128, L2])
        epsb1 = sb("epsb1", [128, 1])
        p.step("vector", lambda e: e.memset(epsb1[:, :], 1.0), inc="eps1")

        zb = [sb(f"zb{i}", [128, 512]) for i in range(2)]
        na = sb("na", [128, 512])
        ab = sb("ab", [128, 512])
        e1 = sb("e1", [128, 512])
        u_sl = [sb(f"usl{i}", [128, L2], BF16) for i in range(2)]
        du = sb("du", [128, L2], BF16)
        dA = sb("dA", [128, L2])
        bbc = sb("bbc", [128, L2], BF16)
        dbu = sb("dbu", [128, L2], BF16)
        h_sb = sb("h_sb", [128, L2], BF16)
        cbc = sb("cbc", [128, L2], BF16)
        ytmp = sb("ytmp", [128, L2], BF16)
        yacc = sb("yacc", [128, L2])
        ressl = sb("ressl", [128, L2], BF16)
        yout = [sb("yout0", [128, L2], BF16)] * 2
        stcol = sb("stcol", [128, DST])
        dl_ps = [psum(f"dlp{i}", [128, 512]) for i in range(2)]
        it2 = 0
        for b_ in range(B):
            for dt2 in range(2):
                for hf in range(2):
                    cols0 = b_ * T + hf * L2
                    ub2 = it2 % 2
                    p.dma("sync", u_sl[ub2][:, :],
                          u_i[dt2 * 128:(dt2 + 1) * 128, cols0:cols0 + L2],
                          waits=[("dud", max(0, p.val("dud") - 1))], inc=f"ldu{ub2}")
                    for lch in range(8):
                        pb = dl_ps[lch % 2]
                        p.step("tensor",
                               lambda e, pb=pb, lch=lch, dt2=dt2, cols0=cols0:
                               e.matmul(pb[:, :],
                                        wdtb[:, dt2 * 128:(dt2 + 1) * 128],
                                        dt_fm[:, cols0 + lch * 512:cols0 + (lch + 1) * 512],
                                        start=True, stop=True),
                               waits=[("wdtc", 1), ("trd2", p.val("trd2")),
                                      ("zbd", max(0, p.val("zbd") - 1))],
                               inc="dmm")
                        zbb = zb[lch % 2]
                        p.step("vector",
                               lambda e, pb=pb, zbb=zbb, dt2=dt2: e.tensor_scalar_add(
                                   zbb[:, :], pb[:, :], bdtc[:, dt2:dt2 + 1]),
                               waits=[("dmm", p.val("dmm")), ("ldc", 64),
                                      ("sfp", max(0, p.val("sfp") - 1))], inc="zbd")
                        p.wait("vector", [("zbd", p.val("zbd"))])
                        p.step("vector",
                               lambda e, zbb=zbb: e.tensor_scalar_mul(na[:, :],
                                                                      zbb[:, :], -1.0),
                               inc="nad")
                        p.wait("vector", [("nad", p.val("nad"))])
                        p.step("vector",
                               lambda e, zbb=zbb: e.tensor_max(ab[:, :], zbb[:, :],
                                                               na[:, :]), inc="abd")
                        p.step("vector",
                               lambda e, zbb=zbb, lch=lch: e.tensor_scalar_max(
                                   delta[:, lch * 512:(lch + 1) * 512], zbb[:, :], 0.0),
                               inc="rld")
                        p.step("scalar",
                               lambda e: e.activation(out=e1[:, :], in_=ab[:, :],
                                                      func=AF.Exp, scale=-1.0),
                               waits=[("abd", p.val("abd")), ("lng2", p.val("lng2"))],
                               inc="lng1")
                        p.wait("scalar", [("lng1", p.val("lng1"))])
                        p.step("scalar",
                               lambda e: e.activation(out=e1[:, :], in_=e1[:, :],
                                                      func=AF.Ln, bias=epsb1[:, 0:1]),
                               inc="sfp")
                        p.step("vector",
                               lambda e, lch=lch: e.tensor_add(
                                   delta[:, lch * 512:(lch + 1) * 512],
                                   delta[:, lch * 512:(lch + 1) * 512], e1[:, :]),
                               waits=[("sfp", p.val("sfp")), ("rld", p.val("rld"))],
                               inc="deld")
                    p.step("vector",
                           lambda e, ub2=ub2: e.tensor_mul(du[:, :], delta[:, :],
                                                           u_sl[ub2][:, :]),
                           waits=[("deld", p.val("deld")),
                                  (f"ldu{ub2}", (it2 // 2 + 1) * 16),
                                  ("dbud", p.val("dbud"))], inc="dud")
                    for n_ in range(DST):
                        p.step("scalar",
                               lambda e, n_=n_, dt2=dt2: e.activation(
                                   out=dA[:, :], in_=delta[:, :], func=AF.Exp,
                                   scale=A_sb[dt2][:, n_:n_ + 1]),
                               waits=[("deld", p.val("deld")), ("acst", 2),
                                      ("scand", p.val("scand"))], inc="dad")
                        p.dma("sync", bbc[:, :],
                              bc_scr[n_:n_ + 1, cols0:cols0 + L2].partition_broadcast(128),
                              waits=[("stbc", 16), ("dbud", p.val("dbud"))],
                              inc="ldbb")
                        p.step("vector",
                               lambda e: e.tensor_mul(dbu[:, :], du[:, :], bbc[:, :]),
                               waits=[("dud", p.val("dud")), ("ldbb", p.val("ldbb")),
                                      ("scand", p.val("scand"))], inc="dbud")
                        p.wait("vector", [("dbud", p.val("dbud"))])
                        init = 0.0 if hf == 0 else stcol[:, n_:n_ + 1]
                        p.step("vector",
                               lambda e, init=init: e.tensor_tensor_scan(
                                   out=h_sb[:, :], data0=dA[:, :], data1=dbu[:, :],
                                   initial=init, op0=ALU.mult, op1=ALU.add),
                               waits=[("dad", p.val("dad")),
                                      ("ytd", p.val("ytd"))], inc="scand")
                        p.wait("vector", [("scand", p.val("scand"))])
                        if hf == 0:
                            p.step("vector",
                                   lambda e, n_=n_: e.tensor_copy(
                                       stcol[:, n_:n_ + 1], h_sb[:, L2 - 1:L2]),
                                   inc="std_")
                        p.dma("scalar", cbc[:, :],
                              bc_scr[16 + n_:17 + n_, cols0:cols0 + L2].partition_broadcast(128),
                              waits=[("stbc", 16), ("ytd", p.val("ytd"))], inc="ldcb")
                        p.step("vector",
                               lambda e: e.tensor_mul(ytmp[:, :], h_sb[:, :], cbc[:, :]),
                               waits=[("ldcb", p.val("ldcb"))], inc="ytd")
                        p.wait("vector", [("ytd", p.val("ytd"))])
                        if n_ == 0:
                            p.step("vector",
                                   lambda e: e.tensor_copy(yacc[:, :], ytmp[:, :]),
                                   inc="yad")
                        else:
                            p.step("vector",
                                   lambda e: e.tensor_add(yacc[:, :], yacc[:, :],
                                                          ytmp[:, :]), inc="yad")
                    p.dma("sync", ressl[:, :],
                          res_i[dt2 * 128:(dt2 + 1) * 128, cols0:cols0 + L2],
                          waits=[("sild", p.val("sild"))], inc="ldrs")
                    p.step("scalar",
                           lambda e: e.activation(out=ressl[:, :], in_=ressl[:, :],
                                                  func=AF.Silu),
                           waits=[("ldrs", p.val("ldrs"))], inc="sild")
                    p.step("vector",
                           lambda e, ub2=ub2, dt2=dt2: e.tensor_scalar(
                               out=du[:, :], in0=u_sl[ub2][:, :],
                               scalar1=dpc[:, dt2:dt2 + 1], scalar2=None,
                               op0=ALU.mult),
                           waits=[("yad", p.val("yad"))], inc="udd")
                    p.wait("vector", [("udd", p.val("udd"))])
                    p.step("vector",
                           lambda e: e.tensor_add(yacc[:, :], yacc[:, :], du[:, :]),
                           inc="ya2")
                    p.wait("vector", [("ya2", p.val("ya2")), ("sild", p.val("sild"))])
                    yo = yout[0]
                    p.step("vector",
                           lambda e, yo=yo: e.tensor_mul(yo[:, :], yacc[:, :],
                                                         ressl[:, :]),
                           waits=[("sty0", 16 * it2)], inc="yod")
                    p.dma("sync", y_o[dt2 * 128:(dt2 + 1) * 128, cols0:cols0 + L2],
                          yo[:, :], waits=[("yod", p.val("yod"))], inc="sty0")
                    it2 += 1
        p.wait("sync", [("sty0", p.val("sty0"))])
        p.emit()
    return nc


def build_mamba_out():
    nc = bass.Bass("TRN2", target_bir_lowering=False, debug=False)
    x = nc.dram_tensor("x", [ROWS, D], F32, kind="ExternalInput").ap()
    y = nc.dram_tensor("y", [DI, ROWS], BF16, kind="ExternalInput").ap()
    wo = nc.dram_tensor("wo", [DI, D], F32, kind="ExternalInput").ap()
    xo = nc.dram_tensor("xo", [ROWS, D], F32, kind="ExternalOutput").ap()

    es = contextlib.ExitStack()
    with es:
        sb = lambda n_, sh, dt=F32: es.enter_context(nc.sbuf_tensor(n_, sh, dt))
        psum = lambda n_, sh, dt=F32: es.enter_context(nc.psum_tensor(n_, sh, dt))
        p = Plan(nc, es)
        wob = [sb(f"wob{k}", [128, D], BF16) for k in range(16)]
        wld = sb("wld", [128, D])
        for k in range(16):
            p.dma("scalar", wld[:, :], wo[k * 128:(k + 1) * 128, :],
                  waits=[("wcst", p.val("wcst"))], inc="wld")
            p.step("vector", lambda e, k=k: e.tensor_copy(wob[k][:, :], wld[:, :]),
                   waits=[("wld", p.val("wld"))], inc="wcst")
        yk = [sb(f"yk{i}", [128, 128], BF16) for i in range(16)]
        xt = [sb(f"xt{i}", [128, D]) for i in range(2)]
        onew = [sb(f"onew{i}", [128, D]) for i in range(2)]
        mm_ps = [psum(f"mmp{i}", [128, 512]) for i in range(4)]
        for it in range(NTILES):
            buf = it % 2
            p.dma("scalar", xt[buf][:, :], x[it * 128:(it + 1) * 128, :],
                  waits=[("oev", max(0, p.val("oev") - 2))], inc=f"ldx{buf}")
            for kk in range(16):
                ub = kk % 4
                p.dma("sync", yk[kk][:, :],
                      y[kk * 128:(kk + 1) * 128, it * 128:(it + 1) * 128],
                      waits=([("pmm", p.val("pmm"))] if kk == 0 else []),
                      inc=f"ldy{ub}")
            for h in range(2):
                pb = mm_ps[p.val("pmm") % 4]
                for kk in range(16):
                    ub = kk % 4
                    p.step("tensor",
                           lambda e, pb=pb, kk=kk, h=h: e.matmul(
                               pb[:, :], yk[kk][:, :],
                               wob[kk][:, h * 512:(h + 1) * 512],
                               start=(kk == 0), stop=(kk == 15)),
                           waits=([(f"ldy{u2}", 64 * (it + 1)) for u2 in range(4)] +
                                  [("wcst", p.val("wcst")),
                                   ("oev", max(0, p.val("oev") - 2))]
                                  if (kk == 0 and h == 0) else []),
                           inc=("pmm" if kk == 15 else None))
                p.step("vector",
                       lambda e, pb=pb, buf=buf, h=h: e.tensor_add(
                           onew[buf][:, h * 512:(h + 1) * 512], pb[:, :],
                           xt[buf][:, h * 512:(h + 1) * 512]),
                       waits=[("pmm", p.val("pmm")),
                              (f"ldx{buf}", (it // 2 + 1) * 16)] +
                             ([(f"stx{buf}", 16 * (it // 2))] if h == 0 else []),
                       inc="oev")
            p.dma("scalar", xo[it * 128:(it + 1) * 128, :], onew[buf][:, :],
                  waits=[("oev", p.val("oev"))], inc=f"stx{buf}")
        p.wait("sync", [("stx0", p.val("stx0")), ("stx1", p.val("stx1"))])
        p.emit()
    return nc


_PROG_CACHE = {}


def _prog(name):
    if name not in _PROG_CACHE:
        _PROG_CACHE[name] = globals()[f"build_{name}"]()
    return _PROG_CACHE[name]


def _run(name, in_maps):
    nc = _prog(name)
    return run_bass_kernel_spmd(nc, in_maps, core_ids=list(range(NCORES))).results


_IDENT = np.eye(128, dtype=BF)


def _gather_rows(shards, key):
    F_ = shards[0][key].shape[-1]
    out = np.empty((B, T, F_), shards[0][key].dtype)
    for c in range(NCORES):
        out[:, c * TS:(c + 1) * TS] = shards[c][key].reshape(B, TS, F_)
    return out


def _gather_fm(shards, key, nrow):
    out = np.empty((nrow, B, T), shards[0][key].dtype)
    for c in range(NCORES):
        out[:, :, c * TS:(c + 1) * TS] = shards[c][key].reshape(nrow, B, TS)
    return out


def _pad_w(w, m, rows_too=False):
    out = np.zeros((D, D), np.float32)
    if rows_too:
        out[:m, :m] = w[:m, :m]
    else:
        out[:, :m] = w[:, :m]
    return out


def l0ffn_call(x, inp):
    shards = np_split_rows(x)
    ms = [D >> (c // 2) for c in range(NCORES)]
    im = []
    for c in range(NCORES):
        m = ms[c]
        im.append(dict(
            x=shards[c], nw=inp["L0_ffn_norm_w"].reshape(1, -1),
            nb=inp["L0_ffn_norm_b"].reshape(1, -1),
            wq=_pad_w(inp["L0_ffn_q_w"], m), wk=_pad_w(inp["L0_ffn_k_w"], m),
            wv=_pad_w(inp["L0_ffn_v_w"], m),
            bq=inp["L0_ffn_q_b"].reshape(1, -1), bk=inp["L0_ffn_k_b"].reshape(1, -1),
            bv=inp["L0_ffn_v_b"].reshape(1, -1), ident=_IDENT))
    r1 = _run("l0ffn_proj", im)
    qf = _gather_fm(r1, "qf", D)
    kf = _gather_fm(r1, "kf", D)
    vt = _gather_rows(r1, "vt")
    im = []
    for c in range(NCORES):
        prs = [(c // 2, (c % 2) * 4 + j) for j in range(4)]
        qf4 = np.stack([qf[h * 128:(h + 1) * 128, b] for (b, h) in prs])
        kf4 = np.stack([kf[h * 128:(h + 1) * 128, b] for (b, h) in prs])
        vt4 = np.stack([vt[b, :, h * 128:(h + 1) * 128] for (b, h) in prs])
        im.append(dict(qf4=np.ascontiguousarray(qf4), kf4=np.ascontiguousarray(kf4),
                       vt4=np.ascontiguousarray(vt4), ident=_IDENT))
    r2 = _run("l0ffn_attn", im)
    af = np.empty((D, B, T), r2[0]["af"].dtype)
    for c in range(NCORES):
        for j in range(4):
            b, h = c // 2, (c % 2) * 4 + j
            af[h * 128:(h + 1) * 128, b] = r2[c]["af"][j]
    im = []
    for c in range(NCORES):
        m = ms[c]
        afc = np.ascontiguousarray(
            af[:, :, c * TS:(c + 1) * TS].reshape(D, ROWS))
        im.append(dict(x=shards[c], af=afc,
                       ow=_pad_w(inp["L0_ffn_o_w"], m, rows_too=True),
                       ob=np.concatenate([inp["L0_ffn_o_b"][:m],
                                          np.zeros(D - m, np.float32)]).reshape(1, -1),
                       ident=_IDENT))
    r3 = _run("l0ffn_oproj", im)
    return _gather_rows(r3, "xo")


def mamba_call(x, inp):
    shards = np_split_rows(x)
    im = []
    for c in range(NCORES):
        if c == 0:
            xhs = np.zeros((8, D), np.float32)
        else:
            xhs = np.ascontiguousarray(
                x[:, c * TS - 2:c * TS].reshape(8, D)).astype(np.float32)
        im.append(dict(x=shards[c], xh=xhs, win=inp["L0_attn_W_in"],
                       wcv=inp["L0_attn_W_conv"], wx=inp["L0_attn_W_x"],
                       ident=_IDENT))
    r1 = _run("mamba_pre", im)
    u_g = _gather_fm(r1, "u_o", DI).reshape(DI, B * T)
    res_g = _gather_fm(r1, "res_o", DI).reshape(DI, B * T)
    xdbl_g = _gather_rows(r1, "xdbl_o").reshape(B * T, 96)
    im = []
    for c in range(NCORES):
        sl = slice(c * 256, (c + 1) * 256)
        im.append(dict(u_i=np.ascontiguousarray(u_g[sl]),
                       res_i=np.ascontiguousarray(res_g[sl]),
                       xdbl=xdbl_g,
                       wdt=np.ascontiguousarray(inp["L0_attn_W_dt"][:, sl]),
                       bdt=np.ascontiguousarray(inp["L0_attn_b_dt"][sl]).reshape(1, -1),
                       alog=np.ascontiguousarray(inp["L0_attn_A_log"][sl]),
                       dpar=np.ascontiguousarray(inp["L0_attn_D_param"][sl]).reshape(1, -1),
                       ident=_IDENT))
    r2 = _run("mamba_scan", im)
    y_g = np.empty((DI, B, T), r2[0]["y_o"].dtype)
    for c in range(NCORES):
        y_g[c * 256:(c + 1) * 256] = r2[c]["y_o"].reshape(256, B, T)
    im = []
    for c in range(NCORES):
        yc = np.ascontiguousarray(y_g[:, :, c * TS:(c + 1) * TS].reshape(DI, ROWS))
        im.append(dict(x=shards[c], y=yc, wo=inp["L0_attn_W_out"]))
    r3 = _run("mamba_out", im)
    return _gather_rows(r3, "xo")


def l1ffn_call(x, inp):
    shards = np_split_rows(x)
    im = []
    for c in range(NCORES):
        im.append(dict(x=shards[c], g1r=inp["L1_ffn_G1_R"], g2r=inp["L1_ffn_G2_R"],
                       g1i=inp["L1_ffn_G1_I"], g2i=inp["L1_ffn_G2_I"],
                       brl=inp["L1_ffn_bias_real"].reshape(1, -1),
                       bim=inp["L1_ffn_bias_imag"].reshape(1, -1), ident=_IDENT))
    r = _run("l1ffn", im)
    return _gather_rows(r, "xo")


def l1attn_call(x, inp):
    shards = np_split_rows(x)
    im = [dict(x=shards[c], wq=inp["L1_attn_W_Q"], wk=inp["L1_attn_W_K"],
               wv=inp["L1_attn_W_V"], ident=_IDENT) for c in range(NCORES)]
    r1 = _run("l1attn_proj", im)
    qf = _gather_fm(r1, "qf", D)
    kf = _gather_fm(r1, "kf", D)
    vt = _gather_rows(r1, "vt")
    im = []
    for c in range(NCORES):
        b, hh = c // 2, c % 2
        im.append(dict(
            qf=np.ascontiguousarray(qf[:, b, hh * 4096:(hh + 1) * 4096]),
            kf=np.ascontiguousarray(kf[:, b]),
            vt=np.ascontiguousarray(vt[b]),
            x=np.ascontiguousarray(x[b, hh * 4096:(hh + 1) * 4096]).astype(np.float32)))
    r2 = _run("l1attn_attn", im)
    out = np.empty((B, T, D), np.float32)
    for c in range(NCORES):
        b, hh = c // 2, c % 2
        out[b, hh * 4096:(hh + 1) * 4096] = r2[c]["xo"]
    return out


def kernel(**inputs):
    return kernel_fused(inputs)


def kernel_unfused(**inputs):
    inp = {k: np.ascontiguousarray(np.asarray(v, dtype=np.float32))
           for k, v in inputs.items()}
    x = inp["L0_attn_x"]
    x = l0ffn_call(x, inp)
    x = mamba_call(x, inp)
    x = l0ffn_call(x, inp)
    x = l1ffn_call(x, inp)
    x = l1attn_call(x, inp)
    x = l1ffn_call(x, inp)
    return x


TF = 8192
HTS = 4096
POLL_MAX = 500000


def build_fused(stop_after=None):
    nc = bass.Bass("TRN2", target_bir_lowering=False, debug=False)
    EI = lambda n_, sh, dt=F32: nc.dram_tensor(n_, sh, dt, kind="ExternalInput").ap()
    x_in = EI("x_in", [HTS, D])
    par_in = EI("par_in", [1, 1], mybir.dt.int32)
    nonce_in = EI("nonce_in", [1, 1], mybir.dt.int32)
    ident = EI("ident", [128, 128], BF16)
    nw = EI("nw", [1, D]); nb = EI("nb", [1, D])
    wq2 = [EI(f"wq{s}", [D, D]) for s in range(2)]
    wk2 = [EI(f"wk{s}", [D, D]) for s in range(2)]
    wv2 = [EI(f"wv{s}", [D, D]) for s in range(2)]
    bq = EI("bq", [1, D]); bk = EI("bk", [1, D]); bv = EI("bv", [1, D])
    ow2 = [EI(f"ow{s}", [D, D]) for s in range(2)]
    ob2 = [EI(f"ob{s}", [1, D]) for s in range(2)]
    win_u = EI("win_u", [D, 1024]); win_r = EI("win_r", [D, 1024])
    wcv_h = EI("wcv_h", [1024, 3]); wx_h = EI("wx_h", [1024, 96])
    wdt_h = EI("wdt_h", [DTR, 1024]); bdt_h = EI("bdt_h", [1, 1024])
    alog_h = EI("alog_h", [1024, DST]); dpar_h = EI("dpar_h", [1, 1024])
    wout_m = EI("wout_m", [1024, D]); wout_t = EI("wout_t", [1024, D])
    g1r = EI("g1r", [1, 32, 4, 32]); g2r = EI("g2r", [4, 16, 1, 16])
    g1i = EI("g1i", [1, 32, 4, 32]); g2i = EI("g2i", [4, 16, 1, 16])
    brl = EI("brl", [1, 512]); bim = EI("bim", [1, 512])
    l1wq = EI("l1wq", [D, D]); l1wk = EI("l1wk", [D, D]); l1wv = EI("l1wv", [D, D])
    xo = nc.dram_tensor("xo", [HTS, D], F32, kind="ExternalOutput").ap()

    LD = lambda n_, sh, dt=F32: nc.dram_tensor(n_, sh, dt).ap()
    x_a = LD("x_a", [HTS, D]); x_b = LD("x_b", [HTS, D])
    qf_s = LD("qf_s", [D, HTS], BF16)
    kf_s = LD("kf_s", [D, HTS], BF16)
    vt_s = LD("vt_s", [HTS, D], BF16)
    af_s = LD("af_s", [D, HTS], BF16)
    ur_s = LD("ur_s", [1024, TF], BF16)
    u_s = LD("u_s", [1024, TF], BF16)
    res_s = LD("res_s", [1024, TF], BF16)
    y_s = LD("y_s", [1024, HTS], BF16)
    y2_s = LD("y2_s", [1024, HTS], BF16)
    xd_s = LD("xd_s", [TF, 96])
    bc_scr = LD("bc_scr", [32, TF], BF16)
    wtmp = [LD(f"fwtmp{i}", [1024, 256]) for i in range(2)]
    w4 = [nc.dram_tensor(f"fw4_{i}", [32, 16, 32, 16], F32).ap() for i in range(2)]
    SH = lambda n_, sh, dt: nc.dram_tensor(n_, sh, dt, addr_space="Shared").ap()
    sh_k = SH("sh_k", [2, D, HTS], BF16)
    sh_v = SH("sh_v", [2, HTS, D], BF16)
    sh_ln2 = SH("sh_ln2", [2, D, HTS], BF16)
    sh_xd = SH("sh_xd", [2, TF, 96], F32)
    sh_y = SH("sh_y", [2, 1024, HTS], BF16)
    flags = nc.dram_tensor("flags", [2, 32], mybir.dt.int32,
                           addr_space="Shared").ap()

    es = contextlib.ExitStack()
    with es:
        gsb = lambda n_, sh, dt=F32: es.enter_context(nc.sbuf_tensor(n_, sh, dt))
        p = Plan(nc, es)
        sident = gsb("sident", [128, 128], BF16)
        par_sb = gsb("par_sb", [1, 1], mybir.dt.int32)
        nonce_sb = gsb("nonce_sb", [1, 1], mybir.dt.int32)
        fl_sb = gsb("fl_sb", [1, 1], mybir.dt.int32)
        p.dma("sync", sident[:, :], ident)
        p.dma("sync", par_sb[:, :], par_in, inc="ld")
        p.dma("sync", nonce_sb[:, :], nonce_in, inc="ld")
        regs = {"sync": {}, "scalar": {}}

        def setup_regs(e, eng):
            e.wait_ge(p.sems["ld"], 48)
            r = regs[eng]
            r["par"] = e.alloc_register(f"parr_{eng}")
            e.reg_load(r["par"], par_sb[0:1, 0:1])
            if eng == "sync":
                r["nd"] = e.alloc_register("ndr")
                r["fv"] = e.alloc_register("fvr")
                r["cnt"] = e.alloc_register("cntr")
                r["non"] = e.alloc_register("nonr")
                e.reg_load(r["non"], nonce_sb[0:1, 0:1])
                e.reg_mov(r["cnt"], 0)
        for _eng in ("sync", "scalar"):
            p.steps[_eng].append(
                (lambda e, _eng=_eng: (setup_regs(e, _eng), None)[1], [], None))

        _bar = [0]

        def barrier(waits=()):
            k = _bar[0]
            _bar[0] += 1
            psem = p.sem("bpoll")
            bfs = p.sem("bflag")
            bard = p.sem("bard")
            p.counts["bflag"] += 16
            p.counts["bard"] += 16

            def fn(e, k=k):
                r = regs["sync"]
                with e.If_eq(r["par"], 0):
                    e.dma_start(out=flags[0:1, k:k + 1],
                                in_=nonce_sb[0:1, 0:1]).then_inc(bfs, 16)
                with e.Else():
                    e.dma_start(out=flags[1:2, k:k + 1],
                                in_=nonce_sb[0:1, 0:1]).then_inc(bfs, 16)
                e.wait_ge(bfs, 16 * (k + 1))
                e.reg_mov(r["nd"], POLL_MAX)
                with e.While(r["nd"]):
                    e.reg_add(r["cnt"], r["cnt"], 16)
                    with e.If_eq(r["par"], 0):
                        e.dma_start(out=fl_sb[:, :],
                                    in_=flags[1:2, k:k + 1]).then_inc(psem, 16)
                    with e.Else():
                        e.dma_start(out=fl_sb[:, :],
                                    in_=flags[0:1, k:k + 1]).then_inc(psem, 16)
                    e.wait_ge(psem, r["cnt"])
                    e.reg_load(r["fv"], fl_sb[0:1, 0:1])
                    with e.If_cmp(r["fv"], r["non"], comp_op="IS_EQ"):
                        e.reg_mov(r["nd"], 0)
                    with e.Else():
                        e.reg_sub(r["nd"], r["nd"], 1)
                e.dma_start(out=fl_sb[:, :],
                            in_=flags[0:1, k:k + 1]).then_inc(bard, 16)
                return None
            p.steps["sync"].append((fn, [(p.sem(s), v) for (s, v) in waits], None))
            return 16 * (k + 1)

        def pardma(eng, out0, in0, out1, in1, waits=(), inc=None):
            sem = p.sem(inc)
            p.counts[inc] += 16
            tgt = p.sems[inc]

            def fn(e, eng=eng):
                with e.If_eq(regs[eng]["par"], 0):
                    e.dma_start(out=out0, in_=in0).then_inc(tgt, 16)
                with e.Else():
                    e.dma_start(out=out1, in_=in1).then_inc(tgt, 16)
                return None
            pf = p.pending.pop(eng, None)
            w = (pf + list(waits)) if pf else list(waits)
            p.steps[eng].append((fn, [(p.sem(s), v) for (s, v) in w], None))

        emit_fused_body(nc, p, es, locals())
        p.emit()
    return nc


def emit_fused_body(nc, p, es, C):
    barrier = C["barrier"]; pardma = C["pardma"]
    sident = C["sident"]
    gsb = C["gsb"]
    x_in, x_a, x_b, xo = C["x_in"], C["x_a"], C["x_b"], C["xo"]
    qf_s, kf_s, vt_s, af_s = C["qf_s"], C["kf_s"], C["vt_s"], C["af_s"]
    sh_k, sh_v, sh_ln2, sh_xd, sh_y = (C["sh_k"], C["sh_v"], C["sh_ln2"],
                                       C["sh_xd"], C["sh_y"])

    epsb = gsb("epsb_g", [128, 1])
    ones_bf = gsb("ones_g", [1, 128], BF16)
    ones_512 = gsb("ones512_g", [1, 512], BF16)
    onecol = gsb("onecol_g", [128, 1], BF16)
    p.step("vector", lambda e: e.memset(epsb[:, :], EPS))
    p.step("vector", lambda e: e.memset(ones_bf[:, :], 1.0))
    p.step("vector", lambda e: e.memset(onecol[:, :], 1.0))
    p.step("vector", lambda e: e.memset(ones_512[:, :], 1.0), inc="ones")

    NT = HTS // 128

    def ph_l0proj(x_src, call, prev_bv=None):
        with contextlib.ExitStack() as pes:
            sb = lambda n_, sh, dt=F32: pes.enter_context(
                nc.sbuf_tensor(f"{call}{n_}", sh, dt))
            psum = lambda n_, sh, dt=F32: pes.enter_context(
                nc.psum_tensor(f"{call}{n_}", sh, dt))
            wcol = sb("wcol", [128, 8])
            nbcol = sb("nbcol", [128, 8], BF16)
            nbf = sb("nbf", [128, 8])
            for j in range(8):
                p.dma("sync", wcol[:, j:j + 1],
                      C["nw"][0:1, j * 128:(j + 1) * 128].rearrange("one b -> b one"),
                      inc="wtld")
                p.dma("sync", nbf[:, j:j + 1],
                      C["nb"][0:1, j * 128:(j + 1) * 128].rearrange("one b -> b one"),
                      inc="wtld")
            p.step("vector", lambda e: e.tensor_copy(nbcol[:, :], nbf[:, :]),
                   waits=[("wtld", p.val("wtld"))], inc="wtcast")
            xt = [sb(f"xt{i}", [128, D]) for i in range(2)]
            z1 = sb("z1", [128, D]); scr = sb("scr", [128, D])
            xn_bf = sb("xn_bf", [128, D], BF16)
            xn_fm = [sb(f"xnfm{j}", [128, HTS], BF16) for j in range(8)]
            mu = sb("mu", [128, 2]); sumsq = sb("sumsq", [128, 2])
            rstd = sb("rstd", [128, 2])
            mu2 = sb("mu2", [128, 2]); sumsq2 = sb("sumsq2", [128, 2])
            rstd2 = sb("rstd2", [128, 2])
            tr_ps = [psum(f"trp{i}", [128, 1024], BF16) for i in range(2)]
            lnd0 = p.val("lnd")
            for it in range(NT):
                buf = it % 2
                p.dma("sync", xt[buf][:, :], x_src[it * 128:(it + 1) * 128, :],
                      waits=[("lnd", max(lnd0, p.val("lnd") - 1))], inc=f"ldx{buf}")
                emit_ln_tile(p, xt[buf][:, :], z1[:, :], scr, mu, sumsq, rstd, None,
                             epsb, waits=[(f"ldx{buf}", p.val(f"ldx{buf}"))],
                             inc="lnd")
                emit_ln_tile(p, z1[:, :], xn_bf[:, :], scr, mu2, sumsq2, rstd2, None,
                             epsb, waits=[("lnd", p.val("lnd"))], inc="lnd")
                for k in range(8):
                    p.step("tensor",
                           lambda e, k=k: e.transpose(
                               tr_ps[k % 2][:, 0:128], xn_bf[:, k * 128:(k + 1) * 128],
                               sident[:, :]),
                           waits=([("lnd", p.val("lnd")), ("ld", 16)] if k == 0 else
                                  [("trev", p.val("trev") - 1)]),
                           inc="trmm")
                    p.step("vector",
                           lambda e, k=k, it=it: e.tensor_copy(
                               xn_fm[k][:, it * 128:(it + 1) * 128],
                               tr_ps[k % 2][:, 0:128]),
                           waits=[("trmm", p.val("trmm"))], inc="trev")
            be_ps = psum("be_ps", [128, 512])
            brow = sb("brow", [1, D])
            mm_ps = [psum(f"mmp{i}", [128, 512]) for i in range(3)]
            ev = sb("ev", [128, 512], BF16)
            evv = sb("evv", [128, 512], BF16)
            wt_ps = [psum(f"wtps{i}", [128, 1024], BF16) for i in range(2)]
            for s in range(2):
                weff = {}
                be_sb = {}
                prev = None
                wes = pes.enter_context(contextlib.ExitStack())
                wsb = lambda n_, sh, dt=F32: wes.enter_context(
                    nc.sbuf_tensor(f"{call}s{s}{n_}", sh, dt))
                wnm = {}
                for nm, wd in (("q", C["wq2"][s]), ("k", C["wk2"][s]),
                               ("v", C["wv2"][s])):
                    weff[nm] = [wsb(f"weff_{nm}{j}", [128, D], BF16) for j in range(8)]
                    wnm[nm] = f"{call}s{s}w{nm}"
                    ew = [("wtev", p.val("wtev"))] if prev else \
                         ([("pev", p.val("pev"))] if (s == 1 or call != "A_") else [])
                    emit_wT(p, nc, wes, wnm[nm], wd, weff[nm], sident, ps=wt_ps,
                            extra_waits=ew, sp="wt")
                    prev = nm
                for nm, bd in (("q", C["bq"]), ("k", C["bk"]), ("v", C["bv"])):
                    be_sb[nm] = wsb(f"be_{nm}", [1, D], BF16)
                    p.dma("scalar", brow[0:1, :], bd,
                          waits=[("bsg", p.val("bsg"))], inc="ldb")
                    for half in range(2):
                        for kk in range(8):
                            p.step("tensor",
                                   lambda e, kk=kk, half=half, nm=nm, weff=weff:
                                   e.matmul(
                                       be_ps[0:1, :], nbcol[:, kk:kk + 1],
                                       weff[nm][kk][:, half * 512:(half + 1) * 512],
                                       start=(kk == 0), stop=(kk == 7)),
                                   waits=([("wtcast", p.val("wtcast")),
                                           ("wtev", p.val("wtev")),
                                           ("bsg", p.val("bsg"))] if kk == 0 else []),
                                   inc=("mmd" if kk == 7 else None))
                        p.step("vector",
                               lambda e, half=half, nm=nm, be_sb=be_sb:
                               e.tensor_add(
                                   be_sb[nm][0:1, half * 512:(half + 1) * 512],
                                   be_ps[0:1, :],
                                   brow[0:1, half * 512:(half + 1) * 512]),
                               waits=[("mmd", p.val("mmd")), ("ldb", p.val("ldb"))],
                               inc="bsg")
                    p.step("vector",
                           lambda e: e.tensor_copy(brow[0:1, 0:1], brow[0:1, 0:1]),
                           waits=[("bsg", p.val("bsg"))], inc="bsg")
                    for j in range(8):
                        p.step("vector",
                               lambda e, j=j, nm=nm, weff=weff: e.tensor_scalar_mul(
                                   weff[nm][j][:, :], weff[nm][j][:, :],
                                   wcol[:, j:j + 1]),
                               waits=[("mmd", p.val("mmd")),
                                      ("wtld", p.val("wtld"))], inc="wtsc")
                prq = [("bsg", p.val("bsg")), ("trev", p.val("trev")), ("ones", 1),
                       ("wtsc", p.val("wtsc"))]
                for nm, out, tm in (("q", qf_s, False), ("k", kf_s, False),
                                    ("v", vt_s, True)):
                    if tm:
                        mlist = range(s * 16, (s + 1) * 16)
                        nlist = range(2)
                    else:
                        mlist = range(8)
                        nlist = range(s * 4, (s + 1) * 4)
                    for mtile in mlist:
                        for nchunk in nlist:
                            pb = mm_ps[p.val("pmm") % 3]
                            for kk in range(8):
                                if tm:
                                    fn = (lambda e, pb=pb, kk=kk, mtile=mtile,
                                          nchunk=nchunk, nm=nm, weff=weff: e.matmul(
                                              pb[:, :],
                                              xn_fm[kk][:, mtile * 128:(mtile + 1) * 128],
                                              weff[nm][kk][:, nchunk * 512:(nchunk + 1) * 512],
                                              start=(kk == 0), stop=False))
                                else:
                                    fn = (lambda e, pb=pb, kk=kk, mtile=mtile,
                                          nchunk=nchunk, nm=nm, weff=weff: e.matmul(
                                              pb[:, :],
                                              weff[nm][kk][:, mtile * 128:(mtile + 1) * 128],
                                              xn_fm[kk][:, nchunk * 512:(nchunk + 1) * 512],
                                              start=(kk == 0), stop=False))
                                p.step("tensor", fn,
                                       waits=(prq if kk == 0 else []) +
                                             ([("pev", max(0, p.val("pev") - 1))]
                                              if kk == 0 else []),
                                       inc=None)
                            if tm:
                                bfn = (lambda e, pb=pb, nchunk=nchunk, nm=nm,
                                       be_sb=be_sb: e.matmul(
                                           pb[:, :], ones_bf[0:1, :],
                                           be_sb[nm][0:1, nchunk * 512:(nchunk + 1) * 512],
                                           start=False, stop=True))
                            else:
                                bfn = (lambda e, pb=pb, mtile=mtile, nm=nm,
                                       be_sb=be_sb: e.matmul(
                                           pb[:, :],
                                           be_sb[nm][0:1, mtile * 128:(mtile + 1) * 128],
                                           ones_512[0:1, :], start=False, stop=True))
                            p.step("tensor", bfn, inc="pmm")
                            g = p.val("pev")
                            slot = g % 2
                            buf_ev = ev if slot else evv
                            p.step("vector",
                                   lambda e, pb=pb, buf_ev=buf_ev: e.tensor_copy(
                                       buf_ev[:, :], pb[:, :]),
                                   waits=[("pmm", p.val("pmm")),
                                          (f"pst{slot}", p.val(f"pst{slot}"))],
                                   inc="pev")
                            dst = (out[mtile * 128:(mtile + 1) * 128,
                                       nchunk * 512:(nchunk + 1) * 512])
                            p.dma("sync", dst, buf_ev[:, :],
                                  waits=[("pev", p.val("pev"))], inc=f"pst{slot}")
                wes.close()
        pardma("sync", sh_k[0], kf_s, sh_k[1], kf_s,
               waits=[("pst0", p.val("pst0")), ("pst1", p.val("pst1"))] +
                     ([("bard", prev_bv)] if prev_bv else []), inc="shst")
        pardma("sync", sh_v[0], vt_s, sh_v[1], vt_s,
               waits=[("shst", p.val("shst"))], inc="shst")
        return barrier(waits=[("shst", p.val("shst"))])

    def ph_l0attn(bv_, call):
        NQB, NKC = 4, 8
        with contextlib.ExitStack() as pes:
            sb = lambda n_, sh, dt=F32: pes.enter_context(
                nc.sbuf_tensor(f"{call}a{n_}", sh, dt))
            psum = lambda n_, sh, dt=F32: pes.enter_context(
                nc.psum_tensor(f"{call}a{n_}", sh, dt))
            qh = [sb(f"qh{i}", [128, HTS], BF16) for i in range(2)]
            kf_sb = [sb(f"kfs{i}", [128, 1024], BF16) for i in range(2)]
            vt_sb = [[sb(f"vts{i}_{kt}", [128, 132], BF16) for kt in range(8)]
                     for i in range(2)]
            for i in range(2):
                for kt in range(8):
                    p.step("vector",
                           lambda e, i=i, kt=kt: e.memset(vt_sb[i][kt][:, 128:132],
                                                          1.0), inc="vones")
            st_sb2 = [[sb(f"st{s_}_{kt}", [128, 1024], BF16) for kt in range(8)]
                      for s_ in range(2)]
            ob_sb = sb("ob_sb", [128, 132], BF16)
            af_sb = [sb(f"afs{i}", [128, 1024], BF16) for i in range(2)]
            rec_sb = sb("rec_sb", [128, 16])
            st_ps = [psum(f"stps{i}", [128, 512]) for i in range(3)]
            av_big = psum("av_big", [128, 1536])
            av_ps = [av_big[:, (q2 // 3) * 512 + (q2 % 3) * 160:
                            (q2 // 3) * 512 + (q2 % 3) * 160 + 132] for q2 in range(8)]
            tr_ps = [psum(f"trp2{i}", [128, 1024], BF16) for i in range(2)]
            qk0 = p.val("qkv"); avv0 = p.val("avv"); ach0 = p.val("avch")
            for h in range(8):
                hs = h % 2
                p.dma("sync", qh[hs][:, :], qf_s[h * 128:(h + 1) * 128, :],
                      waits=[("bard", bv_)] +
                            ([("qkv", p.val("qkv") - 16 * NKC * NQB)] if h >= 2 else []),
                      inc=f"ldq{hs}")
                for qb in range(NQB):
                    for kc in range(NKC):
                        g = (h * NQB + qb) * NKC + kc
                        ks = g % 2
                        p.dma("sync", kf_sb[ks][:, :],
                              sh_k[kc // 4][h * 128:(h + 1) * 128,
                                            (kc % 4) * 1024:(kc % 4 + 1) * 1024],
                              waits=[("bard", bv_)] +
                                    ([("qkv", p.val("qkv") - 16)] if g >= 2 else []),
                              inc=f"ldk{ks}")
                        for kt in range(8):
                            r0 = kc * 1024 + kt * 128
                            p.dma("scalar", vt_sb[ks][kt][:, 0:128],
                                  sh_v[kc // 4][(kc % 4) * 1024 + kt * 128:
                                                (kc % 4) * 1024 + (kt + 1) * 128,
                                                h * 128:(h + 1) * 128],
                                  waits=[("bard", bv_)] +
                                        ([("avv", p.val("avv") - 16)] if g >= 2 else
                                         [("vones", p.val("vones"))]),
                                  inc=f"ldv{ks}")
                        for kt in range(8):
                            for qn in range(2):
                                pb = st_ps[p.val("qkv") % 3]
                                p.step("tensor",
                                       lambda e, pb=pb, kt=kt, qn=qn, ks=ks, hs=hs,
                                              qb=qb: e.matmul(
                                           pb[:, :],
                                           kf_sb[ks][:, kt * 128:(kt + 1) * 128],
                                           qh[hs][:, qb * 1024 + qn * 512:
                                                  qb * 1024 + (qn + 1) * 512],
                                           start=True, stop=True),
                                       waits=[(f"ldk{ks}", p.val(f"ldk{ks}")),
                                              (f"ldq{hs}", p.val(f"ldq{hs}")),
                                              ("expd", max(qk0, p.val("qkv") - 2))],
                                       inc="qkv")
                                p.step("scalar",
                                       lambda e, pb=pb, kt=kt, qn=qn, ks=ks:
                                       e.activation(
                                           out=st_sb2[ks][kt][:, qn * 512:
                                                              (qn + 1) * 512],
                                           in_=pb[:, :], func=AF.Exp, scale=1.0 / 32.0),
                                       waits=[("qkv", p.val("qkv"))] +
                                             ([("avv", max(avv0,
                                                           p.val("avv") - 16))]
                                              if (kt == 0 and qn == 0) else []),
                                       inc="expd")
                        for q2 in range(8):
                            for kt in range(8):
                                w = []
                                if kt == 0:
                                    w = [(f"ldv{ks}", p.val(f"ldv{ks}"))]
                                    if q2 == 0:
                                        w.append(("expd", p.val("expd")))
                                        if kc == 0:
                                            w.append(("recq", p.val("recq")))
                                p.step("tensor",
                                       lambda e, q2=q2, kt=kt, ks=ks, kc=kc: e.matmul(
                                           av_ps[q2],
                                           st_sb2[ks][kt][:, q2 * 128:(q2 + 1) * 128],
                                           vt_sb[ks][kt][:, :],
                                           start=(kc == 0 and kt == 0 and q2 % 3 == 0),
                                           stop=(kc == NKC - 1 and kt == 7),
                                           skip_group_check=True),
                                       waits=w,
                                       inc=("avv" if kt == 7 and q2 == 7 else None),
                                       inc_n=16)
                    for q2 in range(8):
                        p.step("vector",
                               lambda e, q2=q2: e.reciprocal(rec_sb[:, q2:q2 + 1],
                                                             av_ps[q2][:, 128:129]),
                               waits=[("avv", p.val("avv")),
                                      ("trd", max(0, p.val("trd") - 1))], inc="recq")
                        p.wait("vector", [("recq", p.val("recq"))])
                        p.step("vector",
                               lambda e, q2=q2: e.tensor_scalar_mul(
                                   ob_sb[:, 0:128], av_ps[q2][:, 0:128],
                                   rec_sb[:, q2:q2 + 1]),
                               waits=[("trq", p.val("trq"))], inc="nrmv")
                        p.step("tensor",
                               lambda e, q2=q2: e.transpose(
                                   tr_ps[q2 % 2][:, 0:128], ob_sb[:, 0:128],
                                   sident[:, :]),
                               waits=[("nrmv", p.val("nrmv")), ("ld", 16),
                                      ("trd", max(0, p.val("trd") - 1))], inc="trq")
                        gq = h * NQB + qb
                        p.step("vector",
                               lambda e, q2=q2, qb=qb: e.tensor_copy(
                                   af_sb[qb % 2][:, q2 * 128:(q2 + 1) * 128],
                                   tr_ps[q2 % 2][:, 0:128]),
                               waits=[("trq", p.val("trq"))] +
                                     ([(f"staf{qb % 2}", p.val(f"staf{qb % 2}"))]
                                      if q2 == 0 else []),
                               inc="trd")
                    p.step("vector", lambda e: e.tensor_copy(rec_sb[:, 0:1],
                                                             rec_sb[:, 0:1]),
                           waits=[("trd", p.val("trd"))], inc="recq")
                    p.dma("sync", af_s[h * 128:(h + 1) * 128, qb * 1024:(qb + 1) * 1024],
                          af_sb[qb % 2][:, :],
                          waits=[("recq", p.val("recq"))], inc=f"staf{qb % 2}")
        return barrier(waits=[("staf0", p.val("staf0")), ("staf1", p.val("staf1"))])

    def ph_l0oproj(x_src, x_dst, call):
        with contextlib.ExitStack() as pes:
            sb = lambda n_, sh, dt=F32: pes.enter_context(
                nc.sbuf_tensor(f"{call}o{n_}", sh, dt))
            psum = lambda n_, sh, dt=F32: pes.enter_context(
                nc.psum_tensor(f"{call}o{n_}", sh, dt))
            wt_ps = [psum(f"wtp{i}", [128, 1024], BF16) for i in range(2)]
            owt = {}
            obb = {}
            obf = sb("obf", [1, D])
            prev_ev = None
            for s in range(2):
                owt[s] = [sb(f"owt{s}_{j}", [128, D], BF16) for j in range(8)]
                emit_wT(p, nc, pes, f"{call}wo{s}", C["ow2"][s], owt[s], sident,
                        ps=wt_ps, sp="wt",
                        extra_waits=([("wtev", p.val("wtev"))] if prev_ev else []))
                prev_ev = "wtev"
                obb[s] = sb(f"ob_bf{s}", [1, D], BF16)
                p.dma("scalar", obf[0:1, :], C["ob2"][s],
                      waits=[("wtcast", p.val("wtcast"))], inc="wtld")
                p.step("vector",
                       lambda e, s=s: e.tensor_copy(obb[s][0:1, :], obf[0:1, :]),
                       waits=[("wtld", p.val("wtld"))], inc="wtcast")
            af_sb = [sb(f"afsb{k}", [128, HTS], BF16) for k in range(8)]
            for k in range(8):
                p.dma("sync", af_sb[k][:, :], af_s[k * 128:(k + 1) * 128, :],
                      inc="gld")
            xt = [sb(f"xt{i}", [128, D]) for i in range(2)]
            z1 = sb("z1", [128, D]); scr = sb("scr", [128, D])
            base = [sb(f"base{i}", [128, D]) for i in range(2)]
            onew = [sb(f"onew{i}", [128, D]) for i in range(2)]
            mu = sb("mu", [128, 2]); sumsq = sb("sumsq", [128, 2])
            rstd = sb("rstd", [128, 2])
            mm_ps = [psum(f"mmp{i}", [128, 512]) for i in range(4)]
            mmrdy = [("wtev", p.val("wtev")),
                     ("wtcast", p.val("wtcast")), ("ones", 1), ("gld", p.val("gld"))]
            lnd0 = p.val("lnd")
            opmm0 = p.val("opmm")
            for it in range(NT):
                buf = it % 2
                s = it // 16
                p.dma("sync", xt[buf][:, :], x_src[it * 128:(it + 1) * 128, :],
                      waits=[("bsg", max(0, p.val("bsg") - 1))], inc=f"lqx{buf}")
                emit_ln_tile(p, xt[buf][:, :], z1[:, :], scr, mu, sumsq, rstd, None,
                             epsb, waits=[(f"lqx{buf}", p.val(f"lqx{buf}"))],
                             inc="lnd")
                p.step("vector",
                       lambda e, buf=buf: e.scalar_tensor_tensor(
                           out=base[buf][:, :], in0=z1[:, :], scalar=0.5,
                           in1=xt[buf][:, :], op0=ALU.mult, op1=ALU.add),
                       waits=[("lnd", p.val("lnd"))], inc="bsg")
                for hh in range(2):
                    pb = mm_ps[p.val("opmm") % 4]
                    for kk in range(8):
                        p.step("tensor",
                               lambda e, pb=pb, kk=kk, it=it, hh=hh, s=s: e.matmul(
                                   pb[:, :], af_sb[kk][:, it * 128:(it + 1) * 128],
                                   owt[s][kk][:, hh * 512:(hh + 1) * 512],
                                   start=(kk == 0), stop=False),
                               waits=(mmrdy if p.val("opmm") == opmm0 and kk == 0 else
                                      ([("opev", max(0, p.val("opev") - 2))]
                                       if kk == 0 else [])),
                               inc=None)
                    p.step("tensor",
                           lambda e, pb=pb, hh=hh, s=s: e.matmul(
                               pb[:, :], ones_bf[0:1, :],
                               obb[s][0:1, hh * 512:(hh + 1) * 512],
                               start=False, stop=True),
                           inc="opmm")
                    p.step("vector",
                           lambda e, pb=pb, buf=buf, hh=hh: e.scalar_tensor_tensor(
                               out=onew[buf][:, hh * 512:(hh + 1) * 512], in0=pb[:, :],
                               scalar=0.5, in1=base[buf][:, hh * 512:(hh + 1) * 512],
                               op0=ALU.mult, op1=ALU.add),
                           waits=[("opmm", p.val("opmm")), ("bsg", p.val("bsg"))] +
                                 ([(f"osx{buf}", p.val(f"osx{buf}"))]
                                  if hh == 0 else []),
                           inc="opev")
                p.dma("scalar", x_dst[it * 128:(it + 1) * 128, :], onew[buf][:, :],
                      waits=[("opev", p.val("opev"))], inc=f"osx{buf}")
            p.wait("sync", [("osx0", p.val("osx0")), ("osx1", p.val("osx1"))])

    def ph_mamba(x_src, x_dst):
        with contextlib.ExitStack() as pes:
            sb = lambda n_, sh, dt=F32: pes.enter_context(
                nc.sbuf_tensor(f"M{n_}", sh, dt))
            psum = lambda n_, sh, dt=F32: pes.enter_context(
                nc.psum_tensor(f"M{n_}", sh, dt))
            xt = [sb(f"xt{i}", [128, D]) for i in range(2)]
            z1b = sb("z1b", [128, D], BF16)
            scr = sb("scr", [128, D])
            mu = sb("mu", [128, 2]); sumsq = sb("sumsq", [128, 2])
            rstd = sb("rstd", [128, 2])
            ln_fm = sb("ln_fm", [128, 1024], BF16)
            tr_ps = [psum(f"trp{i}", [128, 1024], BF16) for i in range(2)]
            lnd0 = p.val("lnd")
            for it in range(NT):
                buf = it % 2
                p.dma("sync", xt[buf][:, :], x_src[it * 128:(it + 1) * 128, :],
                      waits=[("lnd", max(lnd0, p.val("lnd") - 1))], inc=f"ldx{buf}")
                emit_ln_tile(p, xt[buf][:, :], z1b[:, :], scr, mu, sumsq, rstd, None,
                             epsb, waits=[(f"ldx{buf}", p.val(f"ldx{buf}"))],
                             inc="lnd")
                for k in range(8):
                    p.step("tensor",
                           lambda e, k=k: e.transpose(
                               tr_ps[k % 2][:, 0:128], z1b[:, k * 128:(k + 1) * 128],
                               sident[:, :]),
                           waits=([("lnd", p.val("lnd")), ("ld", 16)] if k == 0 else
                                  [("trev", p.val("trev") - 1)]),
                           inc="trmm")
                    p.step("vector",
                           lambda e, k=k: e.tensor_copy(
                               ln_fm[:, k * 128:(k + 1) * 128], tr_ps[k % 2][:, 0:128]),
                           waits=[("trmm", p.val("trmm")),
                                  ("pst0", p.val("pst0"))], inc="trev")
                for k in range(8):
                    pardma("scalar", sh_ln2[0][k * 128:(k + 1) * 128,
                                               it * 128:(it + 1) * 128],
                           ln_fm[:, k * 128:(k + 1) * 128],
                           sh_ln2[1][k * 128:(k + 1) * 128, it * 128:(it + 1) * 128],
                           ln_fm[:, k * 128:(k + 1) * 128],
                           waits=([("trev", p.val("trev"))] if k == 0 else []),
                           inc="pst0")
            bv1 = barrier(waits=[("pst0", p.val("pst0"))])

        with contextlib.ExitStack() as pes:
            sb = lambda n_, sh, dt=F32: pes.enter_context(
                nc.sbuf_tensor(f"Mb{n_}", sh, dt))
            psum = lambda n_, sh, dt=F32: pes.enter_context(
                nc.psum_tensor(f"Mb{n_}", sh, dt))
            winb = [sb(f"winb{k}", [128, 1024], BF16) for k in range(8)]
            wrsb = [sb(f"wrsb{k}", [128, 1024], BF16) for k in range(8)]
            wld = sb("wld", [128, 1024])
            for k in range(8):
                for src_w, tiles in ((C["win_u"], winb), (C["win_r"], wrsb)):
                    p.dma("scalar", wld[:, :], src_w[k * 128:(k + 1) * 128, :],
                          waits=[("wtcast", p.val("wtcast"))], inc="wtld")
                    p.step("vector",
                           lambda e, t=tiles[k], wld=wld: e.tensor_copy(t[:, :],
                                                                        wld[:, :]),
                           waits=[("wtld", p.val("wtld"))], inc="wtcast")
            wrdy = ("wtcast", p.val("wtcast"))
            rhs = [sb(f"rhs{k}", [128, 512], BF16) for k in range(8)]
            uev = [sb(f"uev{i}", [128, 512], BF16) for i in range(2)]
            mm_ps = [psum(f"mmp{i}", [128, 512]) for i in range(4)]
            for nch in range(TF // 512):
                half = nch // 8
                c0 = (nch % 8) * 512
                for kk in range(8):
                    p.dma("sync", rhs[kk][:, :],
                          sh_ln2[half][kk * 128:(kk + 1) * 128, c0:c0 + 512],
                          waits=[("bard", bv1)] +
                                ([("pmm", p.val("pmm"))] if kk == 0 else []),
                          inc="gld")
                for dt_ in range(16):
                    isu = dt_ < 8
                    dti = dt_ % 8
                    pb = mm_ps[p.val("pmm") % 4]
                    wt = winb if isu else wrsb
                    for kk in range(8):
                        p.step("tensor",
                               lambda e, pb=pb, kk=kk, dti=dti, wt=wt: e.matmul(
                                   pb[:, :], wt[kk][:, dti * 128:(dti + 1) * 128],
                                   rhs[kk][:, :], start=(kk == 0), stop=(kk == 7)),
                               waits=([("gld", p.val("gld")), wrdy,
                                       ("pev", max(0, p.val("pev") - 2))]
                                      if kk == 0 else []),
                               inc=("pmm" if kk == 7 else None))
                    ub = p.val("pev") % 2
                    p.step("vector",
                           lambda e, pb=pb, ub=ub: e.tensor_copy(uev[ub][:, :],
                                                                 pb[:, :]),
                           waits=[("pmm", p.val("pmm")),
                                  (f"pst{ub}", p.val(f"pst{ub}"))], inc="pev")
                    dst = (C["ur_s"] if isu else C["res_s"])[
                        dti * 128:(dti + 1) * 128, nch * 512:(nch + 1) * 512]
                    p.dma("sync", dst, uev[ub][:, :], waits=[("pev", p.val("pev"))],
                          inc=f"pst{ub}")
            ccol = sb("ccol", [128, 3])
            urow = [sb(f"urow{i}", [128, TF], BF16) for i in range(2)]
            ucv = sb("ucv", [128, TF])
            uout = [sb(f"uout{i}", [128, TF], BF16) for i in range(2)]
            for dti in range(8):
                ub = dti % 2
                p.dma("sync", urow[ub][:, :], C["ur_s"][dti * 128:(dti + 1) * 128, :],
                      waits=[("pst0", p.val("pst0")), ("pst1", p.val("pst1")),
                             ("cv", max(0, p.val("cv") - 3))], inc=f"ldx{ub}")
                p.dma("scalar", ccol[:, :], C["wcv_h"][dti * 128:(dti + 1) * 128, :],
                      waits=[("cv", p.val("cv"))], inc="sld")
                p.step("vector",
                       lambda e, ub=ub: e.tensor_scalar_mul(
                           ucv[:, :], urow[ub][:, :], ccol[:, 2:3]),
                       waits=[(f"ldx{ub}", p.val(f"ldx{ub}")), ("sld", p.val("sld")),
                              ("sild", p.val("sild"))], inc="cv")
                p.wait("vector", [("cv", p.val("cv"))])
                p.step("vector",
                       lambda e, ub=ub: e.scalar_tensor_tensor(
                           out=ucv[:, 1:TF], in0=urow[ub][:, 0:TF - 1],
                           scalar=ccol[:, 1:2], in1=ucv[:, 1:TF],
                           op0=ALU.mult, op1=ALU.add), inc="cv")
                p.wait("vector", [("cv", p.val("cv"))])
                p.step("vector",
                       lambda e, ub=ub: e.scalar_tensor_tensor(
                           out=ucv[:, 2:TF], in0=urow[ub][:, 0:TF - 2],
                           scalar=ccol[:, 0:1], in1=ucv[:, 2:TF],
                           op0=ALU.mult, op1=ALU.add), inc="cv")
                p.step("scalar",
                       lambda e, ub=ub: e.activation(out=uout[ub][:, :], in_=ucv[:, :],
                                                     func=AF.Silu),
                       waits=[("cv", p.val("cv")),
                              (f"stu{ub}", p.val(f"stu{ub}"))], inc="sild")
                p.dma("sync", C["u_s"][dti * 128:(dti + 1) * 128, :], uout[ub][:, :],
                      waits=[("sild", p.val("sild"))], inc=f"stu{ub}")
            wxb = [sb(f"wxb{k}", [128, 96], BF16) for k in range(8)]
            wxl = sb("wxl", [128, 96])
            for k in range(8):
                p.dma("scalar", wxl[:, :], C["wx_h"][k * 128:(k + 1) * 128, :],
                      waits=[("wtcast", p.val("wtcast"))], inc="wtld")
                p.step("vector", lambda e, k=k: e.tensor_copy(wxb[k][:, :], wxl[:, :]),
                       waits=[("wtld", p.val("wtld"))], inc="wtcast")
            wxrdy = ("wtcast", p.val("wtcast"))
            uk = [sb(f"uk{i}", [128, 128], BF16) for i in range(4)]
            xde = [sb(f"xde{i}", [128, 96]) for i in range(2)]
            xd_ps = [psum(f"xdp{i}", [128, 512]) for i in range(2)]
            for it in range(TF // 128):
                pb = xd_ps[it % 2]
                for kk in range(8):
                    ub = (it * 8 + kk) % 4
                    p.dma("sync", uk[ub][:, :],
                          C["u_s"][kk * 128:(kk + 1) * 128, it * 128:(it + 1) * 128],
                          waits=[("stu0", p.val("stu0")), ("stu1", p.val("stu1")),
                                 ("xmm", max(0, p.val("xmm") - 3))], inc=f"lduk{ub}")
                    p.step("tensor",
                           lambda e, pb=pb, kk=kk, ub=ub: e.matmul(
                               pb[:, 0:96], uk[ub][:, :], wxb[kk][:, :],
                               start=(kk == 0), stop=(kk == 7)),
                           waits=[(f"lduk{ub}", p.val(f"lduk{ub}"))] +
                                 ([wxrdy, ("xde_", max(0, p.val("xde_") - 1))]
                                  if kk == 0 else []),
                           inc="xmm")
                p.step("vector",
                       lambda e, pb=pb, it=it: e.tensor_copy(xde[it % 2][:, 0:96],
                                                             pb[:, 0:96]),
                       waits=[("xmm", p.val("xmm")),
                              (f"stxd{it % 2}", p.val(f"stxd{it % 2}"))],
                       inc="xde_")
                pardma("scalar", sh_xd[0][it * 128:(it + 1) * 128, :],
                       xde[it % 2][:, 0:96],
                       sh_xd[1][it * 128:(it + 1) * 128, :], xde[it % 2][:, 0:96],
                       waits=[("xde_", p.val("xde_"))], inc=f"stxd{it % 2}")
            bv2 = barrier(waits=[("stxd0", p.val("stxd0")),
                                 ("stxd1", p.val("stxd1"))])

        with contextlib.ExitStack() as pes:
            sb = lambda n_, sh, dt=F32: pes.enter_context(
                nc.sbuf_tensor(f"Ms{n_}", sh, dt))
            psum = lambda n_, sh, dt=F32: pes.enter_context(
                nc.psum_tensor(f"Ms{n_}", sh, dt))
            A_sb = [sb(f"asb{i}", [128, DST]) for i in range(8)]
            ald = sb("ald", [128, DST])
            bdtc = sb("bdtc", [128, 8])
            dpc = sb("dpc", [128, 8])
            wdtl = sb("wdtl", [DTR, 1024])
            wdtb = sb("wdtb", [DTR, 1024], BF16)
            for i in range(8):
                p.dma("scalar", ald[:, :], C["alog_h"][i * 128:(i + 1) * 128, :],
                      waits=[("wtcast", p.val("wtcast"))], inc="wtld")
                p.step("scalar",
                       lambda e, i=i: e.activation(out=A_sb[i][:, :], in_=ald[:, :],
                                                   func=AF.Exp),
                       waits=[("wtld", p.val("wtld"))], inc="sild")
                p.step("vector",
                       lambda e, i=i: e.tensor_scalar_mul(A_sb[i][:, :], A_sb[i][:, :],
                                                          -1.0),
                       waits=[("sild", p.val("sild"))], inc="wtcast")
                p.dma("scalar", bdtc[:, i:i + 1],
                      C["bdt_h"][0:1, i * 128:(i + 1) * 128].rearrange("one b -> b one"),
                      inc="sld")
                p.dma("scalar", dpc[:, i:i + 1],
                      C["dpar_h"][0:1, i * 128:(i + 1) * 128].rearrange("one b -> b one"),
                      inc="sld")
            asb_rdy = ("wtcast", p.val("wtcast"))
            ldc_rdy = ("sld", p.val("sld"))
            p.dma("scalar", wdtl[:, :], C["wdt_h"], inc="wtld")
            p.step("vector", lambda e: e.tensor_copy(wdtb[:, :], wdtl[:, :]),
                   waits=[("wtld", p.val("wtld"))], inc="wtcast")
            wdt_rdy = ("wtcast", p.val("wtcast"))
            dtbc = sb("dtbc", [96, TF], BF16)
            dt_fm = dtbc[0:64, :]
            bc_sb = dtbc[64:96, :]
            xdt = [sb(f"xdt{i}", [128, 96]) for i in range(2)]
            xdt2 = [sb(f"xdt2_{i}", [128, 96]) for i in range(2)]
            xdb = sb("xdb", [128, 96], BF16)
            tp = [psum(f"tp{i}", [128, 1024], BF16) for i in range(2)]
            for blk in range(TF // 128):
                bb = blk % 2
                p.dma("sync", xdt[bb][:, :], sh_xd[0][blk * 128:(blk + 1) * 128, :],
                      waits=[("bard", bv2), ("lnm", p.val("lnm"))],
                      inc=f"ldx{bb}")
                p.dma("sync", xdt2[bb][:, :], sh_xd[1][blk * 128:(blk + 1) * 128, :],
                      inc=f"ldx{bb}")
                p.step("vector",
                       lambda e, bb=bb: e.tensor_add(xdt[bb][:, :], xdt[bb][:, :],
                                                     xdt2[bb][:, :]),
                       waits=[(f"ldx{bb}", p.val(f"ldx{bb}")),
                              ("trq", p.val("trq"))], inc="lnm")
                p.wait("vector", [("lnm", p.val("lnm"))])
                p.step("vector",
                       lambda e, bb=bb: e.tensor_copy(xdb[:, :], xdt[bb][:, :]),
                       inc="xde_")
                p.step("tensor",
                       lambda e: e.transpose(tp[0][0:64, 0:128], xdb[:, 0:64],
                                             sident[:, :]),
                       waits=[("xde_", p.val("xde_")), ("ld", 16),
                              ("trd", max(0, p.val("trd") - 1))], inc="trq")
                p.step("tensor",
                       lambda e: e.transpose(tp[1][64:96, 0:128], xdb[:, 64:96],
                                             sident[:, :], tile_position=(0, 64)),
                       inc="trq")
                p.step("vector",
                       lambda e, blk=blk: e.tensor_copy(
                           dt_fm[:, blk * 128:(blk + 1) * 128], tp[0][0:64, 0:128]),
                       waits=[("trq", p.val("trq"))], inc="trd2")
                p.step("vector",
                       lambda e, blk=blk: e.tensor_copy(
                           bc_sb[:, blk * 128:(blk + 1) * 128], tp[1][64:96, 0:128]),
                       inc="trd")
            p.dma("sync", C["bc_scr"], bc_sb[:, :], waits=[("trd", p.val("trd"))],
                  inc="stbc")
            stbc_rdy = ("stbc", p.val("stbc"))
            L2 = 4096
            delta = sb("delta", [128, L2])
            epsb1 = sb("epsb1", [128, 1])
            p.step("vector", lambda e: e.memset(epsb1[:, :], 1.0), inc="ones")
            zb = [sb(f"zb{i}", [128, 512]) for i in range(2)]
            na = sb("na", [128, 512]); ab = sb("ab", [128, 512])
            e1 = sb("e1", [128, 512])
            u_sl = [sb(f"usl{i}", [128, L2], BF16) for i in range(2)]
            du = sb("du", [128, L2], BF16)
            dA = sb("dA", [128, L2])
            bbc = sb("bbc", [128, L2], BF16)
            dbu = sb("dbu", [128, L2], BF16)
            h_sb = sb("h_sb", [128, L2], BF16)
            cbc = sb("cbc", [128, L2], BF16)
            ytmp = sb("ytmp", [128, L2], BF16)
            yacc = sb("yacc", [128, L2])
            ressl = sb("ressl", [128, L2], BF16)
            yout = sb("yout", [128, L2], BF16)
            stcol = sb("stcol", [128, DST])
            dl_ps = [psum(f"dlp{i}", [128, 512]) for i in range(2)]
            it2 = 0
            for dt2 in range(8):
                for hf in range(2):
                    cols0 = hf * L2
                    ub2 = it2 % 2
                    p.dma("sync", u_sl[ub2][:, :],
                          C["u_s"][dt2 * 128:(dt2 + 1) * 128, cols0:cols0 + L2],
                          waits=[("dud", max(0, p.val("dud") - 1))], inc=f"ldq{ub2}")
                    for lch in range(8):
                        pb = dl_ps[lch % 2]
                        p.step("tensor",
                               lambda e, pb=pb, lch=lch, dt2=dt2, cols0=cols0:
                               e.matmul(pb[:, :],
                                        wdtb[:, dt2 * 128:(dt2 + 1) * 128],
                                        dt_fm[:, cols0 + lch * 512:
                                              cols0 + (lch + 1) * 512],
                                        start=True, stop=True),
                               waits=[wdt_rdy, ("trd2", p.val("trd2")),
                                      ("zbd", max(0, p.val("zbd") - 1))],
                               inc="dmm")
                        zbb = zb[lch % 2]
                        p.step("vector",
                               lambda e, pb=pb, zbb=zbb, dt2=dt2: e.tensor_scalar_add(
                                   zbb[:, :], pb[:, :], bdtc[:, dt2:dt2 + 1]),
                               waits=[("dmm", p.val("dmm")), ldc_rdy,
                                      ("sfp", max(0, p.val("sfp") - 1))], inc="zbd")
                        p.wait("vector", [("zbd", p.val("zbd"))])
                        p.step("vector",
                               lambda e, zbb=zbb: e.tensor_scalar_mul(na[:, :],
                                                                      zbb[:, :], -1.0),
                               inc="nad")
                        p.wait("vector", [("nad", p.val("nad"))])
                        p.step("vector",
                               lambda e, zbb=zbb: e.tensor_max(ab[:, :], zbb[:, :],
                                                               na[:, :]), inc="abd")
                        p.step("vector",
                               lambda e, zbb=zbb, lch=lch: e.tensor_scalar_max(
                                   delta[:, lch * 512:(lch + 1) * 512], zbb[:, :], 0.0),
                               inc="rld")
                        p.step("scalar",
                               lambda e: e.activation(out=e1[:, :], in_=ab[:, :],
                                                      func=AF.Exp, scale=-1.0),
                               waits=[("abd", p.val("abd"))], inc="lng")
                        p.wait("scalar", [("lng", p.val("lng"))])
                        p.step("scalar",
                               lambda e: e.activation(out=e1[:, :], in_=e1[:, :],
                                                      func=AF.Ln, bias=epsb1[:, 0:1]),
                               waits=[("ones", p.val("ones"))], inc="sfp")
                        p.step("vector",
                               lambda e, lch=lch: e.tensor_add(
                                   delta[:, lch * 512:(lch + 1) * 512],
                                   delta[:, lch * 512:(lch + 1) * 512], e1[:, :]),
                               waits=[("sfp", p.val("sfp")), ("rld", p.val("rld"))],
                               inc="deld")
                    p.step("vector",
                           lambda e, ub2=ub2: e.tensor_mul(du[:, :], delta[:, :],
                                                           u_sl[ub2][:, :]),
                           waits=[("deld", p.val("deld")),
                                  (f"ldq{ub2}", p.val(f"ldq{ub2}")),
                                  ("dbud", p.val("dbud"))], inc="dud")
                    for n_ in range(DST):
                        p.step("scalar",
                               lambda e, n_=n_, dt2=dt2: e.activation(
                                   out=dA[:, :], in_=delta[:, :], func=AF.Exp,
                                   scale=A_sb[dt2][:, n_:n_ + 1]),
                               waits=[("deld", p.val("deld")), asb_rdy,
                                      ("scand", p.val("scand"))], inc="dad")
                        p.dma("sync", bbc[:, :],
                              C["bc_scr"][n_:n_ + 1, cols0:cols0 + L2]
                              .partition_broadcast(128),
                              waits=[stbc_rdy, ("dbud", p.val("dbud"))],
                              inc="ldbb")
                        p.step("vector",
                               lambda e: e.tensor_mul(dbu[:, :], du[:, :], bbc[:, :]),
                               waits=[("dud", p.val("dud")), ("ldbb", p.val("ldbb")),
                                      ("scand", p.val("scand"))], inc="dbud")
                        p.wait("vector", [("dbud", p.val("dbud"))])
                        init = 0.0 if hf == 0 else stcol[:, n_:n_ + 1]
                        p.step("vector",
                               lambda e, init=init: e.tensor_tensor_scan(
                                   out=h_sb[:, :], data0=dA[:, :], data1=dbu[:, :],
                                   initial=init, op0=ALU.mult, op1=ALU.add),
                               waits=[("dad", p.val("dad")),
                                      ("ytd", p.val("ytd"))], inc="scand")
                        p.wait("vector", [("scand", p.val("scand"))])
                        if hf == 0:
                            p.step("vector",
                                   lambda e, n_=n_: e.tensor_copy(
                                       stcol[:, n_:n_ + 1], h_sb[:, L2 - 1:L2]),
                                   inc=None)
                        p.dma("scalar", cbc[:, :],
                              C["bc_scr"][16 + n_:17 + n_, cols0:cols0 + L2]
                              .partition_broadcast(128),
                              waits=[stbc_rdy, ("ytd", p.val("ytd"))],
                              inc="ldcb")
                        p.step("vector",
                               lambda e: e.tensor_mul(ytmp[:, :], h_sb[:, :],
                                                      cbc[:, :]),
                               waits=[("ldcb", p.val("ldcb"))], inc="ytd")
                        p.wait("vector", [("ytd", p.val("ytd"))])
                        if n_ == 0:
                            p.step("vector",
                                   lambda e: e.tensor_copy(yacc[:, :], ytmp[:, :]),
                                   inc="yad")
                        else:
                            p.step("vector",
                                   lambda e: e.tensor_add(yacc[:, :], yacc[:, :],
                                                          ytmp[:, :]), inc="yad")
                    p.dma("sync", ressl[:, :],
                          C["res_s"][dt2 * 128:(dt2 + 1) * 128, cols0:cols0 + L2],
                          waits=[("sild", p.val("sild"))], inc="ldrs")
                    p.step("scalar",
                           lambda e: e.activation(out=ressl[:, :], in_=ressl[:, :],
                                                  func=AF.Silu),
                           waits=[("ldrs", p.val("ldrs"))], inc="sild")
                    p.step("vector",
                           lambda e, ub2=ub2, dt2=dt2: e.tensor_scalar(
                               out=du[:, :], in0=u_sl[ub2][:, :],
                               scalar1=dpc[:, dt2:dt2 + 1], scalar2=None,
                               op0=ALU.mult),
                           waits=[("yad", p.val("yad"))], inc="yad")
                    p.wait("vector", [("yad", p.val("yad"))])
                    p.step("vector",
                           lambda e: e.tensor_add(yacc[:, :], yacc[:, :], du[:, :]),
                           inc="yad")
                    p.wait("vector", [("yad", p.val("yad")),
                                      ("sild", p.val("sild"))])
                    p.step("vector",
                           lambda e: e.tensor_mul(yout[:, :], yacc[:, :], ressl[:, :]),
                           waits=[("sty0", p.val("sty0"))], inc="yod")
                    mine0 = (hf == 0)
                    pardma("sync",
                           (C["y_s"] if mine0 else sh_y[0])[
                               dt2 * 128:(dt2 + 1) * 128, :],
                           yout[:, :],
                           (sh_y[1] if mine0 else C["y_s"])[
                               dt2 * 128:(dt2 + 1) * 128, :],
                           yout[:, :],
                           waits=[("yod", p.val("yod"))], inc="sty0")
                    it2 += 1
            bv3 = barrier(waits=[("sty0", p.val("sty0"))])

        with contextlib.ExitStack() as pes:
            sb = lambda n_, sh, dt=F32: pes.enter_context(
                nc.sbuf_tensor(f"Mo{n_}", sh, dt))
            psum = lambda n_, sh, dt=F32: pes.enter_context(
                nc.psum_tensor(f"Mo{n_}", sh, dt))
            pardma("sync", C["y2_s"], sh_y[1], C["y2_s"], sh_y[0],
                   waits=[("bard", bv3)], inc="gld")
            y2rdy = ("gld", p.val("gld"))
            wob = [sb(f"wob{k}", [128, D], BF16) for k in range(16)]
            wld = sb("wld", [128, D])
            for k in range(16):
                src_w = C["wout_m"] if k < 8 else C["wout_t"]
                p.dma("scalar", wld[:, :], src_w[(k % 8) * 128:(k % 8 + 1) * 128, :],
                      waits=[("wtcast", p.val("wtcast"))], inc="wtld")
                p.step("vector",
                       lambda e, k=k, wld=wld: e.tensor_copy(wob[k][:, :],
                                                             wld[:, :]),
                       waits=[("wtld", p.val("wtld"))], inc="wtcast")
            wordy = ("wtcast", p.val("wtcast"))
            yk = [sb(f"yk{i}", [128, 128], BF16) for i in range(16)]
            xt = [sb(f"xt{i}", [128, D]) for i in range(2)]
            onew = [sb(f"onew{i}", [128, D]) for i in range(2)]
            mm_ps = [psum(f"mmp{i}", [128, 512]) for i in range(4)]
            for it in range(NT):
                buf = it % 2
                p.dma("scalar", xt[buf][:, :], x_src[it * 128:(it + 1) * 128, :],
                      waits=[("opev", max(0, p.val("opev") - 2))], inc=f"lqx{buf}")
                for kk in range(16):
                    ub = kk % 4
                    src_y = C["y_s"] if kk < 8 else C["y2_s"]
                    p.dma("sync", yk[kk][:, :],
                          src_y[(kk % 8) * 128:(kk % 8 + 1) * 128,
                                it * 128:(it + 1) * 128],
                          waits=([("opmm", p.val("opmm")), y2rdy]
                                 if kk == 0 else []),
                          inc=f"lduk{ub}")
                for hh in range(2):
                    pb = mm_ps[p.val("opmm") % 4]
                    for kk in range(16):
                        p.step("tensor",
                               lambda e, pb=pb, kk=kk, hh=hh: e.matmul(
                                   pb[:, :], yk[kk][:, :],
                                   wob[kk][:, hh * 512:(hh + 1) * 512],
                                   start=(kk == 0), stop=(kk == 15)),
                               waits=([(f"lduk{u2}", p.val(f"lduk{u2}"))
                                       for u2 in range(4)] +
                                      [wordy, ("opev", max(0, p.val("opev") - 2))]
                                      if (kk == 0 and hh == 0) else []),
                               inc=("opmm" if kk == 15 else None))
                    p.step("vector",
                           lambda e, pb=pb, buf=buf, hh=hh: e.tensor_add(
                               onew[buf][:, hh * 512:(hh + 1) * 512], pb[:, :],
                               xt[buf][:, hh * 512:(hh + 1) * 512]),
                           waits=[("opmm", p.val("opmm")),
                                  (f"lqx{buf}", p.val(f"lqx{buf}"))] +
                                 ([(f"osx{buf}", p.val(f"osx{buf}"))]
                                  if hh == 0 else []),
                           inc="opev")
                p.dma("scalar", x_dst[it * 128:(it + 1) * 128, :], onew[buf][:, :],
                      waits=[("opev", p.val("opev"))], inc=f"osx{buf}")
            p.wait("sync", [("osx0", p.val("osx0")), ("osx1", p.val("osx1"))])

    def ph_l1ffn(x_src, x_dst, call):
        wtmp, w4 = C["wtmp"], C["w4"]
        with contextlib.ExitStack() as pes:
            sb = lambda n_, sh, dt=F32: pes.enter_context(
                nc.sbuf_tensor(f"{call}f{n_}", sh, dt))
            psum = lambda n_, sh, dt=F32: pes.enter_context(
                nc.psum_tensor(f"{call}f{n_}", sh, dt))
            sg1 = sb("sg1", [4, 1024])
            sg2 = sb("sg2", [4, 256])
            sg1b = sb("sg1b", [4, 1024], BF16)
            sg2b = sb("sg2b", [4, 256], BF16)
            wt_ps_full = psum("wt_ps", [128, 512])
            wt_ps = wt_ps_full[:, 0:256]
            wt_sb = sb("wt_sb", [128, 256])
            wc = [sb(f"wc{k}", [128, 1024], BF16) for k in range(8)]
            bias_bc = sb("bias_bc", [128, 1024])
            for i, (g1, g2) in enumerate(((C["g1r"], C["g2r"]),
                                          (C["g1i"], C["g2i"]))):
                p.dma("sync", sg1.rearrange("c (b d) -> c b d", b=32),
                      g1.rearrange("one b c d -> c b d"),
                      waits=[("wtcast", p.val("wtcast"))], inc="gld")
                p.dma("sync", sg2[:, :], g2.rearrange("c e one g -> c (e g)"),
                      inc="gld")
                p.step("vector", lambda e: e.tensor_copy(sg1b[:, :], sg1[:, :]),
                       waits=[("gld", p.val("gld")), ("wttr", p.val("wttr"))])
                p.step("vector", lambda e: e.tensor_copy(sg2b[:, :], sg2[:, :]),
                       inc="wtcast")
                for m in range(8):
                    p.step("tensor",
                           lambda e, m=m: e.matmul(wt_ps,
                                                   sg1b[:, m * 128:(m + 1) * 128],
                                                   sg2b[:, :], start=True, stop=True),
                           waits=[("wtcast", p.val("wtcast")),
                                  ("wtev", p.val("wtev"))],
                           inc="wttr")
                    p.step("vector",
                           lambda e, m=m: e.tensor_copy(
                               wt_sb.rearrange("p (g e) -> p g e", g=16),
                               wt_ps.rearrange("p (e g) -> p g e", g=16)),
                           waits=[("wttr", p.val("wttr")), ("pst0", p.val("pst0"))],
                           inc="wtev")
                    p.dma("scalar", wtmp[i][m * 128:(m + 1) * 128, :], wt_sb[:, :],
                          waits=[("wtev", p.val("wtev"))], inc="pst0")
                w4src = wtmp[i].rearrange("(b d) (g e) -> d g b e", b=32, g=16)
                for q in range(32):
                    eng = ("scalar", "sync")[q % 2]
                    p.dma(eng, w4[i][q:q + 1], w4src[q:q + 1],
                          waits=[("pst0", p.val("pst0"))], inc="gld")
            wcf = sb("wcf", [128, 512])
            for k in range(8):
                half = k // 4
                dk = (k % 4) * 8
                srcs = (0, 1) if half == 0 else (1, 0)
                for j, si in enumerate(srcs):
                    src = w4[si].rearrange("d g b e -> (d g) (b e)")[
                        dk * 16:(dk + 8) * 16, :]
                    p.dma("sync", wcf[:, :], src,
                          waits=[("gld", p.val("gld")),
                                 ("wtcast", p.val("wtcast"))], inc="gld")
                    p.step("vector",
                           lambda e, k=k, j=j: e.tensor_copy(
                               wc[k][:, j * 512:(j + 1) * 512], wcf[:, :]),
                           waits=[("gld", p.val("gld"))], inc="wtcast")
            for k in range(4, 8):
                p.step("vector",
                       lambda e, k=k: e.tensor_scalar_mul(wc[k][:, 0:512],
                                                          wc[k][:, 0:512], -1.0),
                       waits=[("wtcast", p.val("wtcast"))], inc="wtsc")
            p.dma("scalar", bias_bc[:, 0:512], C["brl"].partition_broadcast(128),
                  inc="ldb")
            p.dma("scalar", bias_bc[:, 512:1024], C["bim"].partition_broadcast(128),
                  inc="ldb")
            p.step("vector", lambda e: e.tensor_scalar_mul(bias_bc[:, :],
                                                           bias_bc[:, :], 0.5),
                   waits=[("ldb", p.val("ldb"))], inc="wtcast")
            xt = [sb(f"xt{i}", [128, D]) for i in range(2)]
            scr = sb("scr", [128, D])
            xn_bf = sb("xn_bf", [128, D], BF16)
            xn_fm = sb("xn_fm", [128, D], BF16)
            mu = sb("mu", [128, 2]); sumsq = sb("sumsq", [128, 2])
            rstd = sb("rstd", [128, 2])
            tr_ps_full = [psum(f"tr_ps{i}", [128, 1024], BF16) for i in range(2)]
            tr_ps = [t[:, 0:128] for t in tr_ps_full]
            mm_ps = [psum(f"mm_ps{i}", [128, 512]) for i in range(2)]
            out_t = [sb(f"out_t{i}", [128, D]) for i in range(2)]
            wc_ready = [("wtsc", p.val("wtsc")), ("wtcast", p.val("wtcast")),
                        ("wtcast", p.val("wtcast")), ("ld", 16)]
            lnd0 = p.val("lnd")
            oev0 = p.val("opev")
            for it in range(NT):
                buf = it % 2
                p.dma("sync", xt[buf][:, :], x_src[it * 128:(it + 1) * 128, :],
                      waits=[("opev", max(oev0, p.val("opev") - 1))],
                      inc=f"ldx{buf}")
                emit_ln_tile(p, xt[buf][:, :], xn_bf[:, :], scr, mu, sumsq, rstd,
                             None, epsb,
                             waits=[(f"ldx{buf}", p.val(f"ldx{buf}"))], inc="lnd")
                for k in range(8):
                    p.step("tensor",
                           lambda e, k=k: e.transpose(
                               tr_ps[k % 2], xn_bf[:, k * 128:(k + 1) * 128],
                               sident[:, :]),
                           waits=([("lnd", p.val("lnd"))] if k == 0 else
                                  [("trev", p.val("trev") - 1)]) +
                                 (wc_ready if it == 0 and k == 0 else []),
                           inc="trmm")
                    p.step("vector",
                           lambda e, k=k: e.tensor_copy(
                               xn_fm[:, k * 128:(k + 1) * 128], tr_ps[k % 2]),
                           waits=[("trmm", p.val("trmm"))], inc="trev")
                for h in range(2):
                    for k in range(8):
                        p.step("tensor",
                               lambda e, h=h, k=k: e.matmul(
                                   mm_ps[h][:, :], xn_fm[:, k * 128:(k + 1) * 128],
                                   wc[k][:, h * 512:(h + 1) * 512],
                                   start=(k == 0), stop=(k == 7)),
                               waits=([("trev", p.val("trev"))] if k == 0 else []) +
                                     ([("opev", p.val("opev"))]
                                      if (k == 0 and it > 0) else []),
                               inc=("mmd" if k == 7 else None))
                for h in range(2):
                    p.step("vector",
                           lambda e, h=h, buf=buf: e.scalar_tensor_tensor(
                               out=out_t[buf][:, h * 512:(h + 1) * 512],
                               in0=mm_ps[h][:, :], scalar=0.5,
                               in1=xt[buf][:, h * 512:(h + 1) * 512],
                               op0=ALU.mult, op1=ALU.add),
                           waits=[("mmd", p.val("mmd"))] +
                                 ([(f"osx{buf}", p.val(f"osx{buf}"))]
                                  if h == 0 else []),
                           inc=("bsg" if h == 1 else None))
                p.wait("vector", [("bsg", p.val("bsg"))])
                p.step("vector",
                       lambda e, buf=buf: e.tensor_add(out_t[buf][:, :],
                                                       out_t[buf][:, :],
                                                       bias_bc[:, :]),
                       inc="opev")
                p.dma("scalar", x_dst[it * 128:(it + 1) * 128, :], out_t[buf][:, :],
                      waits=[("opev", p.val("opev"))], inc=f"osx{buf}")
            p.wait("sync", [("osx0", p.val("osx0")), ("osx1", p.val("osx1"))])

    def ph_l1proj(x_src, prev_bv):
        with contextlib.ExitStack() as pes:
            sb = lambda n_, sh, dt=F32: pes.enter_context(
                nc.sbuf_tensor(f"J{n_}", sh, dt))
            psum = lambda n_, sh, dt=F32: pes.enter_context(
                nc.psum_tensor(f"J{n_}", sh, dt))
            wq_sb = [sb(f"wq{i}", [128, D], BF16) for i in range(8)]
            wk_sb = [sb(f"wk{i}", [128, D], BF16) for i in range(8)]
            wv_sb = [sb(f"wv{i}", [128, D], BF16) for i in range(8)]
            wld = sb("wld", [128, D])
            for dram, tiles in ((C["l1wq"], wq_sb), (C["l1wk"], wk_sb),
                                (C["l1wv"], wv_sb)):
                for i in range(8):
                    p.dma("scalar", wld[:, :], dram[i * 128:(i + 1) * 128, :],
                          waits=[("wtcast", p.val("wtcast"))], inc="wtld")
                    p.step("vector",
                           lambda e, t=tiles[i]: e.tensor_copy(t[:, :], wld[:, :]),
                           waits=[("wtld", p.val("wtld"))], inc="wtcast")
            xt = [sb(f"xt{i}", [128, D]) for i in range(2)]
            scr = sb("scr", [128, D])
            xn_bf = sb("xn_bf", [128, D], BF16)
            xn_fm = [sb(f"xnfm{j}", [128, HTS], BF16) for j in range(8)]
            mu = sb("mu", [128, 2]); sumsq = sb("sumsq", [128, 2])
            rstd = sb("rstd", [128, 2])
            tr_ps = [psum(f"trp{i}", [128, 1024], BF16) for i in range(2)]
            lnd0 = p.val("lnd")
            for it in range(NT):
                buf = it % 2
                p.dma("sync", xt[buf][:, :], x_src[it * 128:(it + 1) * 128, :],
                      waits=[("lnd", max(lnd0, p.val("lnd") - 1))], inc=f"ldx{buf}")
                emit_ln_tile(p, xt[buf][:, :], xn_bf[:, :], scr, mu, sumsq, rstd,
                             None, epsb,
                             waits=[(f"ldx{buf}", p.val(f"ldx{buf}"))], inc="lnd")
                for k in range(8):
                    p.step("tensor",
                           lambda e, k=k: e.transpose(
                               tr_ps[k % 2][:, 0:128], xn_bf[:, k * 128:(k + 1) * 128],
                               sident[:, :]),
                           waits=([("lnd", p.val("lnd")), ("ld", 16)] if k == 0 else
                                  [("trev", p.val("trev") - 1)]),
                           inc="trmm")
                    p.step("vector",
                           lambda e, k=k, it=it: e.tensor_copy(
                               xn_fm[k][:, it * 128:(it + 1) * 128],
                               tr_ps[k % 2][:, 0:128]),
                           waits=[("trmm", p.val("trmm"))], inc="trev")
            prq = [("wtcast", p.val("wtcast")), ("trev", p.val("trev"))]
            pmm0 = p.val("pmm")
            mm_ps = [psum(f"mmp{i}", [128, 512]) for i in range(4)]
            ev = sb("ev", [128, 512], BF16)
            evv = sb("evv", [128, 512], BF16)
            for tiles, out in ((wq_sb, qf_s), (wk_sb, kf_s)):
                for mtile in range(8):
                    for nchunk in range(HTS // 512):
                        pb = mm_ps[p.val("pmm") % 4]
                        for kk in range(8):
                            p.step("tensor",
                                   lambda e, pb=pb, kk=kk, mtile=mtile,
                                          nchunk=nchunk, tiles=tiles: e.matmul(
                                       pb[:, :],
                                       tiles[kk][:, mtile * 128:(mtile + 1) * 128],
                                       xn_fm[kk][:, nchunk * 512:(nchunk + 1) * 512],
                                       start=(kk == 0), stop=(kk == 7)),
                                   waits=(prq if p.val("pmm") == pmm0 and kk == 0
                                          else ([("pev", max(0, p.val("pev") - 2))]
                                                if kk == 0 else [])),
                                   inc=("pmm" if kk == 7 else None))
                        g = p.val("pev")
                        slot = g % 2
                        buf_ev = ev if slot else evv
                        p.step("vector",
                               lambda e, pb=pb, buf_ev=buf_ev: e.tensor_copy(
                                   buf_ev[:, :], pb[:, :]),
                               waits=[("pmm", p.val("pmm")),
                                      (f"pst{slot}", p.val(f"pst{slot}"))],
                               inc="pev")
                        p.dma("sync",
                              out[mtile * 128:(mtile + 1) * 128,
                                  nchunk * 512:(nchunk + 1) * 512],
                              buf_ev[:, :], waits=[("pev", p.val("pev"))],
                              inc=f"pst{slot}")
            for ttile in range(NT):
                for half in range(2):
                    pb = mm_ps[p.val("pmm") % 4]
                    for kk in range(8):
                        p.step("tensor",
                               lambda e, pb=pb, kk=kk, ttile=ttile, half=half:
                               e.matmul(
                                   pb[:, :],
                                   xn_fm[kk][:, ttile * 128:(ttile + 1) * 128],
                                   wv_sb[kk][:, half * 512:(half + 1) * 512],
                                   start=(kk == 0), stop=(kk == 7)),
                               waits=([("pev", max(0, p.val("pev") - 2))]
                                      if kk == 0 else []),
                               inc=("pmm" if kk == 7 else None))
                    g = p.val("pev")
                    slot = g % 2
                    buf_ev = ev if slot else evv
                    p.step("vector",
                           lambda e, pb=pb, buf_ev=buf_ev: e.tensor_copy(
                               buf_ev[:, :], pb[:, :]),
                           waits=[("pmm", p.val("pmm")),
                                  (f"pst{slot}", p.val(f"pst{slot}"))], inc="pev")
                    p.dma("sync",
                          vt_s[ttile * 128:(ttile + 1) * 128,
                               half * 512:(half + 1) * 512],
                          buf_ev[:, :], waits=[("pev", p.val("pev"))],
                          inc=f"pst{slot}")
        pardma("sync", sh_k[0], kf_s, sh_k[1], kf_s,
               waits=[("pst0", p.val("pst0")), ("pst1", p.val("pst1")),
                      ("bard", prev_bv)], inc="shst")
        pardma("sync", sh_v[0], vt_s, sh_v[1], vt_s,
               waits=[("shst", p.val("shst"))], inc="shst")
        return barrier(waits=[("shst", p.val("shst"))])

    def ph_l1attn(x_src, x_dst, bv_):
        NQB, NKC = 4, 8
        with contextlib.ExitStack() as pes:
            sb = lambda n_, sh, dt=F32: pes.enter_context(
                nc.sbuf_tensor(f"K{n_}", sh, dt))
            psum = lambda n_, sh, dt=F32: pes.enter_context(
                nc.psum_tensor(f"K{n_}", sh, dt))
            ones1 = sb("ones1", [128, 1], BF16)
            p.step("vector", lambda e: e.memset(ones1[:, :], 1.0), inc="vones")
            qf_sb = [[sb(f"qf{s}_{dk}", [128, 1024], BF16) for dk in range(8)]
                     for s in range(2)]
            kf_sb = [[sb(f"kf{s}_{dk}", [128, 1024], BF16) for dk in range(8)]
                     for s in range(2)]
            vt_sb = [[sb(f"vt{s}_{kt}", [128, 1024], BF16) for kt in range(8)]
                     for s in range(2)]
            st_sb = [sb(f"st{kt}", [128, 1024], BF16) for kt in range(8)]
            o_sb = [sb(f"osb{qt}", [128, 1024]) for qt in range(8)]
            xt_sb = [sb(f"xts{qt}", [128, D]) for qt in range(8)]
            rec = sb("rec", [128, 8])
            outb = [sb(f"outb{i}", [128, D]) for i in range(2)]
            st_ps = [psum(f"stps{i}", [128, 512]) for i in range(2)]
            av_ps = [psum(f"avps{i}", [128, 512]) for i in range(4)]
            dn_ps = psum("dnps", [128, 8])
            qk0 = p.val("qkv")
            avv0 = p.val("avv")
            for qb in range(NQB):
                qs = qb % 2
                for dk in range(8):
                    p.dma("sync", qf_sb[qs][dk][:, :],
                          qf_s[dk * 128:(dk + 1) * 128, qb * 1024:(qb + 1) * 1024],
                          waits=([("qkv", max(qk0, p.val("qkv") - 16 * NKC))]
                                 if qb >= 2 else []),
                          inc=f"ldq{qs}")
                for qt in range(8):
                    p.dma("scalar", xt_sb[qt][:, :],
                          x_src[qb * 1024 + qt * 128:qb * 1024 + (qt + 1) * 128, :],
                          waits=([("nrmv", p.val("nrmv"))] if qb >= 1 else []),
                          inc="lqx0")
                for kc in range(NKC):
                    g = qb * NKC + kc
                    ks = g % 2
                    for dk in range(8):
                        p.dma("sync", kf_sb[ks][dk][:, :],
                              sh_k[kc // 4][dk * 128:(dk + 1) * 128,
                                            (kc % 4) * 1024:(kc % 4 + 1) * 1024],
                              waits=[("bard", bv_)] +
                                    ([("qkv", max(qk0, p.val("qkv") - 16))]
                                     if g >= 2 else []),
                              inc=f"ldk{ks}")
                    for kt in range(8):
                        p.dma("scalar", vt_sb[ks][kt][:, :],
                              sh_v[kc // 4][(kc % 4) * 1024 + kt * 128:
                                            (kc % 4) * 1024 + (kt + 1) * 128, :],
                              waits=[("bard", bv_)] +
                                    ([("avv", max(avv0, p.val("avv") - 16))]
                                     if g >= 2 else []),
                              inc=f"ldv{ks}")
                    for kt in range(8):
                        for qn in range(2):
                            pb = st_ps[p.val("qkv") % 2]
                            for dk in range(8):
                                w = []
                                if dk == 0:
                                    w = [(f"ldk{ks}", p.val(f"ldk{ks}")),
                                         (f"ldq{qs}", p.val(f"ldq{qs}")),
                                         ("expd", max(qk0, p.val("qkv") - 1))]
                                p.step("tensor",
                                       lambda e, pb=pb, dk=dk, kt=kt, qn=qn, ks=ks,
                                              qs=qs: e.matmul(
                                           pb[:, :],
                                           kf_sb[ks][dk][:, kt * 128:(kt + 1) * 128],
                                           qf_sb[qs][dk][:, qn * 512:(qn + 1) * 512],
                                           start=(dk == 0), stop=(dk == 7)),
                                       waits=w, inc=("qkv" if dk == 7 else None))
                            p.step("scalar",
                                   lambda e, pb=pb, kt=kt, qn=qn: e.activation(
                                       out=st_sb[kt][:, qn * 512:(qn + 1) * 512],
                                       in_=pb[:, :], func=AF.Exp, scale=1.0 / 32.0),
                                   waits=[("qkv", p.val("qkv"))] +
                                         ([("avch", p.val("avch"))]
                                          if g >= 1 and kt == 0 and qn == 0 else []),
                                   inc="expd")
                    for qt in range(8):
                        for kt in range(8):
                            first = (kc == 0 and qt == 0 and kt == 0)
                            last = (kc == NKC - 1 and qt == 7 and kt == 7)
                            w = []
                            if qt == 0 and kt == 0:
                                w = [("expd", p.val("expd")),
                                     ("vones", p.val("vones"))]
                                if first and qb >= 1:
                                    w.append(("recq", p.val("recq")))
                            p.step("tensor",
                                   lambda e, qt=qt, kt=kt, first=first, last=last:
                                   e.matmul(dn_ps[:, qt:qt + 1],
                                            st_sb[kt][:, qt * 128:(qt + 1) * 128],
                                            ones1[:, :], start=first, stop=last,
                                            skip_group_check=True),
                                   waits=w, inc=None)
                    for qt in range(8):
                        for dv in range(2):
                            pb = av_ps[p.val("avv") % 4]
                            for kt in range(8):
                                w = []
                                if kt == 0:
                                    w = [(f"ldv{ks}", p.val(f"ldv{ks}")),
                                         ("pev", max(0, p.val("pev") - 3))]
                                    if qt == 0 and dv == 0:
                                        w.append(("expd", p.val("expd")))
                                        if kc == 0 and qb >= 1:
                                            w.append(("nrmv", p.val("nrmv")))
                                p.step("tensor",
                                       lambda e, pb=pb, kt=kt, qt=qt, dv=dv, ks=ks:
                                       e.matmul(pb[:, :],
                                                st_sb[kt][:, qt * 128:(qt + 1) * 128],
                                                vt_sb[ks][kt][:, dv * 512:(dv + 1) * 512],
                                                start=(kt == 0), stop=(kt == 7)),
                                       waits=w, inc=("avv" if kt == 7 else None))
                            if kc == 0:
                                p.step("vector",
                                       lambda e, pb=pb, qt=qt, dv=dv: e.tensor_copy(
                                           o_sb[qt][:, dv * 512:(dv + 1) * 512],
                                           pb[:, :]),
                                       waits=[("avv", p.val("avv"))], inc="pev")
                            else:
                                p.step("vector",
                                       lambda e, pb=pb, qt=qt, dv=dv: e.tensor_add(
                                           o_sb[qt][:, dv * 512:(dv + 1) * 512],
                                           o_sb[qt][:, dv * 512:(dv + 1) * 512],
                                           pb[:, :]),
                                       waits=[("avv", p.val("avv"))], inc="pev")
                    p.step("vector", lambda e: e.tensor_copy(ones1[:, :], ones1[:, :]),
                           waits=[], inc="avch")
                p.step("vector", lambda e: e.reciprocal(rec[:, :], dn_ps[:, :]),
                       waits=[("qkv", p.val("qkv")), ("expd", p.val("expd")),
                              ("pev", p.val("pev"))], inc="recq")
                for qt in range(8):
                    idx = qb * 8 + qt
                    slot = idx % 2
                    ob = outb[slot]
                    p.step("vector",
                           lambda e, qt=qt, ob=ob: e.scalar_tensor_tensor(
                               out=ob[:, :], in0=o_sb[qt][:, :],
                               scalar=rec[:, qt:qt + 1],
                               in1=xt_sb[qt][:, :], op0=ALU.mult, op1=ALU.add),
                           waits=[("recq", p.val("recq")), ("lqx0", p.val("lqx0")),
                                  (f"staf{slot}", p.val(f"staf{slot}"))],
                           inc="nrmv")
                    p.dma("sync",
                          x_dst[qb * 1024 + qt * 128:qb * 1024 + (qt + 1) * 128, :],
                          ob[:, :], waits=[("nrmv", p.val("nrmv"))],
                          inc=f"staf{slot}")
            p.wait("sync", [("staf0", p.val("staf0")), ("staf1", p.val("staf1"))])

    st = {"bv": None, "bva": None, "bvb": None}
    seq = [
        ("l0projA", lambda: st.__setitem__("bv", ph_l0proj(x_in, "A_")), x_in),
        ("l0attnA", lambda: st.__setitem__("bva", ph_l0attn(st["bv"], "A_")),
         x_in),
        ("l0oprojA", lambda: ph_l0oproj(x_in, x_a, "A_"), x_a),
        ("mamba", lambda: ph_mamba(x_a, x_b), x_b),
        ("l0projB",
         lambda: st.__setitem__("bv", ph_l0proj(x_b, "B_", prev_bv=st["bva"])),
         x_b),
        ("l0attnB", lambda: st.__setitem__("bvb", ph_l0attn(st["bv"], "B_")),
         x_b),
        ("l0oprojB", lambda: ph_l0oproj(x_b, x_a, "B_"), x_a),
        ("l1ffnF", lambda: ph_l1ffn(x_a, x_b, "F_"), x_b),
        ("l1proj", lambda: st.__setitem__("bv", ph_l1proj(x_b, prev_bv=st["bvb"])),
         x_b),
        ("l1attn", lambda: ph_l1attn(x_b, x_a, st["bv"]), x_a),
        ("l1ffnG", lambda: ph_l1ffn(x_a, xo, "G_"), xo),
    ]
    stop_after = C.get("stop_after")
    n = len(seq) if stop_after is None else stop_after
    cur = x_in
    for name, fn, xaft in seq[:n]:
        fn()
        p.fence()
        cur = xaft
    if n < len(seq):
        p.dma("sync", xo, cur, inc="gld")
        p.wait("sync", [("gld", p.val("gld"))])


_FUSED_CACHE = {}


def _fused_in_maps(inp, nonce, cores=range(NCORES)):
    x = inp["L0_attn_x"]
    DI_ = 2048
    win = inp["L0_attn_W_in"]
    in_maps = []
    for c in cores:
        par = c % 2
        b = c // 2
        ch = slice(par * 1024, (par + 1) * 1024)
        ch2 = slice((1 - par) * 1024, (2 - par) * 1024)
        im = dict(
            x_in=np.ascontiguousarray(x[b, par * HTS:(par + 1) * HTS]),
            par_in=np.array([[par]], np.int32),
            nonce_in=nonce,
            ident=_IDENT,
            nw=inp["L0_ffn_norm_w"].reshape(1, -1),
            nb=inp["L0_ffn_norm_b"].reshape(1, -1),
            bq=inp["L0_ffn_q_b"].reshape(1, -1),
            bk=inp["L0_ffn_k_b"].reshape(1, -1),
            bv=inp["L0_ffn_v_b"].reshape(1, -1),
            win_u=np.ascontiguousarray(win[:, par * 1024:(par + 1) * 1024]),
            win_r=np.ascontiguousarray(win[:, DI_ + par * 1024:
                                           DI_ + (par + 1) * 1024]),
            wcv_h=np.ascontiguousarray(inp["L0_attn_W_conv"][ch]),
            wx_h=np.ascontiguousarray(inp["L0_attn_W_x"][ch]),
            wdt_h=np.ascontiguousarray(inp["L0_attn_W_dt"][:, ch]),
            bdt_h=np.ascontiguousarray(inp["L0_attn_b_dt"][ch]).reshape(1, -1),
            alog_h=np.ascontiguousarray(inp["L0_attn_A_log"][ch]),
            dpar_h=np.ascontiguousarray(inp["L0_attn_D_param"][ch]).reshape(1, -1),
            wout_m=np.ascontiguousarray(inp["L0_attn_W_out"][ch]),
            wout_t=np.ascontiguousarray(inp["L0_attn_W_out"][ch2]),
            g1r=inp["L1_ffn_G1_R"], g2r=inp["L1_ffn_G2_R"],
            g1i=inp["L1_ffn_G1_I"], g2i=inp["L1_ffn_G2_I"],
            brl=inp["L1_ffn_bias_real"].reshape(1, 512),
            bim=inp["L1_ffn_bias_imag"].reshape(1, 512),
            l1wq=inp["L1_attn_W_Q"], l1wk=inp["L1_attn_W_K"],
            l1wv=inp["L1_attn_W_V"],
        )
        for s in range(2):
            m = 1024 >> (2 * par + s)
            im[f"wq{s}"] = _pad_w(inp["L0_ffn_q_w"], m)
            im[f"wk{s}"] = _pad_w(inp["L0_ffn_k_w"], m)
            im[f"wv{s}"] = _pad_w(inp["L0_ffn_v_w"], m)
            im[f"ow{s}"] = _pad_w(inp["L0_ffn_o_w"], m, rows_too=True)
            im[f"ob{s}"] = np.concatenate(
                [inp["L0_ffn_o_b"][:m], np.zeros(D - m, np.float32)]).reshape(1, -1)
        in_maps.append(im)
    return in_maps


def kernel_fused(inputs):
    inp = {k: np.ascontiguousarray(np.asarray(v, dtype=np.float32))
           for k, v in inputs.items()}
    _ = inp["L0_ffn_router_prob"].shape
    if "nc" not in _FUSED_CACHE:
        _FUSED_CACHE["nc"] = build_fused()
    nc = _FUSED_CACHE["nc"]
    nonce = np.array([[np.random.randint(1, 2 ** 31 - 1)]], np.int32)
    in_maps = _fused_in_maps(inp, nonce)
    res = run_bass_kernel_spmd(nc, in_maps, core_ids=list(range(NCORES)))
    out = np.empty((B, T, D), np.float32)
    for c in range(NCORES):
        par = c % 2
        b = c // 2
        out[b, par * HTS:(par + 1) * HTS] = res.results[c]["xo"]
    return out


# version 06
# speedup vs baseline: 1.6240x; 1.6240x over previous
import contextlib
import math
import numpy as np
import ml_dtypes
import concourse.bass as bass
import concourse.mybir as mybir
from concourse.bass_utils import run_bass_kernel_spmd

F32 = mybir.dt.float32
BF16 = mybir.dt.bfloat16
AF = mybir.ActivationFunctionType
ALU = mybir.AluOpType
AX = mybir.AxisListType
BF = ml_dtypes.bfloat16

NCORES = 8
B, T, D = 4, 8192, 1024
TS = T // NCORES
ROWS = B * TS
NTILES = ROWS // 128
EPS = 1e-5

ENGS = ("sync", "scalar", "vector", "tensor", "gpsimd")


class Plan:

    def __init__(self, nc, es):
        self.nc = nc
        self.es = es
        self.steps = {e: [] for e in ENGS}
        self.sems = {}
        self.counts = {}
        self.pending = {}
        self._last_fence = {}

    def sem(self, name):
        if name not in self.sems:
            self.sems[name] = self.es.enter_context(self.nc.semaphore(name))
            self.counts[name] = 0
        return self.sems[name]

    def val(self, name):
        self.sem(name)
        return self.counts[name]

    def fence(self):
        snap = [(n, c) for n, c in self.counts.items()
                if c > 0 and self._last_fence.get(n) != c]
        for n, c in snap:
            self._last_fence[n] = c
        for e in ENGS:
            self.pending[e] = list(self.pending.get(e, [])) + snap

    def step(self, eng, fn, waits=(), inc=None, inc_n=1):
        pf = self.pending.pop(eng, None)
        if pf:
            waits = pf + list(waits)
        w = [(self.sem(s), v) for (s, v) in waits]
        tgt = None
        if inc is not None:
            self.sem(inc)
            self.counts[inc] += inc_n
            tgt = (self.sems[inc], inc_n)
        self.steps[eng].append((fn, w, tgt))

    def dma(self, eng, out, in_, waits=(), inc="ld", **kw):
        self.step(eng, lambda e, o=out, i=in_: e.dma_start(out=o, in_=i, **kw),
                  waits, inc, 16)

    def wait(self, eng, waits):
        w = [(self.sem(s), v) for (s, v) in waits]
        self.steps[eng].append((None, w, None))

    def emit(self):
        with self.nc.Block() as block:
            for e in ENGS:
                steps = self.steps[e]
                if not steps:
                    continue

                def body(eng, steps=steps):
                    for fn, waits, tgt in steps:
                        for (s, v) in waits:
                            eng.wait_ge(s, v)
                        if fn is None:
                            continue
                        inst = fn(eng)
                        if tgt is not None:
                            inst.then_inc(*tgt)

                getattr(block, e)(body)


def emit_ln_tile(p, x_tile, xn_bf, scr, mu, sumsq, rstd, nmr, epsb=None,
                 waits=(), inc=None, D_=D):
    st = p.step
    st("vector", lambda e: e.reduce_sum(mu[:, 0:1], x_tile, axis=AX.X),
       waits=list(waits) + [("lnr", p.val("lnr"))])
    st("vector", lambda e: e.tensor_mul(scr[:, :], x_tile, x_tile), inc="lnm")
    p.wait("vector", [("lnm", p.val("lnm"))])
    st("vector", lambda e: e.reduce_sum(sumsq[:, 0:1], scr[:, :], axis=AX.X),
       inc="lnv")
    st("scalar", lambda e: e.activation(out=mu[:, 1:2], in_=mu[:, 0:1],
                                        func=AF.Copy, scale=-1.0 / D_),
       waits=[("lnv", p.val("lnv"))])
    st("scalar", lambda e: e.activation(out=sumsq[:, 1:2], in_=mu[:, 0:1],
                                        func=AF.Square, scale=1.0 / D_))
    st("scalar", lambda e: e.activation(out=rstd[:, 0:1], in_=sumsq[:, 0:1],
                                        func=AF.Copy, scale=1.0 / D_), inc="lnq")
    st("vector", lambda e: e.tensor_sub(sumsq[:, 0:1], rstd[:, 0:1], sumsq[:, 1:2]),
       waits=[("lnq", p.val("lnq"))], inc="lnw")
    st("scalar", lambda e: e.activation(out=rstd[:, 0:1], in_=sumsq[:, 0:1],
                                        func=AF.Ln, bias=epsb[:, 0:1]),
       waits=[("lnw", p.val("lnw"))], inc="lng")
    p.wait("scalar", [("lng", p.val("lng"))])
    st("scalar", lambda e: e.activation(out=rstd[:, 1:2], in_=rstd[:, 0:1],
                                        func=AF.Exp, scale=-0.5), inc="lnr")
    st("vector", lambda e: e.tensor_scalar(out=xn_bf, in0=x_tile,
                                           scalar1=mu[:, 1:2], scalar2=rstd[:, 1:2],
                                           op0=ALU.add, op1=ALU.mult),
       waits=[("lnr", p.val("lnr"))], inc=inc)


def np_split_rows(a):
    out = []
    for c in range(NCORES):
        out.append(np.ascontiguousarray(
            a[:, c * TS:(c + 1) * TS].reshape(ROWS, *a.shape[2:])))
    return out


def build_l1ffn(debug=False, ntiles=NTILES):
    nc = bass.Bass("TRN2", target_bir_lowering=False, debug=False)
    x = nc.dram_tensor("x", [ROWS, D], F32, kind="ExternalInput").ap()
    g1r = nc.dram_tensor("g1r", [1, 32, 4, 32], F32, kind="ExternalInput").ap()
    g2r = nc.dram_tensor("g2r", [4, 16, 1, 16], F32, kind="ExternalInput").ap()
    g1i = nc.dram_tensor("g1i", [1, 32, 4, 32], F32, kind="ExternalInput").ap()
    g2i = nc.dram_tensor("g2i", [4, 16, 1, 16], F32, kind="ExternalInput").ap()
    brl = nc.dram_tensor("brl", [1, 512], F32, kind="ExternalInput").ap()
    bim = nc.dram_tensor("bim", [1, 512], F32, kind="ExternalInput").ap()
    ident = nc.dram_tensor("ident", [128, 128], BF16, kind="ExternalInput").ap()
    xo = nc.dram_tensor("xo", [ROWS, D], F32, kind="ExternalOutput").ap()
    if debug:
        dbg_xn = nc.dram_tensor("dbg_xn", [128, D], F32, kind="ExternalOutput").ap()
        dbg_fm = nc.dram_tensor("dbg_fm", [128, D], F32, kind="ExternalOutput").ap()
        dbg_wc = nc.dram_tensor("dbg_wc", [8 * 128, D], F32, kind="ExternalOutput").ap()
        dbg_bb = nc.dram_tensor("dbg_bb", [128, D], F32, kind="ExternalOutput").ap()
        dbg_wt = nc.dram_tensor("dbg_wt", [1024, 256], F32, kind="ExternalOutput").ap()
        dbg_st = nc.dram_tensor("dbg_st", [128, 4], F32, kind="ExternalOutput").ap()

    wtmp = [nc.dram_tensor(f"wtmp{i}", [1024, 256], F32).ap() for i in range(2)]
    w4 = [nc.dram_tensor(f"w4_{i}", [32, 16, 32, 16], F32).ap() for i in range(2)]

    es = contextlib.ExitStack()
    with es:
        sb = lambda n, sh, dt=F32: es.enter_context(nc.sbuf_tensor(n, sh, dt))
        psum = lambda n, sh, dt=F32: es.enter_context(nc.psum_tensor(n, sh, dt))
        p = Plan(nc, es)

        sg1 = sb("sg1", [4, 1024])
        sg2 = sb("sg2", [4, 256])
        sg1b = sb("sg1b", [4, 1024], BF16)
        sg2b = sb("sg2b", [4, 256], BF16)
        wt_ps_full = psum("wt_ps", [128, 512])
        wt_ps = wt_ps_full[:, 0:256]
        wt_sb = sb("wt_sb", [128, 256])
        wc = [sb(f"wc{k}", [128, 1024], BF16) for k in range(8)]
        sident = sb("sident", [128, 128], BF16)
        bias_bc = sb("bias_bc", [128, 1024])

        p.dma("sync", sident[:, :], ident)
        for i, (g1, g2) in enumerate(((g1r, g2r), (g1i, g2i))):
            p.dma("sync", sg1.rearrange("c (b d) -> c b d", b=32),
                  g1.rearrange("one b c d -> c b d"),
                  waits=[("wcast", p.val("wcast"))], inc="ldw")
            p.dma("sync", sg2[:, :], g2.rearrange("c e one g -> c (e g)"), inc="ldw")
            p.step("vector", lambda e, i=i: e.tensor_copy(sg1b[:, :], sg1[:, :]),
                   waits=[("ldw", p.val("ldw")), ("wtmm", p.val("wtmm"))])
            p.step("vector", lambda e: e.tensor_copy(sg2b[:, :], sg2[:, :]),
                   inc="wcast")
            for m in range(8):
                p.step("tensor",
                       lambda e, m=m: e.matmul(wt_ps, sg1b[:, m * 128:(m + 1) * 128],
                                               sg2b[:, :], start=True, stop=True),
                       waits=[("wcast", p.val("wcast")), ("wtev", p.val("wtev"))],
                       inc="wtmm")
                p.step("vector",
                       lambda e, m=m: e.tensor_copy(
                           wt_sb.rearrange("p (g e) -> p g e", g=16),
                           wt_ps.rearrange("p (e g) -> p g e", g=16)),
                       waits=[("wtmm", p.val("wtmm")), ("wtst", p.val("wtst"))],
                       inc="wtev")
                p.dma("scalar", wtmp[i][m * 128:(m + 1) * 128, :], wt_sb[:, :],
                      waits=[("wtev", p.val("wtev"))], inc="wtst")
            w4src = wtmp[i].rearrange("(b d) (g e) -> d g b e", b=32, g=16)
            for q in range(32):
                eng = ("scalar", "sync")[q % 2]
                p.dma(eng, w4[i][q:q + 1], w4src[q:q + 1],
                      waits=[("wtst", p.val("wtst"))], inc="w4st")
        wcf = sb("wcf", [128, 512])
        for k in range(8):
            half = k // 4
            dk = (k % 4) * 8
            srcs = (0, 1) if half == 0 else (1, 0)
            for j, si in enumerate(srcs):
                src = w4[si].rearrange("d g b e -> (d g) (b e)")[
                    dk * 16:(dk + 8) * 16, :]
                p.dma("sync", wcf[:, :], src,
                      waits=[("w4st", p.val("w4st")),
                             ("ldwc", p.val("ldwc"))], inc="ldwcf")
                p.step("vector",
                       lambda e, k=k, j=j: e.tensor_copy(
                           wc[k][:, j * 512:(j + 1) * 512], wcf[:, :]),
                       waits=[("ldwcf", p.val("ldwcf"))], inc="ldwc")
        for k in range(4, 8):
            p.step("vector",
                   lambda e, k=k: e.tensor_scalar_mul(wc[k][:, 0:512],
                                                      wc[k][:, 0:512], -1.0),
                   waits=[("ldwc", p.val("ldwc"))], inc="wneg")
        p.dma("scalar", bias_bc[:, 0:512], brl.partition_broadcast(128), inc="ldb")
        p.dma("scalar", bias_bc[:, 512:1024], bim.partition_broadcast(128), inc="ldb")
        p.step("vector", lambda e: e.tensor_scalar_mul(bias_bc[:, :], bias_bc[:, :], 0.5),
               waits=[("ldb", p.val("ldb"))], inc="bprep")

        xt = [sb(f"xt{i}", [128, D]) for i in range(2)]
        scr = sb("scr", [128, D])
        xn_bf = sb("xn_bf", [128, D], BF16)
        xn_fm = sb("xn_fm", [128, D], BF16)
        mu = sb("mu", [128, 2]); sumsq = sb("sumsq", [128, 2])
        rstd = sb("rstd", [128, 2]); nmr = sb("nmr", [128, 2])
        epsb = sb("epsb", [128, 1])
        p.step("vector", lambda e: e.memset(epsb[:, :], EPS))
        tr_ps_full = [psum(f"tr_ps{i}", [128, 1024], BF16) for i in range(2)]
        tr_ps = [t[:, 0:128] for t in tr_ps_full]
        mm_ps = [psum(f"mm_ps{i}", [128, 512]) for i in range(2)]
        out_t = [sb(f"out_t{i}", [128, D]) for i in range(2)]

        wc_ready = [("wneg", p.val("wneg")), ("bprep", p.val("bprep")),
                    ("ldwc", p.val("ldwc")), ("ld", p.val("ld"))]
        if debug:
            dbg_sb = sb("dbg_sb", [128, D])
            for k in range(8):
                p.step("vector", lambda e, k=k: e.tensor_copy(
                    dbg_sb[:, :], wc[k][:, :]), waits=wc_ready, inc="dbgv")
                p.dma("scalar", dbg_wc[k * 128:(k + 1) * 128, :], dbg_sb[:, :],
                      waits=[("dbgv", p.val("dbgv"))], inc="dbgst")
            p.dma("scalar", dbg_bb, bias_bc[:, :], waits=wc_ready, inc="dbgst")
            p.dma("scalar", dbg_wt, wtmp[0], waits=wc_ready, inc="dbgst")
        for it in range(ntiles):
            buf = it % 2
            p.dma("sync", xt[buf][:, :], x[it * 128:(it + 1) * 128, :],
                  waits=[("oev", max(0, p.val("oev") - 1))], inc=f"ldx{buf}")
            emit_ln_tile(p, xt[buf][:, :], xn_bf[:, :], scr, mu, sumsq, rstd, nmr,
                         epsb, waits=[(f"ldx{buf}", (it // 2 + 1) * 16)], inc="lnd")
            if debug and it == 0:
                p.step("vector", lambda e: e.tensor_copy(dbg_sb[:, :], xn_bf[:, :]),
                       waits=[("lnd", p.val("lnd"))], inc="dbgv2")
                p.dma("scalar", dbg_xn, dbg_sb[:, :],
                      waits=[("dbgv2", p.val("dbgv2"))], inc="dbgst")
                p.step("vector", lambda e: e.tensor_copy(dbg_sb[:, 0:2], mu[:, :]),
                       waits=[("lnd", p.val("lnd"))])
                p.step("vector", lambda e: e.tensor_copy(dbg_sb[:, 2:4], rstd[:, :]),
                       inc="dbgv2")
                p.dma("scalar", dbg_st, dbg_sb[:, 0:4],
                      waits=[("dbgv2", p.val("dbgv2"))], inc="dbgst")
            for k in range(8):
                p.step("tensor",
                       lambda e, k=k, buf=buf: e.transpose(
                           tr_ps[k % 2], xn_bf[:, k * 128:(k + 1) * 128],
                           sident[:, :]),
                       waits=([("lnd", p.val("lnd"))] if k == 0 else
                              [("trev", p.val("trev") - 1)]) + (wc_ready if it == 0 and k == 0 else []),
                       inc="trmm")
                p.step("vector",
                       lambda e, k=k: e.tensor_copy(xn_fm[:, k * 128:(k + 1) * 128],
                                                    tr_ps[k % 2]),
                       waits=[("trmm", p.val("trmm"))], inc="trev")
            if debug and it == 0:
                p.step("vector", lambda e: e.tensor_copy(dbg_sb[:, :], xn_fm[:, :]),
                       waits=[("trev", p.val("trev"))], inc="dbgv3")
                p.dma("scalar", dbg_fm, dbg_sb[:, :],
                      waits=[("dbgv3", p.val("dbgv3"))], inc="dbgst")
            for h in range(2):
                for k in range(8):
                    p.step("tensor",
                           lambda e, h=h, k=k, buf=buf: e.matmul(
                               mm_ps[h][:, :], xn_fm[:, k * 128:(k + 1) * 128],
                               wc[k][:, h * 512:(h + 1) * 512],
                               start=(k == 0), stop=(k == 7)),
                           waits=([("trev", p.val("trev"))] if k == 0 else []) +
                                 ([("oev", p.val("oev"))] if (k == 0 and it > 0) else []),
                           inc=("mmd" if k == 7 else None))
            for h in range(2):
                p.step("vector",
                       lambda e, h=h, buf=buf: e.scalar_tensor_tensor(
                           out=out_t[buf][:, h * 512:(h + 1) * 512],
                           in0=mm_ps[h][:, :], scalar=0.5,
                           in1=xt[buf][:, h * 512:(h + 1) * 512],
                           op0=ALU.mult, op1=ALU.add),
                       waits=[("mmd", p.val("mmd"))] +
                             ([(f"stx{buf}", (it // 2) * 16)] if h == 0 else []),
                       inc=("sttg" if h == 1 else None))
            p.wait("vector", [("sttg", p.val("sttg"))])
            p.step("vector",
                   lambda e, buf=buf: e.tensor_add(out_t[buf][:, :], out_t[buf][:, :],
                                                   bias_bc[:, :]),
                   inc="oev")
            p.dma("scalar", xo[it * 128:(it + 1) * 128, :], out_t[buf][:, :],
                  waits=[("oev", p.val("oev"))], inc=f"stx{buf}")
        final = [("stx0", p.val("stx0")), ("stx1", p.val("stx1"))]
        if debug:
            final.append(("dbgst", p.val("dbgst")))
        p.wait("sync", final)
        p.emit()
    return nc


_L1FFN_CACHE = {}


def run_l1ffn(x_full, G1_R, G2_R, G1_I, G2_I, b_r, b_i):
    if "nc" not in _L1FFN_CACHE:
        _L1FFN_CACHE["nc"] = build_l1ffn()
    nc = _L1FFN_CACHE["nc"]
    ident = np.eye(128, dtype=BF)
    shards = np_split_rows(x_full)
    in_maps = []
    for c in range(NCORES):
        in_maps.append(dict(x=shards[c], g1r=G1_R, g2r=G2_R, g1i=G1_I, g2i=G2_I,
                            brl=b_r.reshape(1, 512), bim=b_i.reshape(1, 512),
                            ident=ident))
    res = run_bass_kernel_spmd(nc, in_maps, core_ids=list(range(NCORES)))
    out = np.empty((B, T, D), np.float32)
    for c in range(NCORES):
        out[:, c * TS:(c + 1) * TS] = res.results[c]["xo"].reshape(B, TS, D)
    return out




def emit_wT(p, nc, es, name, w_dram, wt_sb, sident, n=D, m=D, row_scale=None,
            eng_dma="scalar", ps=None, extra_waits=(), sp=None):
    if sp is None:
        sp = name
    sb = lambda nm, sh, dt=F32: es.enter_context(nc.sbuf_tensor(nm, sh, dt))
    wld = sb(f"{name}_ld", [128, m])
    wbf = sb(f"{name}_bf", [128, m], BF16)
    if ps is None:
        ps = [es.enter_context(nc.psum_tensor(f"{name}_ps{i}", [128, 1024], BF16))
              for i in range(2)]
    nb = n // 128
    mb = m // 128
    for i in range(nb):
        p.dma(eng_dma, wld[:, :], w_dram[i * 128:(i + 1) * 128, :],
              waits=[(f"{sp}cast", p.val(f"{sp}cast"))], inc=f"{sp}ld")
        p.step("vector", lambda e: e.tensor_copy(wbf[:, :], wld[:, :]),
               waits=[(f"{sp}ld", p.val(f"{sp}ld")),
                      (f"{sp}tr", p.val(f"{sp}tr"))], inc=f"{sp}cast")
        for j in range(mb):
            sl = p.val(f"{sp}tr") % 2
            p.step("tensor",
                   lambda e, i=i, j=j, sl=sl: e.transpose(
                       ps[sl][:, 0:128],
                       wbf[:, j * 128:(j + 1) * 128], sident[:, :]),
                   waits=[(f"{sp}cast", p.val(f"{sp}cast")),
                          (f"{sp}ev", max(0, p.val(f"{sp}ev") - 1)),
                          ("ld", 16)] + (list(extra_waits) if (i == 0 and j == 0) else []),
                   inc=f"{sp}tr")
            p.step("vector",
                   lambda e, i=i, j=j, sl=sl: e.tensor_copy(
                       wt_sb[j][:, i * 128:(i + 1) * 128],
                       ps[sl][:, 0:128]),
                   waits=[(f"{sp}tr", p.val(f"{sp}tr"))], inc=f"{sp}ev")
    if row_scale is not None:
        for j in range(mb):
            p.step("vector",
                   lambda e, j=j: e.tensor_scalar_mul(wt_sb[j][:, :], wt_sb[j][:, :],
                                                      row_scale[j]),
                   waits=[(f"{sp}ev", p.val(f"{sp}ev"))], inc=f"{sp}sc")


def build_l1attn_proj():
    nc = bass.Bass("TRN2", target_bir_lowering=False, debug=False)
    x = nc.dram_tensor("x", [ROWS, D], F32, kind="ExternalInput").ap()
    wq = nc.dram_tensor("wq", [D, D], F32, kind="ExternalInput").ap()
    wk = nc.dram_tensor("wk", [D, D], F32, kind="ExternalInput").ap()
    wv = nc.dram_tensor("wv", [D, D], F32, kind="ExternalInput").ap()
    ident = nc.dram_tensor("ident", [128, 128], BF16, kind="ExternalInput").ap()
    qf = nc.dram_tensor("qf", [D, ROWS], BF16, kind="ExternalOutput").ap()
    kf = nc.dram_tensor("kf", [D, ROWS], BF16, kind="ExternalOutput").ap()
    vt = nc.dram_tensor("vt", [ROWS, D], BF16, kind="ExternalOutput").ap()

    es = contextlib.ExitStack()
    with es:
        sb = lambda n_, sh, dt=F32: es.enter_context(nc.sbuf_tensor(n_, sh, dt))
        psum = lambda n_, sh, dt=F32: es.enter_context(nc.psum_tensor(n_, sh, dt))
        p = Plan(nc, es)
        sident = sb("sident", [128, 128], BF16)
        p.dma("sync", sident[:, :], ident)
        wq_sb = [sb(f"wq{i}", [128, D], BF16) for i in range(8)]
        wk_sb = [sb(f"wk{i}", [128, D], BF16) for i in range(8)]
        wv_sb = [sb(f"wv{i}", [128, D], BF16) for i in range(8)]
        wld = sb("wld", [128, D])
        for nm, dram, tiles in (("q", wq, wq_sb), ("k", wk, wk_sb), ("v", wv, wv_sb)):
            for i in range(8):
                p.dma("scalar", wld[:, :], dram[i * 128:(i + 1) * 128, :],
                      waits=[("wcst", p.val("wcst"))], inc="wld")
                p.step("vector",
                       lambda e, t=tiles[i]: e.tensor_copy(t[:, :], wld[:, :]),
                       waits=[("wld", p.val("wld"))], inc="wcst")
        xt = [sb(f"xt{i}", [128, D]) for i in range(2)]
        scr = sb("scr", [128, D])
        xn_bf = sb("xn_bf", [128, D], BF16)
        xn_fm = [sb(f"xnfm{j}", [128, ROWS], BF16) for j in range(8)]
        mu = sb("mu", [128, 2]); sumsq = sb("sumsq", [128, 2])
        rstd = sb("rstd", [128, 2]); epsb = sb("epsb", [128, 1])
        p.step("vector", lambda e: e.memset(epsb[:, :], EPS))
        tr_ps = [es.enter_context(nc.psum_tensor(f"trp{i}", [128, 1024], BF16))
                 for i in range(2)]
        for it in range(NTILES):
            buf = it % 2
            p.dma("sync", xt[buf][:, :], x[it * 128:(it + 1) * 128, :],
                  waits=[("lnd", max(0, p.val("lnd") - 1))], inc=f"ldx{buf}")
            emit_ln_tile(p, xt[buf][:, :], xn_bf[:, :], scr, mu, sumsq, rstd, None,
                         epsb, waits=[(f"ldx{buf}", (it // 2 + 1) * 16)], inc="lnd")
            for k in range(8):
                p.step("tensor",
                       lambda e, k=k: e.transpose(
                           tr_ps[k % 2][:, 0:128], xn_bf[:, k * 128:(k + 1) * 128],
                           sident[:, :]),
                       waits=([("lnd", p.val("lnd")), ("ld", 16)] if k == 0 else
                              [("trev", p.val("trev") - 1)]),
                       inc="trmm")
                p.step("vector",
                       lambda e, k=k, it=it: e.tensor_copy(
                           xn_fm[k][:, it * 128:(it + 1) * 128], tr_ps[k % 2][:, 0:128]),
                       waits=[("trmm", p.val("trmm"))], inc="trev")
        prq = [("wcst", p.val("wcst")), ("trev", p.val("trev"))]
        mm_ps = [psum(f"mmp{i}", [128, 512]) for i in range(4)]
        ev = sb("ev", [128, 512], BF16)
        evv = sb("evv", [128, 512], BF16)
        for nm, tiles, out in (("q", wq_sb, qf), ("k", wk_sb, kf)):
            for mtile in range(8):
                for nchunk in range(ROWS // 512):
                    pb = mm_ps[p.val("pmm") % 4]
                    for kk in range(8):
                        p.step("tensor",
                               lambda e, pb=pb, kk=kk, mtile=mtile, nchunk=nchunk,
                                      tiles=tiles: e.matmul(
                                   pb[:, :], tiles[kk][:, mtile * 128:(mtile + 1) * 128],
                                   xn_fm[kk][:, nchunk * 512:(nchunk + 1) * 512],
                                   start=(kk == 0), stop=(kk == 7)),
                               waits=(prq if p.val("pmm") == 0 and kk == 0 else
                                      ([("pev", max(0, p.val("pev") - 2))] if kk == 0 else [])),
                               inc=("pmm" if kk == 7 else None))
                    g = p.val("pev")
                    slot = g % 2
                    buf_ev = ev if slot else evv
                    p.step("vector",
                           lambda e, pb=pb, buf_ev=buf_ev: e.tensor_copy(buf_ev[:, :], pb[:, :]),
                           waits=[("pmm", p.val("pmm")),
                                  (f"pst{slot}", (g // 2) * 16)], inc="pev")
                    p.dma("sync",
                          out[mtile * 128:(mtile + 1) * 128,
                              nchunk * 512:(nchunk + 1) * 512],
                          buf_ev[:, :], waits=[("pev", p.val("pev"))], inc=f"pst{slot}")
        for ttile in range(NTILES):
            for half in range(2):
                pb = mm_ps[p.val("pmm") % 4]
                for kk in range(8):
                    p.step("tensor",
                           lambda e, pb=pb, kk=kk, ttile=ttile, half=half: e.matmul(
                               pb[:, :], xn_fm[kk][:, ttile * 128:(ttile + 1) * 128],
                               wv_sb[kk][:, half * 512:(half + 1) * 512],
                               start=(kk == 0), stop=(kk == 7)),
                           waits=([("pev", max(0, p.val("pev") - 2))] if kk == 0 else []),
                           inc=("pmm" if kk == 7 else None))
                g = p.val("pev")
                slot = g % 2
                buf_ev = ev if slot else evv
                p.step("vector",
                       lambda e, pb=pb, buf_ev=buf_ev: e.tensor_copy(buf_ev[:, :], pb[:, :]),
                       waits=[("pmm", p.val("pmm")),
                              (f"pst{slot}", (g // 2) * 16)], inc="pev")
                p.dma("sync",
                      vt[ttile * 128:(ttile + 1) * 128, half * 512:(half + 1) * 512],
                      buf_ev[:, :], waits=[("pev", p.val("pev"))], inc=f"pst{slot}")
        p.wait("sync", [("pst0", p.val("pst0")), ("pst1", p.val("pst1"))])
        p.emit()
    return nc


def build_l1attn_attn():
    QR = 4096
    NQB, NKC = 4, 8
    nc = bass.Bass("TRN2", target_bir_lowering=False, debug=False)
    qf = nc.dram_tensor("qf", [D, QR], BF16, kind="ExternalInput").ap()
    kf = nc.dram_tensor("kf", [D, T], BF16, kind="ExternalInput").ap()
    vt = nc.dram_tensor("vt", [T, D], BF16, kind="ExternalInput").ap()
    x = nc.dram_tensor("x", [QR, D], F32, kind="ExternalInput").ap()
    xo = nc.dram_tensor("xo", [QR, D], F32, kind="ExternalOutput").ap()

    es = contextlib.ExitStack()
    with es:
        sb = lambda n_, sh, dt=F32: es.enter_context(nc.sbuf_tensor(n_, sh, dt))
        psum = lambda n_, sh, dt=F32: es.enter_context(nc.psum_tensor(n_, sh, dt))
        p = Plan(nc, es)
        ones_bf = sb("ones_bf", [128, 1], BF16)
        p.step("vector", lambda e: e.memset(ones_bf[:, :], 1.0), inc="ones")
        qf_sb = [[sb(f"qf{s}_{dk}", [128, 1024], BF16) for dk in range(8)]
                 for s in range(2)]
        kf_sb = [[sb(f"kf{s}_{dk}", [128, 1024], BF16) for dk in range(8)]
                 for s in range(2)]
        vt_sb = [[sb(f"vt{s}_{kt}", [128, 1024], BF16) for kt in range(8)]
                 for s in range(2)]
        st_sb = [sb(f"st{kt}", [128, 1024], BF16) for kt in range(8)]
        o_sb = [sb(f"osb{qt}", [128, 1024]) for qt in range(8)]
        xt_sb = [sb(f"xts{qt}", [128, D]) for qt in range(8)]
        rec = sb("rec", [128, 8])
        outb = [sb(f"outb{i}", [128, D]) for i in range(2)]
        st_ps = [psum(f"stps{i}", [128, 512]) for i in range(2)]
        av_ps = [psum(f"avps{i}", [128, 512]) for i in range(4)]
        dn_ps = psum("dnps", [128, 8])

        for qb in range(NQB):
            qs = qb % 2
            for dk in range(8):
                p.dma("sync", qf_sb[qs][dk][:, :],
                      qf[dk * 128:(dk + 1) * 128, qb * 1024:(qb + 1) * 1024],
                      waits=([("qk", 16 * NKC * (qb - 1))] if qb >= 2 else []),
                      inc=f"ldq{qs}")
            for qt in range(8):
                p.dma("scalar", xt_sb[qt][:, :],
                      x[qb * 1024 + qt * 128:qb * 1024 + (qt + 1) * 128, :],
                      waits=([("resv", 8 * qb)] if qb >= 1 else []), inc="ldxt")
            for kc in range(NKC):
                g = qb * NKC + kc
                ks = g % 2
                for dk in range(8):
                    p.dma("sync", kf_sb[ks][dk][:, :],
                          kf[dk * 128:(dk + 1) * 128, kc * 1024:(kc + 1) * 1024],
                          waits=([("qk", 16 * (g - 1))] if g >= 2 else []),
                          inc=f"ldk{ks}")
                for kt in range(8):
                    p.dma("scalar", vt_sb[ks][kt][:, :],
                          vt[kc * 1024 + kt * 128:kc * 1024 + (kt + 1) * 128, :],
                          waits=([("av", 16 * (g - 1))] if g >= 2 else []),
                          inc=f"ldv{ks}")
                for kt in range(8):
                    for qn in range(2):
                        pb = st_ps[p.val("qk") % 2]
                        for dk in range(8):
                            w = []
                            if dk == 0:
                                w = [(f"ldk{ks}", ((g // 2) + 1) * 128),
                                     (f"ldq{qs}", ((qb // 2) + 1) * 128),
                                     ("expd", max(0, p.val("qk") - 1))]
                            p.step("tensor",
                                   lambda e, pb=pb, dk=dk, kt=kt, qn=qn, ks=ks, qs=qs:
                                   e.matmul(pb[:, :],
                                            kf_sb[ks][dk][:, kt * 128:(kt + 1) * 128],
                                            qf_sb[qs][dk][:, qn * 512:(qn + 1) * 512],
                                            start=(dk == 0), stop=(dk == 7)),
                                   waits=w, inc=("qk" if dk == 7 else None))
                        p.step("scalar",
                               lambda e, pb=pb, kt=kt, qn=qn: e.activation(
                                   out=st_sb[kt][:, qn * 512:(qn + 1) * 512],
                                   in_=pb[:, :], func=AF.Exp, scale=1.0 / 32.0),
                               waits=[("qk", p.val("qk"))] +
                                     ([("avch", g)] if g >= 1 and kt == 0 and qn == 0 else []),
                               inc="expd")
                for qt in range(8):
                    for kt in range(8):
                        first = (kc == 0 and qt == 0 and kt == 0)
                        last = (kc == NKC - 1 and qt == 7 and kt == 7)
                        w = []
                        if qt == 0 and kt == 0:
                            w = [("expd", p.val("expd")), ("ones", 1)]
                            if first and qb >= 1:
                                w.append(("recd", qb))
                        p.step("tensor",
                               lambda e, qt=qt, kt=kt, first=first, last=last:
                               e.matmul(dn_ps[:, qt:qt + 1],
                                        st_sb[kt][:, qt * 128:(qt + 1) * 128],
                                        ones_bf[:, :], start=first, stop=last,
                                        skip_group_check=True),
                               waits=w, inc=None)
                for qt in range(8):
                    for dv in range(2):
                        pb = av_ps[p.val("av") % 4]
                        for kt in range(8):
                            w = []
                            if kt == 0:
                                w = [(f"ldv{ks}", ((g // 2) + 1) * 128),
                                     ("ave", max(0, p.val("av") - 3))]
                                if qt == 0 and dv == 0:
                                    w.append(("expd", p.val("expd")))
                                    if kc == 0 and qb >= 1:
                                        w.append(("resv", 8 * qb))
                            p.step("tensor",
                                   lambda e, pb=pb, kt=kt, qt=qt, dv=dv, ks=ks:
                                   e.matmul(pb[:, :],
                                            st_sb[kt][:, qt * 128:(qt + 1) * 128],
                                            vt_sb[ks][kt][:, dv * 512:(dv + 1) * 512],
                                            start=(kt == 0), stop=(kt == 7)),
                                   waits=w, inc=("av" if kt == 7 else None))
                        if kc == 0:
                            p.step("vector",
                                   lambda e, pb=pb, qt=qt, dv=dv: e.tensor_copy(
                                       o_sb[qt][:, dv * 512:(dv + 1) * 512], pb[:, :]),
                                   waits=[("av", p.val("av"))], inc="ave")
                        else:
                            p.step("vector",
                                   lambda e, pb=pb, qt=qt, dv=dv: e.tensor_add(
                                       o_sb[qt][:, dv * 512:(dv + 1) * 512],
                                       o_sb[qt][:, dv * 512:(dv + 1) * 512], pb[:, :]),
                                   waits=[("av", p.val("av"))], inc="ave")
                p.step("vector", lambda e: e.tensor_copy(ones_bf[:, :], ones_bf[:, :]),
                       waits=[], inc="avch")
            p.step("vector", lambda e: e.reciprocal(rec[:, :], dn_ps[:, :]),
                   waits=[("qk", p.val("qk")), ("expd", p.val("expd")),
                          ("ave", p.val("ave"))], inc="recd")
            for qt in range(8):
                idx = qb * 8 + qt
                slot = idx % 2
                ob = outb[slot]
                p.step("vector",
                       lambda e, qt=qt, ob=ob: e.scalar_tensor_tensor(
                           out=ob[:, :], in0=o_sb[qt][:, :], scalar=rec[:, qt:qt + 1],
                           in1=xt_sb[qt][:, :], op0=ALU.mult, op1=ALU.add),
                       waits=[("recd", p.val("recd")), ("ldxt", (qb + 1) * 128),
                              (f"stres{slot}", 16 * (idx // 2))],
                       inc="resv")
                p.dma("sync", xo[qb * 1024 + qt * 128:qb * 1024 + (qt + 1) * 128, :],
                      ob[:, :], waits=[("resv", p.val("resv"))], inc=f"stres{slot}")
        p.wait("sync", [("stres0", p.val("stres0")), ("stres1", p.val("stres1"))])
        p.emit()
    return nc


def build_l0ffn_proj():
    nc = bass.Bass("TRN2", target_bir_lowering=False, debug=False)
    x = nc.dram_tensor("x", [ROWS, D], F32, kind="ExternalInput").ap()
    nw = nc.dram_tensor("nw", [1, D], F32, kind="ExternalInput").ap()
    nb = nc.dram_tensor("nb", [1, D], F32, kind="ExternalInput").ap()
    wq = nc.dram_tensor("wq", [D, D], F32, kind="ExternalInput").ap()
    wk = nc.dram_tensor("wk", [D, D], F32, kind="ExternalInput").ap()
    wv = nc.dram_tensor("wv", [D, D], F32, kind="ExternalInput").ap()
    bq = nc.dram_tensor("bq", [1, D], F32, kind="ExternalInput").ap()
    bk = nc.dram_tensor("bk", [1, D], F32, kind="ExternalInput").ap()
    bv = nc.dram_tensor("bv", [1, D], F32, kind="ExternalInput").ap()
    ident = nc.dram_tensor("ident", [128, 128], BF16, kind="ExternalInput").ap()
    qf = nc.dram_tensor("qf", [D, ROWS], BF16, kind="ExternalOutput").ap()
    kf = nc.dram_tensor("kf", [D, ROWS], BF16, kind="ExternalOutput").ap()
    vt = nc.dram_tensor("vt", [ROWS, D], BF16, kind="ExternalOutput").ap()

    es = contextlib.ExitStack()
    with es:
        sb = lambda n_, sh, dt=F32: es.enter_context(nc.sbuf_tensor(n_, sh, dt))
        psum = lambda n_, sh, dt=F32: es.enter_context(nc.psum_tensor(n_, sh, dt))
        p = Plan(nc, es)
        sident = sb("sident", [128, 128], BF16)
        p.dma("sync", sident[:, :], ident)
        wcol = sb("wcol", [128, 8])
        nbcol = sb("nbcol", [128, 8], BF16)
        nbf = sb("nbf", [128, 8])
        for j in range(8):
            p.dma("sync", wcol[:, j:j + 1],
                  nw[0:1, j * 128:(j + 1) * 128].rearrange("one b -> b one"),
                  inc="ldnw")
            p.dma("sync", nbf[:, j:j + 1],
                  nb[0:1, j * 128:(j + 1) * 128].rearrange("one b -> b one"),
                  inc="ldnw")
        p.step("vector", lambda e: e.tensor_copy(nbcol[:, :], nbf[:, :]),
               waits=[("ldnw", p.val("ldnw"))], inc="nbc")
        weff = {}
        wt_ps_sh = [es.enter_context(nc.psum_tensor(f"wt_ps_sh{i}", [128, 1024], BF16))
                    for i in range(2)]
        prev = None
        for nm, wd in (("q", wq), ("k", wk), ("v", wv)):
            weff[nm] = [sb(f"weff_{nm}{j}", [128, D], BF16) for j in range(8)]
            ew = [(f"w{prev}ev", p.val(f"w{prev}ev"))] if prev else []
            emit_wT(p, nc, es, f"w{nm}", wd, weff[nm], sident, ps=wt_ps_sh,
                    extra_waits=ew)
            prev = nm
        be_ps = psum("be_ps", [128, 512])
        brow = sb("brow", [1, D])
        be_sb = {}
        for nm, bd in (("q", bq), ("k", bk), ("v", bv)):
            be_sb[nm] = sb(f"be_{nm}", [1, D], BF16)
            p.dma("scalar", brow[0:1, :], bd, waits=[("bev", p.val("bev"))], inc="ldb")
            for half in range(2):
                for kk in range(8):
                    p.step("tensor",
                           lambda e, kk=kk, half=half, nm=nm: e.matmul(
                               be_ps[0:1, :], nbcol[:, kk:kk + 1],
                               weff[nm][kk][:, half * 512:(half + 1) * 512],
                               start=(kk == 0), stop=(kk == 7)),
                           waits=([("nbc", p.val("nbc")),
                                   (f"w{nm}ev", p.val(f"w{nm}ev")),
                                   ("bevv", p.val("bevv"))] if kk == 0 else []),
                           inc=("bemm" if kk == 7 else None))
                p.step("vector",
                       lambda e, half=half, nm=nm: e.tensor_add(
                           be_sb[nm][0:1, half * 512:(half + 1) * 512],
                           be_ps[0:1, :], brow[0:1, half * 512:(half + 1) * 512]),
                       waits=[("bemm", p.val("bemm")), ("ldb", p.val("ldb"))],
                       inc="bevv")
            p.step("vector", lambda e: e.tensor_copy(brow[0:1, 0:1], brow[0:1, 0:1]),
                   waits=[("bevv", p.val("bevv"))], inc="bev")
            for j in range(8):
                p.step("vector",
                       lambda e, j=j, nm=nm: e.tensor_scalar_mul(
                           weff[nm][j][:, :], weff[nm][j][:, :], wcol[:, j:j + 1]),
                       waits=[("bemm", p.val("bemm")), ("ldnw", p.val("ldnw"))],
                       inc=f"w{nm}sc")
        ones_bf = sb("ones_row", [1, 128], BF16)
        ones_512 = sb("ones_512", [1, 512], BF16)
        p.step("vector", lambda e: e.memset(ones_bf[:, :], 1.0))
        p.step("vector", lambda e: e.memset(ones_512[:, :], 1.0), inc="ones", inc_n=1)

        xt = [sb(f"xt{i}", [128, D]) for i in range(2)]
        z1 = sb("z1", [128, D])
        scr = sb("scr", [128, D])
        xn_bf = sb("xn_bf", [128, D], BF16)
        xn_fm = [sb(f"xnfm{j}", [128, ROWS], BF16) for j in range(8)]
        mu = sb("mu", [128, 2]); sumsq = sb("sumsq", [128, 2])
        rstd = sb("rstd", [128, 2]); epsb = sb("epsb", [128, 1])
        mu2 = sb("mu2", [128, 2]); sumsq2 = sb("sumsq2", [128, 2])
        rstd2 = sb("rstd2", [128, 2])
        p.step("vector", lambda e: e.memset(epsb[:, :], EPS))
        tr_ps = [es.enter_context(nc.psum_tensor(f"trp{i}", [128, 1024], BF16))
                 for i in range(2)]
        for it in range(NTILES):
            buf = it % 2
            p.dma("sync", xt[buf][:, :], x[it * 128:(it + 1) * 128, :],
                  waits=[("lnd", max(0, p.val("lnd") - 1))], inc=f"ldx{buf}")
            emit_ln_tile(p, xt[buf][:, :], z1[:, :], scr, mu, sumsq, rstd, None,
                         epsb, waits=[(f"ldx{buf}", (it // 2 + 1) * 16)], inc="ln1")
            emit_ln_tile(p, z1[:, :], xn_bf[:, :], scr, mu2, sumsq2, rstd2, None,
                         epsb, waits=[("ln1", p.val("ln1"))], inc="lnd")
            for k in range(8):
                p.step("tensor",
                       lambda e, k=k: e.transpose(
                           tr_ps[k % 2][:, 0:128], xn_bf[:, k * 128:(k + 1) * 128],
                           sident[:, :]),
                       waits=([("lnd", p.val("lnd")), ("ld", 16)] if k == 0 else
                              [("trev", p.val("trev") - 1)]),
                       inc="trmm")
                p.step("vector",
                       lambda e, k=k, it=it: e.tensor_copy(
                           xn_fm[k][:, it * 128:(it + 1) * 128], tr_ps[k % 2][:, 0:128]),
                       waits=[("trmm", p.val("trmm"))], inc="trev")
        prq = [("bev", p.val("bev")), ("trev", p.val("trev")), ("ones", 1),
               ("wqsc", p.val("wqsc")), ("wksc", p.val("wksc")),
               ("wvsc", p.val("wvsc"))]
        mm_ps = [psum(f"mmp{i}", [128, 512]) for i in range(3)]
        ev = sb("ev", [128, 512], BF16)
        evv = sb("evv", [128, 512], BF16)
        for nm, out, tm in (("q", qf, False), ("k", kf, False), ("v", vt, True)):
            n_out = NTILES if tm else 8
            for mtile in range(n_out):
                for nchunk in range(2 if tm else ROWS // 512):
                    pb = mm_ps[p.val("pmm") % 3]
                    for kk in range(8):
                        if tm:
                            fn = (lambda e, pb=pb, kk=kk, mtile=mtile, nchunk=nchunk, nm=nm:
                                  e.matmul(pb[:, :],
                                           xn_fm[kk][:, mtile * 128:(mtile + 1) * 128],
                                           weff[nm][kk][:, nchunk * 512:(nchunk + 1) * 512],
                                           start=(kk == 0), stop=False))
                        else:
                            fn = (lambda e, pb=pb, kk=kk, mtile=mtile, nchunk=nchunk, nm=nm:
                                  e.matmul(pb[:, :],
                                           weff[nm][kk][:, mtile * 128:(mtile + 1) * 128],
                                           xn_fm[kk][:, nchunk * 512:(nchunk + 1) * 512],
                                           start=(kk == 0), stop=False))
                        p.step("tensor", fn,
                               waits=(prq if p.val("pmm") == 0 and kk == 0 else
                                      ([("pev", max(0, p.val("pev") - 1))] if kk == 0 else [])),
                               inc=None)
                    if tm:
                        bfn = (lambda e, pb=pb, mtile=mtile, nchunk=nchunk, nm=nm:
                               e.matmul(pb[:, :], ones_bf[0:1, :],
                                        be_sb[nm][0:1, nchunk * 512:(nchunk + 1) * 512],
                                        start=False, stop=True))
                    else:
                        bfn = (lambda e, pb=pb, mtile=mtile, nm=nm:
                               e.matmul(pb[:, :],
                                        be_sb[nm][0:1, mtile * 128:(mtile + 1) * 128],
                                        ones_512[0:1, :], start=False, stop=True))
                    p.step("tensor", bfn, inc="pmm")
                    g = p.val("pev")
                    slot = g % 2
                    buf_ev = ev if slot else evv
                    p.step("vector",
                           lambda e, pb=pb, buf_ev=buf_ev: e.tensor_copy(buf_ev[:, :], pb[:, :]),
                           waits=[("pmm", p.val("pmm")), (f"pst{slot}", (g // 2) * 16)],
                           inc="pev")
                    dst = (out[mtile * 128:(mtile + 1) * 128,
                               nchunk * 512:(nchunk + 1) * 512])
                    p.dma("sync", dst, buf_ev[:, :],
                          waits=[("pev", p.val("pev"))], inc=f"pst{slot}")
        p.wait("sync", [("pst0", p.val("pst0")), ("pst1", p.val("pst1"))])
        p.emit()
    return nc


def build_l0ffn_attn():
    NP_, NQB, NKC = 4, 8, 8
    QBS = 1024
    nc = bass.Bass("TRN2", target_bir_lowering=False, debug=False)
    qf4 = nc.dram_tensor("qf4", [NP_, 128, T], BF16, kind="ExternalInput").ap()
    kf4 = nc.dram_tensor("kf4", [NP_, 128, T], BF16, kind="ExternalInput").ap()
    vt4 = nc.dram_tensor("vt4", [NP_, T, 128], BF16, kind="ExternalInput").ap()
    ident = nc.dram_tensor("ident", [128, 128], BF16, kind="ExternalInput").ap()
    af = nc.dram_tensor("af", [NP_, 128, T], BF16, kind="ExternalOutput").ap()

    es = contextlib.ExitStack()
    with es:
        sb = lambda n_, sh, dt=F32: es.enter_context(nc.sbuf_tensor(n_, sh, dt))
        psum = lambda n_, sh, dt=F32: es.enter_context(nc.psum_tensor(n_, sh, dt))
        p = Plan(nc, es)
        sident = sb("sident", [128, 128], BF16)
        p.dma("sync", sident[:, :], ident)
        qf_sb = [sb(f"qfs{s}", [128, T], BF16) for s in range(2)]
        kf_sb = [sb(f"kfs{s}", [128, 1024], BF16) for s in range(2)]
        vt_sb = [[sb(f"vts{s}_{kt}", [128, 132], BF16) for kt in range(8)]
                 for s in range(2)]
        for s in range(2):
            for kt in range(8):
                p.step("vector",
                       lambda e, s=s, kt=kt: e.memset(vt_sb[s][kt][:, 128:132], 1.0),
                       inc="vones")
        st_sb = [sb(f"st{kt}", [128, QBS], BF16) for kt in range(8)]
        ob_sb = sb("ob_sb", [128, 132], BF16)
        mkr = sb("mkr", [1, 4])
        p.step("vector", lambda e: e.memset(mkr[:, :], 0.0), inc="mkri")
        af_sb = [sb(f"afs{i}", [128, QBS], BF16) for i in range(2)]
        rec_sb = sb("rec_sb", [128, 16])
        st_ps = [psum(f"stps{i}", [128, 512]) for i in range(2)]
        av_big = es.enter_context(nc.psum_tensor("av_big", [128, 1536]))
        av_ps = [av_big[:, (s // 3) * 512 + (s % 3) * 160:
                        (s // 3) * 512 + (s % 3) * 160 + 132] for s in range(8)]
        tr_ps = [es.enter_context(nc.psum_tensor(f"trp{i}", [128, 1024], BF16))
                 for i in range(2)]

        for pr in range(NP_):
            ps_ = pr % 2
            p.dma("sync", qf_sb[ps_][:, :], qf4[pr],
                  waits=([("qkv", 16 * NKC * NQB * (pr - 1))] if pr >= 2 else []),
                  inc=f"ldq{ps_}")
            for qb in range(NQB):
                for kc in range(NKC):
                    g = (pr * NQB + qb) * NKC + kc
                    ks = g % 2
                    p.dma("sync", kf_sb[ks][:, :],
                          kf4[pr][:, kc * 1024:(kc + 1) * 1024],
                          waits=([("qkv", 16 * (g - 1))] if g >= 2 else []),
                          inc=f"ldk{ks}")
                    for kt in range(8):
                        p.dma("scalar", vt_sb[ks][kt][:, 0:128],
                              vt4[pr][kc * 1024 + kt * 128:kc * 1024 + (kt + 1) * 128, :],
                              waits=([("avv", 16 * (g - 1))] if g >= 2 else
                                     ([("vones", p.val("vones"))] if g == 0 else [])),
                              inc=f"ldv{ks}")
                    for kt in range(8):
                        for qn in range(QBS // 512):
                            pb = st_ps[p.val("qkv") % 2]
                            p.step("tensor",
                                   lambda e, pb=pb, kt=kt, qn=qn, ks=ks, ps_=ps_, qb=qb:
                                   e.matmul(pb[:, :],
                                            kf_sb[ks][:, kt * 128:(kt + 1) * 128],
                                            qf_sb[ps_][:, qb * QBS + qn * 512:
                                                       qb * QBS + (qn + 1) * 512],
                                            start=True, stop=True),
                                   waits=[(f"ldk{ks}", ((g // 2) + 1) * 16),
                                          (f"ldq{ps_}", ((pr // 2) + 1) * 16),
                                          ("expd", max(0, p.val("qkv") - 1))] +
                                         ([("avch", g)] if (kt == 0 and qn == 0 and g >= 1) else []),
                                   inc="qkv")
                            p.step("scalar",
                                   lambda e, pb=pb, kt=kt, qn=qn: e.activation(
                                       out=st_sb[kt][:, qn * 512:(qn + 1) * 512],
                                       in_=pb[:, :], func=AF.Exp, scale=1.0 / 32.0),
                                   waits=[("qkv", p.val("qkv"))], inc="expd")
                    for qs2 in range(8):
                        for kt in range(8):
                            w = []
                            if kt == 0:
                                w = [(f"ldv{ks}", ((g // 2) + 1) * 128)]
                                if qs2 == 0:
                                    w.append(("expd", p.val("expd")))
                                    if kc == 0:
                                        w.append(("nrm", p.val("nrm")))
                            p.step("tensor",
                                   lambda e, qs2=qs2, kt=kt, ks=ks, kc=kc: e.matmul(
                                       av_ps[qs2],
                                       st_sb[kt][:, qs2 * 128:(qs2 + 1) * 128],
                                       vt_sb[ks][kt][:, :],
                                       start=(kc == 0 and kt == 0 and qs2 % 3 == 0),
                                       stop=(kc == NKC - 1 and kt == 7),
                                       skip_group_check=True),
                                   waits=w, inc=("avv" if kt == 7 and qs2 == 7 else None),
                                   inc_n=16)
                    p.step("scalar", lambda e: e.copy(mkr[0:1, 0:1], mkr[0:1, 2:3]),
                           waits=[("avv", p.val("avv")), ("mkri", 1)], inc="avch")
                for qs2 in range(8):
                    p.step("vector",
                           lambda e, qs2=qs2: e.reciprocal(rec_sb[:, qs2:qs2 + 1],
                                                           av_ps[qs2][:, 128:129]),
                           waits=[("avv", p.val("avv")), ("trd", max(0, p.val("trd") - 1))],
                           inc="recq")
                    p.wait("vector", [("recq", p.val("recq"))])
                    p.step("vector",
                           lambda e, qs2=qs2: e.tensor_scalar_mul(
                               ob_sb[:, 0:128], av_ps[qs2][:, 0:128],
                               rec_sb[:, qs2:qs2 + 1]),
                           waits=[("trq", p.val("trq"))], inc="nrmv")
                    p.step("tensor",
                           lambda e, qs2=qs2: e.transpose(
                               tr_ps[qs2 % 2][:, 0:128],
                               ob_sb[:, 0:128], sident[:, :]),
                           waits=[("nrmv", p.val("nrmv")), ("ld", 16),
                                  ("trd", max(0, p.val("trd") - 1))], inc="trq")
                    gq = pr * NQB + qb
                    p.step("vector",
                           lambda e, qs2=qs2, qb=qb: e.tensor_copy(
                               af_sb[qb % 2][:, qs2 * 128:(qs2 + 1) * 128],
                               tr_ps[qs2 % 2][:, 0:128]),
                           waits=[("trq", p.val("trq"))] +
                                 ([(f"staf{qb % 2}", 16 * (gq // 2))] if qs2 == 0 else []),
                           inc="trd")
                p.step("vector", lambda e: e.tensor_copy(rec_sb[:, 0:1], rec_sb[:, 0:1]),
                       waits=[("trd", p.val("trd"))], inc="nrm")
                p.dma("sync", af[pr][:, qb * QBS:(qb + 1) * QBS], af_sb[qb % 2][:, :],
                      waits=[("nrm", p.val("nrm")),
                             (f"staf{qb % 2}", 16 * ((pr * NQB + qb) // 2))],
                      inc=f"staf{qb % 2}")
        p.wait("sync", [("staf0", p.val("staf0")), ("staf1", p.val("staf1"))])
        p.emit()
    return nc


def build_l0ffn_oproj():
    nc = bass.Bass("TRN2", target_bir_lowering=False, debug=False)
    x = nc.dram_tensor("x", [ROWS, D], F32, kind="ExternalInput").ap()
    af = nc.dram_tensor("af", [D, ROWS], BF16, kind="ExternalInput").ap()
    ow = nc.dram_tensor("ow", [D, D], F32, kind="ExternalInput").ap()
    ob = nc.dram_tensor("ob", [1, D], F32, kind="ExternalInput").ap()
    ident = nc.dram_tensor("ident", [128, 128], BF16, kind="ExternalInput").ap()
    xo = nc.dram_tensor("xo", [ROWS, D], F32, kind="ExternalOutput").ap()

    es = contextlib.ExitStack()
    with es:
        sb = lambda n_, sh, dt=F32: es.enter_context(nc.sbuf_tensor(n_, sh, dt))
        psum = lambda n_, sh, dt=F32: es.enter_context(nc.psum_tensor(n_, sh, dt))
        p = Plan(nc, es)
        sident = sb("sident", [128, 128], BF16)
        p.dma("sync", sident[:, :], ident)
        owt = [sb(f"owt{j}", [128, D], BF16) for j in range(8)]
        emit_wT(p, nc, es, "wo", ow, owt, sident)
        obf = sb("obf", [1, D])
        ob_bf = sb("ob_bf", [1, D], BF16)
        p.dma("scalar", obf[0:1, :], ob, inc="ldob")
        p.step("vector", lambda e: e.tensor_copy(ob_bf[0:1, :], obf[0:1, :]),
               waits=[("ldob", p.val("ldob"))], inc="obc")
        ones_bf = sb("ones_row", [1, 128], BF16)
        p.step("vector", lambda e: e.memset(ones_bf[:, :], 1.0), inc="ones")
        af_sb = [sb(f"afsb{k}", [128, ROWS], BF16) for k in range(8)]
        for k in range(8):
            p.dma("sync", af_sb[k][:, :], af[k * 128:(k + 1) * 128, :], inc="ldaf")

        xt = [sb(f"xt{i}", [128, D]) for i in range(2)]
        z1 = sb("z1", [128, D])
        scr = sb("scr", [128, D])
        base = [sb(f"base{i}", [128, D]) for i in range(2)]
        onew = [sb(f"onew{i}", [128, D]) for i in range(2)]
        mu = sb("mu", [128, 2]); sumsq = sb("sumsq", [128, 2])
        rstd = sb("rstd", [128, 2]); epsb = sb("epsb", [128, 1])
        p.step("vector", lambda e: e.memset(epsb[:, :], EPS))
        mm_ps = [psum(f"mmp{i}", [128, 512]) for i in range(4)]
        mmrdy = [("woev", p.val("woev")), ("obc", p.val("obc")), ("ones", 1),
                 ("ldaf", p.val("ldaf"))]
        for it in range(NTILES):
            buf = it % 2
            p.dma("sync", xt[buf][:, :], x[it * 128:(it + 1) * 128, :],
                  waits=[("bsg", max(0, p.val("bsg") - 1))], inc=f"ldx{buf}")
            emit_ln_tile(p, xt[buf][:, :], z1[:, :], scr, mu, sumsq, rstd, None,
                         epsb, waits=[(f"ldx{buf}", (it // 2 + 1) * 16)], inc="lnd")
            p.step("vector",
                   lambda e, buf=buf: e.scalar_tensor_tensor(
                       out=base[buf][:, :], in0=z1[:, :], scalar=0.5,
                       in1=xt[buf][:, :], op0=ALU.mult, op1=ALU.add),
                   waits=[("lnd", p.val("lnd"))], inc="bsg")
            for h in range(2):
                pb = mm_ps[p.val("pmm") % 4]
                for kk in range(8):
                    p.step("tensor",
                           lambda e, pb=pb, kk=kk, it=it, h=h: e.matmul(
                               pb[:, :], af_sb[kk][:, it * 128:(it + 1) * 128],
                               owt[kk][:, h * 512:(h + 1) * 512],
                               start=(kk == 0), stop=False),
                           waits=(mmrdy if p.val("pmm") == 0 and kk == 0 else
                                  ([("pev", max(0, p.val("pev") - 2))] if kk == 0 else [])),
                           inc=None)
                p.step("tensor",
                       lambda e, pb=pb, h=h: e.matmul(
                           pb[:, :], ones_bf[0:1, :],
                           ob_bf[0:1, h * 512:(h + 1) * 512], start=False, stop=True),
                       inc="pmm")
                p.step("vector",
                       lambda e, pb=pb, buf=buf, h=h: e.scalar_tensor_tensor(
                           out=onew[buf][:, h * 512:(h + 1) * 512], in0=pb[:, :],
                           scalar=0.5, in1=base[buf][:, h * 512:(h + 1) * 512],
                           op0=ALU.mult, op1=ALU.add),
                       waits=[("pmm", p.val("pmm")), ("bsg", p.val("bsg"))] +
                             ([(f"stx{buf}", 16 * (it // 2))] if h == 0 else []),
                       inc="pev")
            p.dma("scalar", xo[it * 128:(it + 1) * 128, :], onew[buf][:, :],
                  waits=[("pev", p.val("pev"))], inc=f"stx{buf}")
        p.wait("sync", [("stx0", p.val("stx0")), ("stx1", p.val("stx1"))])
        p.emit()
    return nc


DI = 2048
DTR = 64
DST = 16
SEG = TS + 2


def build_mamba_pre():
    nc = bass.Bass("TRN2", target_bir_lowering=False, debug=False)
    x = nc.dram_tensor("x", [ROWS, D], F32, kind="ExternalInput").ap()
    xh = nc.dram_tensor("xh", [8, D], F32, kind="ExternalInput").ap()
    win = nc.dram_tensor("win", [D, 2 * DI], F32, kind="ExternalInput").ap()
    wcv = nc.dram_tensor("wcv", [DI, 3], F32, kind="ExternalInput").ap()
    wx = nc.dram_tensor("wx", [DI, 96], F32, kind="ExternalInput").ap()
    ident = nc.dram_tensor("ident", [128, 128], BF16, kind="ExternalInput").ap()
    u_o = nc.dram_tensor("u_o", [DI, ROWS], BF16, kind="ExternalOutput").ap()
    res_o = nc.dram_tensor("res_o", [DI, ROWS], BF16, kind="ExternalOutput").ap()
    xdbl_o = nc.dram_tensor("xdbl_o", [ROWS, 96], F32, kind="ExternalOutput").ap()

    es = contextlib.ExitStack()
    with es:
        sb = lambda n_, sh, dt=F32: es.enter_context(nc.sbuf_tensor(n_, sh, dt))
        psum = lambda n_, sh, dt=F32: es.enter_context(nc.psum_tensor(n_, sh, dt))
        p = Plan(nc, es)
        sident = sb("sident", [128, 128], BF16)
        p.dma("sync", sident[:, :], ident)
        xt = [sb(f"xt{i}", [128, D]) for i in range(2)]
        z1 = sb("z1", [128, D])
        scr = sb("scr", [128, D])
        xn_bf = sb("xn_bf", [128, D], BF16)
        xn_fm = [sb(f"xnfm{j}", [128, 4 * SEG], BF16) for j in range(8)]
        mu = sb("mu", [128, 2]); sumsq = sb("sumsq", [128, 2])
        rstd = sb("rstd", [128, 2]); epsb = sb("epsb", [128, 1])
        p.step("vector", lambda e: e.memset(epsb[:, :], EPS))
        tr_ps = [es.enter_context(nc.psum_tensor(f"trp{i}", [128, 1024], BF16))
                 for i in range(2)]
        xth = sb("xth", [8, D])
        p.dma("sync", xth[:, :], xh, inc="ldxh")
        emit_ln_tile(p, xth[:, :], xn_bf[0:8, :], scr[0:8, :], mu[0:8, :],
                     sumsq[0:8, :], rstd[0:8, :], None, epsb[0:8, :],
                     waits=[("ldxh", 16)], inc="lnd", D_=D)
        for k in range(8):
            p.step("tensor",
                   lambda e, k=k: e.transpose(tr_ps[k % 2][:, 0:8],
                                              xn_bf[0:8, k * 128:(k + 1) * 128],
                                              sident[0:8, 0:8]),
                   waits=([("lnd", p.val("lnd")), ("ld", 16)] if k == 0 else
                          [("trev", p.val("trev") - 1)]),
                   inc="trmm")
            p.step("vector",
                   lambda e, k=k: e.tensor_copy(
                       xn_fm[k].rearrange("p (b s) -> p b s", b=4)[:, :, 0:2],
                       tr_ps[k % 2][:, 0:8].rearrange("p (b t) -> p b t", b=4)),
                   waits=[("trmm", p.val("trmm"))], inc="trev")
        for it in range(NTILES):
            buf = it % 2
            b_, lt = it // 8, it % 8
            p.dma("sync", xt[buf][:, :], x[it * 128:(it + 1) * 128, :],
                  waits=[("lnd", max(0, p.val("lnd") - 1))], inc=f"ldx{buf}")
            emit_ln_tile(p, xt[buf][:, :], xn_bf[:, :], scr, mu, sumsq, rstd, None,
                         epsb, waits=[(f"ldx{buf}", (it // 2 + 1) * 16)], inc="lnd")
            for k in range(8):
                col = b_ * SEG + 2 + lt * 128
                p.step("tensor",
                       lambda e, k=k: e.transpose(tr_ps[k % 2][:, 0:128],
                                                  xn_bf[:, k * 128:(k + 1) * 128],
                                                  sident[:, :]),
                       waits=([("lnd", p.val("lnd"))] if k == 0 else
                              [("trev", p.val("trev") - 1)]),
                       inc="trmm")
                p.step("vector",
                       lambda e, k=k, col=col: e.tensor_copy(
                           xn_fm[k][:, col:col + 128], tr_ps[k % 2][:, 0:128]),
                       waits=[("trmm", p.val("trmm"))], inc="trev")
        winb = [sb(f"winb{k}", [128, DI], BF16) for k in range(8)]
        wld = sb("wld", [128, DI])
        ccol = sb("ccol", [128, 3])
        wxb = sb("wxb", [128, 96], BF16)
        wxl = sb("wxl", [128, 96])
        u_raw = sb("u_raw", [128, 4 * SEG])
        u_cv = sb("u_cv", [128, ROWS])
        u_out = [sb(f"uout{i}", [128, ROWS], BF16) for i in range(2)]
        mm_ps = [psum(f"mmp{i}", [128, 456]) for i in range(4)]
        xd_ps = [psum(f"xdp{i}", [128, 96]) for i in range(2)]
        for half, dst in ((0, u_o), (1, res_o)):
            for k in range(8):
                p.dma("scalar", wld[:, :],
                      win[k * 128:(k + 1) * 128, half * DI:(half + 1) * DI],
                      waits=[("wcst", p.val("wcst"))], inc="wld")
                p.step("vector",
                       lambda e, k=k: e.tensor_copy(winb[k][:, :], wld[:, :]),
                       waits=[("wld", p.val("wld")), ("umm", p.val("umm"))],
                       inc="wcst")
            for dt_ in range(16):
                ub = dt_ % 2
                if half == 0:
                    p.dma("scalar", ccol[:, :], wcv[dt_ * 128:(dt_ + 1) * 128, :],
                          waits=[("cvd", p.val("cvd"))], inc="ldcc")
                for nch in range(9):
                    cw = min(456, 4 * SEG - nch * 456)
                    pb = mm_ps[p.val("umm") % 4]
                    for kk in range(8):
                        p.step("tensor",
                               lambda e, pb=pb, kk=kk, dt_=dt_, nch=nch, cw=cw, half=half:
                               e.matmul(pb[:, 0:cw],
                                        winb[kk][:, dt_ * 128:(dt_ + 1) * 128],
                                        xn_fm[kk][:, nch * 456:nch * 456 + cw],
                                        start=(kk == 0), stop=(kk == 7)),
                               waits=([("wcst", p.val("wcst")), ("trev", p.val("trev")),
                                       ("uev", max(0, p.val("uev") - 2))] if kk == 0 else []),
                               inc=("umm" if kk == 7 else None))
                    warw = ([("cv3", p.val("cv3"))] if half == 0 else
                            [("cvd", p.val("cvd"))]) if nch == 0 else []
                    p.step("vector",
                           lambda e, pb=pb, nch=nch, cw=cw: e.tensor_copy(
                               u_raw[:, nch * 456:nch * 456 + cw], pb[:, 0:cw]),
                           waits=[("umm", p.val("umm"))] + warw, inc="uev")
                if half == 1:
                    p.step("vector",
                           lambda e, ub=ub: e.tensor_copy(
                               u_out[ub].rearrange("p (b t) -> p b t", b=4),
                               u_raw.rearrange("p (b s) -> p b s", b=4)[:, :, 2:SEG]),
                           waits=[("uev", p.val("uev")),
                                  (f"stu{ub}", 16 * (((half * 16 + dt_) // 2)))],
                           inc="cvd")
                    p.dma("sync", dst[dt_ * 128:(dt_ + 1) * 128, :], u_out[ub][:, :],
                          waits=[("cvd", p.val("cvd"))], inc=f"stu{ub}")
                    continue
                for b_ in range(4):
                    s0 = b_ * SEG
                    o0 = b_ * TS
                    p.step("vector",
                           lambda e, s0=s0, o0=o0: e.tensor_scalar_mul(
                               u_cv[:, o0:o0 + TS], u_raw[:, s0 + 2:s0 + 2 + TS],
                               ccol[:, 2:3]),
                           waits=[("uev", p.val("uev")), ("ldcc", p.val("ldcc"))],
                           inc="cv1")
                    p.wait("vector", [("cv1", p.val("cv1"))])
                    p.step("vector",
                           lambda e, s0=s0, o0=o0: e.scalar_tensor_tensor(
                               out=u_cv[:, o0:o0 + TS], in0=u_raw[:, s0 + 1:s0 + 1 + TS],
                               scalar=ccol[:, 1:2], in1=u_cv[:, o0:o0 + TS],
                               op0=ALU.mult, op1=ALU.add), inc="cv2")
                    p.wait("vector", [("cv2", p.val("cv2"))])
                    p.step("vector",
                           lambda e, s0=s0, o0=o0: e.scalar_tensor_tensor(
                               out=u_cv[:, o0:o0 + TS], in0=u_raw[:, s0:s0 + TS],
                               scalar=ccol[:, 0:1], in1=u_cv[:, o0:o0 + TS],
                               op0=ALU.mult, op1=ALU.add), inc="cv3")
                p.step("scalar",
                       lambda e, ub=ub: e.activation(out=u_out[ub][:, :], in_=u_cv[:, :],
                                                     func=AF.Silu),
                       waits=[("cv3", p.val("cv3")),
                              (f"stu{ub}", 16 * ((dt_ // 2)))], inc="cvd")
                p.dma("sync", dst[dt_ * 128:(dt_ + 1) * 128, :], u_out[ub][:, :],
                      waits=[("cvd", p.val("cvd"))], inc=f"stu{ub}")
        wxb2 = [sb(f"wxb2_{k}", [128, 96], BF16) for k in range(16)]
        for k in range(16):
            p.dma("scalar", wxl[:, :], wx[k * 128:(k + 1) * 128, :],
                  waits=[("wxc", p.val("wxc"))], inc="wxld")
            p.step("vector", lambda e, k=k: e.tensor_copy(wxb2[k][:, :], wxl[:, :]),
                   waits=[("wxld", p.val("wxld"))], inc="wxc")
        uk = [sb(f"uk{i}", [128, 128], BF16) for i in range(4)]
        xde = [sb(f"xde{i}", [128, 96]) for i in range(2)]
        for it in range(NTILES):
            pb = xd_ps[it % 2]
            for kk in range(16):
                gi = it * 16 + kk
                ub = gi % 4
                p.dma("sync", uk[ub][:, :],
                      u_o[kk * 128:(kk + 1) * 128, it * 128:(it + 1) * 128],
                      waits=[("stu0", p.val("stu0")), ("stu1", p.val("stu1")),
                             ("xmm", max(0, p.val("xmm") - 3))], inc=f"lduk{ub}")
                p.step("tensor",
                       lambda e, pb=pb, kk=kk, ub=ub: e.matmul(
                           pb[:, :], uk[ub][:, :], wxb2[kk][:, :],
                           start=(kk == 0), stop=(kk == 15)),
                       waits=[(f"lduk{ub}", (gi // 4 + 1) * 16)] +
                             ([("wxc", p.val("wxc")),
                               ("xde_", max(0, p.val("xde_") - 1))] if kk == 0 else []),
                       inc="xmm")
            p.step("vector",
                   lambda e, pb=pb, it=it: e.tensor_copy(xde[it % 2][:, :], pb[:, :]),
                   waits=[("xmm", p.val("xmm")),
                          (f"stxd{it % 2}", 16 * (it // 2))], inc="xde_")
            p.dma("scalar", xdbl_o[it * 128:(it + 1) * 128, :], xde[it % 2][:, :],
                  waits=[("xde_", p.val("xde_"))], inc=f"stxd{it % 2}")
        p.wait("sync", [("stxd0", p.val("stxd0")), ("stxd1", p.val("stxd1")),
                        ("stu0", p.val("stu0")), ("stu1", p.val("stu1"))])
        p.emit()
    return nc


def build_mamba_scan():
    BT = B * T
    L2 = 4096
    nc = bass.Bass("TRN2", target_bir_lowering=False, debug=False)
    u_i = nc.dram_tensor("u_i", [256, BT], BF16, kind="ExternalInput").ap()
    res_i = nc.dram_tensor("res_i", [256, BT], BF16, kind="ExternalInput").ap()
    xdbl = nc.dram_tensor("xdbl", [BT, 96], F32, kind="ExternalInput").ap()
    wdt = nc.dram_tensor("wdt", [DTR, 256], F32, kind="ExternalInput").ap()
    bdt = nc.dram_tensor("bdt", [1, 256], F32, kind="ExternalInput").ap()
    alog = nc.dram_tensor("alog", [256, DST], F32, kind="ExternalInput").ap()
    dpar = nc.dram_tensor("dpar", [1, 256], F32, kind="ExternalInput").ap()
    ident = nc.dram_tensor("ident", [128, 128], BF16, kind="ExternalInput").ap()
    y_o = nc.dram_tensor("y_o", [256, BT], BF16, kind="ExternalOutput").ap()
    bc_scr = nc.dram_tensor("bc_scr", [32, BT], BF16).ap()

    es = contextlib.ExitStack()
    with es:
        sb = lambda n_, sh, dt=F32: es.enter_context(nc.sbuf_tensor(n_, sh, dt))
        psum = lambda n_, sh, dt=F32: es.enter_context(nc.psum_tensor(n_, sh, dt))
        p = Plan(nc, es)
        sident = sb("sident", [128, 128], BF16)
        p.dma("sync", sident[:, :], ident)
        A_sb = [sb(f"asb{i}", [128, DST]) for i in range(2)]
        ald = sb("ald", [128, DST])
        bdtc = sb("bdtc", [128, 2])
        dpc = sb("dpc", [128, 2])
        wdtl = sb("wdtl", [DTR, 256])
        wdtb = sb("wdtb", [DTR, 256], BF16)
        for i in range(2):
            p.dma("scalar", ald[:, :], alog[i * 128:(i + 1) * 128, :],
                  waits=[("acst", p.val("acst"))], inc="lda")
            p.step("scalar",
                   lambda e, i=i: e.activation(out=A_sb[i][:, :], in_=ald[:, :],
                                               func=AF.Exp),
                   waits=[("lda", p.val("lda"))], inc="aex")
            p.step("vector",
                   lambda e, i=i: e.tensor_scalar_mul(A_sb[i][:, :], A_sb[i][:, :],
                                                      -1.0),
                   waits=[("aex", p.val("aex"))], inc="acst")
            p.dma("scalar", bdtc[:, i:i + 1],
                  bdt[0:1, i * 128:(i + 1) * 128].rearrange("one b -> b one"),
                  inc="ldc")
            p.dma("scalar", dpc[:, i:i + 1],
                  dpar[0:1, i * 128:(i + 1) * 128].rearrange("one b -> b one"),
                  inc="ldc")
        p.dma("scalar", wdtl[:, :], wdt, inc="ldw")
        p.step("vector", lambda e: e.tensor_copy(wdtb[:, :], wdtl[:, :]),
               waits=[("ldw", p.val("ldw"))], inc="wdtc")
        dtbc = sb("dtbc", [96, BT], BF16)
        dt_fm = dtbc[0:64, :]
        bc_sb = dtbc[64:96, :]
        xdt = [sb(f"xdt{i}", [128, 96]) for i in range(2)]
        xdb = sb("xdb", [128, 96], BF16)
        tp = [es.enter_context(nc.psum_tensor(f"tp{i}", [128, 1024], BF16))
              for i in range(2)]
        NBLK = BT // 128
        for blk in range(NBLK):
            bb = blk % 2
            p.dma("sync", xdt[bb][:, :], xdbl[blk * 128:(blk + 1) * 128, :],
                  waits=[("xdc", max(0, p.val("xdc") - 1))], inc=f"ldxd{bb}")
            p.step("vector",
                   lambda e, bb=bb: e.tensor_copy(xdb[:, :], xdt[bb][:, :]),
                   waits=[(f"ldxd{bb}", (blk // 2 + 1) * 16),
                          ("trq", p.val("trq"))], inc="xdc")
            p.step("tensor",
                   lambda e: e.transpose(tp[0][0:64, 0:128],
                                         xdb[:, 0:64], sident[:, :]),
                   waits=[("xdc", p.val("xdc")), ("ld", 16),
                          ("trd", max(0, p.val("trd") - 1))], inc="trq")
            p.step("tensor",
                   lambda e: e.transpose(tp[1][64:96, 0:128], xdb[:, 64:96],
                                         sident[:, :], tile_position=(0, 64)),
                   inc="trq")
            p.step("vector",
                   lambda e, blk=blk: e.tensor_copy(
                       dt_fm[:, blk * 128:(blk + 1) * 128], tp[0][0:64, 0:128]),
                   waits=[("trq", p.val("trq"))], inc="trd2")
            p.step("vector",
                   lambda e, blk=blk: e.tensor_copy(
                       bc_sb[:, blk * 128:(blk + 1) * 128], tp[1][64:96, 0:128]),
                   inc="trd")
        p.dma("sync", bc_scr, bc_sb[:, :], waits=[("trd", p.val("trd"))], inc="stbc")
        delta = sb("delta", [128, L2])
        epsb1 = sb("epsb1", [128, 1])
        p.step("vector", lambda e: e.memset(epsb1[:, :], 1.0), inc="eps1")

        zb = [sb(f"zb{i}", [128, 512]) for i in range(2)]
        na = sb("na", [128, 512])
        ab = sb("ab", [128, 512])
        e1 = sb("e1", [128, 512])
        u_sl = [sb(f"usl{i}", [128, L2], BF16) for i in range(2)]
        du = sb("du", [128, L2], BF16)
        dA = sb("dA", [128, L2])
        bbc = sb("bbc", [128, L2], BF16)
        dbu = sb("dbu", [128, L2], BF16)
        h_sb = sb("h_sb", [128, L2], BF16)
        cbc = sb("cbc", [128, L2], BF16)
        ytmp = sb("ytmp", [128, L2], BF16)
        yacc = sb("yacc", [128, L2])
        ressl = sb("ressl", [128, L2], BF16)
        yout = [sb("yout0", [128, L2], BF16)] * 2
        stcol = sb("stcol", [128, DST])
        dl_ps = [psum(f"dlp{i}", [128, 512]) for i in range(2)]
        it2 = 0
        for b_ in range(B):
            for dt2 in range(2):
                for hf in range(2):
                    cols0 = b_ * T + hf * L2
                    ub2 = it2 % 2
                    p.dma("sync", u_sl[ub2][:, :],
                          u_i[dt2 * 128:(dt2 + 1) * 128, cols0:cols0 + L2],
                          waits=[("dud", max(0, p.val("dud") - 1))], inc=f"ldu{ub2}")
                    for lch in range(8):
                        pb = dl_ps[lch % 2]
                        p.step("tensor",
                               lambda e, pb=pb, lch=lch, dt2=dt2, cols0=cols0:
                               e.matmul(pb[:, :],
                                        wdtb[:, dt2 * 128:(dt2 + 1) * 128],
                                        dt_fm[:, cols0 + lch * 512:cols0 + (lch + 1) * 512],
                                        start=True, stop=True),
                               waits=[("wdtc", 1), ("trd2", p.val("trd2")),
                                      ("zbd", max(0, p.val("zbd") - 1))],
                               inc="dmm")
                        zbb = zb[lch % 2]
                        p.step("vector",
                               lambda e, pb=pb, zbb=zbb, dt2=dt2: e.tensor_scalar_add(
                                   zbb[:, :], pb[:, :], bdtc[:, dt2:dt2 + 1]),
                               waits=[("dmm", p.val("dmm")), ("ldc", 64),
                                      ("sfp", max(0, p.val("sfp") - 1))], inc="zbd")
                        p.wait("vector", [("zbd", p.val("zbd"))])
                        p.step("vector",
                               lambda e, zbb=zbb: e.tensor_scalar_mul(na[:, :],
                                                                      zbb[:, :], -1.0),
                               inc="nad")
                        p.wait("vector", [("nad", p.val("nad"))])
                        p.step("vector",
                               lambda e, zbb=zbb: e.tensor_max(ab[:, :], zbb[:, :],
                                                               na[:, :]), inc="abd")
                        p.step("vector",
                               lambda e, zbb=zbb, lch=lch: e.tensor_scalar_max(
                                   delta[:, lch * 512:(lch + 1) * 512], zbb[:, :], 0.0),
                               inc="rld")
                        p.step("scalar",
                               lambda e: e.activation(out=e1[:, :], in_=ab[:, :],
                                                      func=AF.Exp, scale=-1.0),
                               waits=[("abd", p.val("abd")), ("lng2", p.val("lng2"))],
                               inc="lng1")
                        p.wait("scalar", [("lng1", p.val("lng1"))])
                        p.step("scalar",
                               lambda e: e.activation(out=e1[:, :], in_=e1[:, :],
                                                      func=AF.Ln, bias=epsb1[:, 0:1]),
                               inc="sfp")
                        p.step("vector",
                               lambda e, lch=lch: e.tensor_add(
                                   delta[:, lch * 512:(lch + 1) * 512],
                                   delta[:, lch * 512:(lch + 1) * 512], e1[:, :]),
                               waits=[("sfp", p.val("sfp")), ("rld", p.val("rld"))],
                               inc="deld")
                    p.step("vector",
                           lambda e, ub2=ub2: e.tensor_mul(du[:, :], delta[:, :],
                                                           u_sl[ub2][:, :]),
                           waits=[("deld", p.val("deld")),
                                  (f"ldu{ub2}", (it2 // 2 + 1) * 16),
                                  ("dbud", p.val("dbud"))], inc="dud")
                    for n_ in range(DST):
                        p.step("scalar",
                               lambda e, n_=n_, dt2=dt2: e.activation(
                                   out=dA[:, :], in_=delta[:, :], func=AF.Exp,
                                   scale=A_sb[dt2][:, n_:n_ + 1]),
                               waits=[("deld", p.val("deld")), ("acst", 2),
                                      ("scand", p.val("scand"))], inc="dad")
                        p.dma("sync", bbc[:, :],
                              bc_scr[n_:n_ + 1, cols0:cols0 + L2].partition_broadcast(128),
                              waits=[("stbc", 16), ("dbud", p.val("dbud"))],
                              inc="ldbb")
                        p.step("vector",
                               lambda e: e.tensor_mul(dbu[:, :], du[:, :], bbc[:, :]),
                               waits=[("dud", p.val("dud")), ("ldbb", p.val("ldbb")),
                                      ("scand", p.val("scand"))], inc="dbud")
                        p.wait("vector", [("dbud", p.val("dbud"))])
                        init = 0.0 if hf == 0 else stcol[:, n_:n_ + 1]
                        p.step("vector",
                               lambda e, init=init: e.tensor_tensor_scan(
                                   out=h_sb[:, :], data0=dA[:, :], data1=dbu[:, :],
                                   initial=init, op0=ALU.mult, op1=ALU.add),
                               waits=[("dad", p.val("dad")),
                                      ("ytd", p.val("ytd"))], inc="scand")
                        p.wait("vector", [("scand", p.val("scand"))])
                        if hf == 0:
                            p.step("vector",
                                   lambda e, n_=n_: e.tensor_copy(
                                       stcol[:, n_:n_ + 1], h_sb[:, L2 - 1:L2]),
                                   inc="std_")
                        p.dma("scalar", cbc[:, :],
                              bc_scr[16 + n_:17 + n_, cols0:cols0 + L2].partition_broadcast(128),
                              waits=[("stbc", 16), ("ytd", p.val("ytd"))], inc="ldcb")
                        p.step("vector",
                               lambda e: e.tensor_mul(ytmp[:, :], h_sb[:, :], cbc[:, :]),
                               waits=[("ldcb", p.val("ldcb"))], inc="ytd")
                        p.wait("vector", [("ytd", p.val("ytd"))])
                        if n_ == 0:
                            p.step("vector",
                                   lambda e: e.tensor_copy(yacc[:, :], ytmp[:, :]),
                                   inc="yad")
                        else:
                            p.step("vector",
                                   lambda e: e.tensor_add(yacc[:, :], yacc[:, :],
                                                          ytmp[:, :]), inc="yad")
                    p.dma("sync", ressl[:, :],
                          res_i[dt2 * 128:(dt2 + 1) * 128, cols0:cols0 + L2],
                          waits=[("sild", p.val("sild"))], inc="ldrs")
                    p.step("scalar",
                           lambda e: e.activation(out=ressl[:, :], in_=ressl[:, :],
                                                  func=AF.Silu),
                           waits=[("ldrs", p.val("ldrs"))], inc="sild")
                    p.step("vector",
                           lambda e, ub2=ub2, dt2=dt2: e.tensor_scalar(
                               out=du[:, :], in0=u_sl[ub2][:, :],
                               scalar1=dpc[:, dt2:dt2 + 1], scalar2=None,
                               op0=ALU.mult),
                           waits=[("yad", p.val("yad"))], inc="udd")
                    p.wait("vector", [("udd", p.val("udd"))])
                    p.step("vector",
                           lambda e: e.tensor_add(yacc[:, :], yacc[:, :], du[:, :]),
                           inc="ya2")
                    p.wait("vector", [("ya2", p.val("ya2")), ("sild", p.val("sild"))])
                    yo = yout[0]
                    p.step("vector",
                           lambda e, yo=yo: e.tensor_mul(yo[:, :], yacc[:, :],
                                                         ressl[:, :]),
                           waits=[("sty0", 16 * it2)], inc="yod")
                    p.dma("sync", y_o[dt2 * 128:(dt2 + 1) * 128, cols0:cols0 + L2],
                          yo[:, :], waits=[("yod", p.val("yod"))], inc="sty0")
                    it2 += 1
        p.wait("sync", [("sty0", p.val("sty0"))])
        p.emit()
    return nc


def build_mamba_out():
    nc = bass.Bass("TRN2", target_bir_lowering=False, debug=False)
    x = nc.dram_tensor("x", [ROWS, D], F32, kind="ExternalInput").ap()
    y = nc.dram_tensor("y", [DI, ROWS], BF16, kind="ExternalInput").ap()
    wo = nc.dram_tensor("wo", [DI, D], F32, kind="ExternalInput").ap()
    xo = nc.dram_tensor("xo", [ROWS, D], F32, kind="ExternalOutput").ap()

    es = contextlib.ExitStack()
    with es:
        sb = lambda n_, sh, dt=F32: es.enter_context(nc.sbuf_tensor(n_, sh, dt))
        psum = lambda n_, sh, dt=F32: es.enter_context(nc.psum_tensor(n_, sh, dt))
        p = Plan(nc, es)
        wob = [sb(f"wob{k}", [128, D], BF16) for k in range(16)]
        wld = sb("wld", [128, D])
        for k in range(16):
            p.dma("scalar", wld[:, :], wo[k * 128:(k + 1) * 128, :],
                  waits=[("wcst", p.val("wcst"))], inc="wld")
            p.step("vector", lambda e, k=k: e.tensor_copy(wob[k][:, :], wld[:, :]),
                   waits=[("wld", p.val("wld"))], inc="wcst")
        yk = [sb(f"yk{i}", [128, 128], BF16) for i in range(16)]
        xt = [sb(f"xt{i}", [128, D]) for i in range(2)]
        onew = [sb(f"onew{i}", [128, D]) for i in range(2)]
        mm_ps = [psum(f"mmp{i}", [128, 512]) for i in range(4)]
        for it in range(NTILES):
            buf = it % 2
            p.dma("scalar", xt[buf][:, :], x[it * 128:(it + 1) * 128, :],
                  waits=[("oev", max(0, p.val("oev") - 2))], inc=f"ldx{buf}")
            for kk in range(16):
                ub = kk % 4
                p.dma("sync", yk[kk][:, :],
                      y[kk * 128:(kk + 1) * 128, it * 128:(it + 1) * 128],
                      waits=([("pmm", p.val("pmm"))] if kk == 0 else []),
                      inc=f"ldy{ub}")
            for h in range(2):
                pb = mm_ps[p.val("pmm") % 4]
                for kk in range(16):
                    ub = kk % 4
                    p.step("tensor",
                           lambda e, pb=pb, kk=kk, h=h: e.matmul(
                               pb[:, :], yk[kk][:, :],
                               wob[kk][:, h * 512:(h + 1) * 512],
                               start=(kk == 0), stop=(kk == 15)),
                           waits=([(f"ldy{u2}", 64 * (it + 1)) for u2 in range(4)] +
                                  [("wcst", p.val("wcst")),
                                   ("oev", max(0, p.val("oev") - 2))]
                                  if (kk == 0 and h == 0) else []),
                           inc=("pmm" if kk == 15 else None))
                p.step("vector",
                       lambda e, pb=pb, buf=buf, h=h: e.tensor_add(
                           onew[buf][:, h * 512:(h + 1) * 512], pb[:, :],
                           xt[buf][:, h * 512:(h + 1) * 512]),
                       waits=[("pmm", p.val("pmm")),
                              (f"ldx{buf}", (it // 2 + 1) * 16)] +
                             ([(f"stx{buf}", 16 * (it // 2))] if h == 0 else []),
                       inc="oev")
            p.dma("scalar", xo[it * 128:(it + 1) * 128, :], onew[buf][:, :],
                  waits=[("oev", p.val("oev"))], inc=f"stx{buf}")
        p.wait("sync", [("stx0", p.val("stx0")), ("stx1", p.val("stx1"))])
        p.emit()
    return nc


_PROG_CACHE = {}


def _prog(name):
    if name not in _PROG_CACHE:
        _PROG_CACHE[name] = globals()[f"build_{name}"]()
    return _PROG_CACHE[name]


def _run(name, in_maps):
    nc = _prog(name)
    return run_bass_kernel_spmd(nc, in_maps, core_ids=list(range(NCORES))).results


_IDENT = np.eye(128, dtype=BF)


def _gather_rows(shards, key):
    F_ = shards[0][key].shape[-1]
    out = np.empty((B, T, F_), shards[0][key].dtype)
    for c in range(NCORES):
        out[:, c * TS:(c + 1) * TS] = shards[c][key].reshape(B, TS, F_)
    return out


def _gather_fm(shards, key, nrow):
    out = np.empty((nrow, B, T), shards[0][key].dtype)
    for c in range(NCORES):
        out[:, :, c * TS:(c + 1) * TS] = shards[c][key].reshape(nrow, B, TS)
    return out


def _pad_w(w, m, rows_too=False):
    out = np.zeros((D, D), np.float32)
    if rows_too:
        out[:m, :m] = w[:m, :m]
    else:
        out[:, :m] = w[:, :m]
    return out


def l0ffn_call(x, inp):
    shards = np_split_rows(x)
    ms = [D >> (c // 2) for c in range(NCORES)]
    im = []
    for c in range(NCORES):
        m = ms[c]
        im.append(dict(
            x=shards[c], nw=inp["L0_ffn_norm_w"].reshape(1, -1),
            nb=inp["L0_ffn_norm_b"].reshape(1, -1),
            wq=_pad_w(inp["L0_ffn_q_w"], m), wk=_pad_w(inp["L0_ffn_k_w"], m),
            wv=_pad_w(inp["L0_ffn_v_w"], m),
            bq=inp["L0_ffn_q_b"].reshape(1, -1), bk=inp["L0_ffn_k_b"].reshape(1, -1),
            bv=inp["L0_ffn_v_b"].reshape(1, -1), ident=_IDENT))
    r1 = _run("l0ffn_proj", im)
    qf = _gather_fm(r1, "qf", D)
    kf = _gather_fm(r1, "kf", D)
    vt = _gather_rows(r1, "vt")
    im = []
    for c in range(NCORES):
        prs = [(c // 2, (c % 2) * 4 + j) for j in range(4)]
        qf4 = np.stack([qf[h * 128:(h + 1) * 128, b] for (b, h) in prs])
        kf4 = np.stack([kf[h * 128:(h + 1) * 128, b] for (b, h) in prs])
        vt4 = np.stack([vt[b, :, h * 128:(h + 1) * 128] for (b, h) in prs])
        im.append(dict(qf4=np.ascontiguousarray(qf4), kf4=np.ascontiguousarray(kf4),
                       vt4=np.ascontiguousarray(vt4), ident=_IDENT))
    r2 = _run("l0ffn_attn", im)
    af = np.empty((D, B, T), r2[0]["af"].dtype)
    for c in range(NCORES):
        for j in range(4):
            b, h = c // 2, (c % 2) * 4 + j
            af[h * 128:(h + 1) * 128, b] = r2[c]["af"][j]
    im = []
    for c in range(NCORES):
        m = ms[c]
        afc = np.ascontiguousarray(
            af[:, :, c * TS:(c + 1) * TS].reshape(D, ROWS))
        im.append(dict(x=shards[c], af=afc,
                       ow=_pad_w(inp["L0_ffn_o_w"], m, rows_too=True),
                       ob=np.concatenate([inp["L0_ffn_o_b"][:m],
                                          np.zeros(D - m, np.float32)]).reshape(1, -1),
                       ident=_IDENT))
    r3 = _run("l0ffn_oproj", im)
    return _gather_rows(r3, "xo")


def mamba_call(x, inp):
    shards = np_split_rows(x)
    im = []
    for c in range(NCORES):
        if c == 0:
            xhs = np.zeros((8, D), np.float32)
        else:
            xhs = np.ascontiguousarray(
                x[:, c * TS - 2:c * TS].reshape(8, D)).astype(np.float32)
        im.append(dict(x=shards[c], xh=xhs, win=inp["L0_attn_W_in"],
                       wcv=inp["L0_attn_W_conv"], wx=inp["L0_attn_W_x"],
                       ident=_IDENT))
    r1 = _run("mamba_pre", im)
    u_g = _gather_fm(r1, "u_o", DI).reshape(DI, B * T)
    res_g = _gather_fm(r1, "res_o", DI).reshape(DI, B * T)
    xdbl_g = _gather_rows(r1, "xdbl_o").reshape(B * T, 96)
    im = []
    for c in range(NCORES):
        sl = slice(c * 256, (c + 1) * 256)
        im.append(dict(u_i=np.ascontiguousarray(u_g[sl]),
                       res_i=np.ascontiguousarray(res_g[sl]),
                       xdbl=xdbl_g,
                       wdt=np.ascontiguousarray(inp["L0_attn_W_dt"][:, sl]),
                       bdt=np.ascontiguousarray(inp["L0_attn_b_dt"][sl]).reshape(1, -1),
                       alog=np.ascontiguousarray(inp["L0_attn_A_log"][sl]),
                       dpar=np.ascontiguousarray(inp["L0_attn_D_param"][sl]).reshape(1, -1),
                       ident=_IDENT))
    r2 = _run("mamba_scan", im)
    y_g = np.empty((DI, B, T), r2[0]["y_o"].dtype)
    for c in range(NCORES):
        y_g[c * 256:(c + 1) * 256] = r2[c]["y_o"].reshape(256, B, T)
    im = []
    for c in range(NCORES):
        yc = np.ascontiguousarray(y_g[:, :, c * TS:(c + 1) * TS].reshape(DI, ROWS))
        im.append(dict(x=shards[c], y=yc, wo=inp["L0_attn_W_out"]))
    r3 = _run("mamba_out", im)
    return _gather_rows(r3, "xo")


def l1ffn_call(x, inp):
    shards = np_split_rows(x)
    im = []
    for c in range(NCORES):
        im.append(dict(x=shards[c], g1r=inp["L1_ffn_G1_R"], g2r=inp["L1_ffn_G2_R"],
                       g1i=inp["L1_ffn_G1_I"], g2i=inp["L1_ffn_G2_I"],
                       brl=inp["L1_ffn_bias_real"].reshape(1, -1),
                       bim=inp["L1_ffn_bias_imag"].reshape(1, -1), ident=_IDENT))
    r = _run("l1ffn", im)
    return _gather_rows(r, "xo")


def l1attn_call(x, inp):
    shards = np_split_rows(x)
    im = [dict(x=shards[c], wq=inp["L1_attn_W_Q"], wk=inp["L1_attn_W_K"],
               wv=inp["L1_attn_W_V"], ident=_IDENT) for c in range(NCORES)]
    r1 = _run("l1attn_proj", im)
    qf = _gather_fm(r1, "qf", D)
    kf = _gather_fm(r1, "kf", D)
    vt = _gather_rows(r1, "vt")
    im = []
    for c in range(NCORES):
        b, hh = c // 2, c % 2
        im.append(dict(
            qf=np.ascontiguousarray(qf[:, b, hh * 4096:(hh + 1) * 4096]),
            kf=np.ascontiguousarray(kf[:, b]),
            vt=np.ascontiguousarray(vt[b]),
            x=np.ascontiguousarray(x[b, hh * 4096:(hh + 1) * 4096]).astype(np.float32)))
    r2 = _run("l1attn_attn", im)
    out = np.empty((B, T, D), np.float32)
    for c in range(NCORES):
        b, hh = c // 2, c % 2
        out[b, hh * 4096:(hh + 1) * 4096] = r2[c]["xo"]
    return out


def kernel(**inputs):
    return kernel_fused(inputs)


def kernel_unfused(**inputs):
    inp = {k: np.ascontiguousarray(np.asarray(v, dtype=np.float32))
           for k, v in inputs.items()}
    x = inp["L0_attn_x"]
    x = l0ffn_call(x, inp)
    x = mamba_call(x, inp)
    x = l0ffn_call(x, inp)
    x = l1ffn_call(x, inp)
    x = l1attn_call(x, inp)
    x = l1ffn_call(x, inp)
    return x


TF = 8192
HTS = 4096
POLL_MAX = 500000


def build_fused(stop_after=None):
    nc = bass.Bass("TRN2", target_bir_lowering=False, debug=False)
    EI = lambda n_, sh, dt=F32: nc.dram_tensor(n_, sh, dt, kind="ExternalInput").ap()
    x_in = EI("x_in", [HTS, D])
    par_in = EI("par_in", [1, 1], mybir.dt.int32)
    nonce_in = EI("nonce_in", [1, 1], mybir.dt.int32)
    ident = EI("ident", [128, 128], BF16)
    nw = EI("nw", [1, D]); nb = EI("nb", [1, D])
    wq2 = [EI(f"wq{s}", [D, D]) for s in range(2)]
    wk2 = [EI(f"wk{s}", [D, D]) for s in range(2)]
    wv2 = [EI(f"wv{s}", [D, D]) for s in range(2)]
    bq = EI("bq", [1, D]); bk = EI("bk", [1, D]); bv = EI("bv", [1, D])
    ow2 = [EI(f"ow{s}", [D, D]) for s in range(2)]
    ob2 = [EI(f"ob{s}", [1, D]) for s in range(2)]
    win_u = EI("win_u", [D, 1024]); win_r = EI("win_r", [D, 1024])
    wcv_h = EI("wcv_h", [1024, 3]); wx_h = EI("wx_h", [1024, 96])
    wdt_h = EI("wdt_h", [DTR, 1024]); bdt_h = EI("bdt_h", [1, 1024])
    alog_h = EI("alog_h", [1024, DST]); dpar_h = EI("dpar_h", [1, 1024])
    wout_m = EI("wout_m", [1024, D]); wout_t = EI("wout_t", [1024, D])
    g1r = EI("g1r", [1, 32, 4, 32]); g2r = EI("g2r", [4, 16, 1, 16])
    g1i = EI("g1i", [1, 32, 4, 32]); g2i = EI("g2i", [4, 16, 1, 16])
    brl = EI("brl", [1, 512]); bim = EI("bim", [1, 512])
    l1wq = EI("l1wq", [D, D]); l1wk = EI("l1wk", [D, D]); l1wv = EI("l1wv", [D, D])
    xo = nc.dram_tensor("xo", [HTS, D], F32, kind="ExternalOutput").ap()

    LD = lambda n_, sh, dt=F32: nc.dram_tensor(n_, sh, dt).ap()
    x_a = LD("x_a", [HTS, D]); x_b = LD("x_b", [HTS, D])
    qf_s = LD("qf_s", [D, HTS], BF16)
    kf_s = LD("kf_s", [D, HTS], BF16)
    vt_s = LD("vt_s", [HTS, D], BF16)
    af_s = LD("af_s", [D, HTS], BF16)
    ur_s = LD("ur_s", [1024, TF], BF16)
    u_s = LD("u_s", [1024, TF], BF16)
    res_s = LD("res_s", [1024, TF], BF16)
    y_s = LD("y_s", [1024, HTS], BF16)
    y2_s = LD("y2_s", [1024, HTS], BF16)
    xd_s = LD("xd_s", [TF, 96])
    bc_scr = LD("bc_scr", [32, TF], BF16)
    wtmp = [LD(f"fwtmp{i}", [1024, 256]) for i in range(2)]
    w4 = [nc.dram_tensor(f"fw4_{i}", [32, 16, 32, 16], F32).ap() for i in range(2)]
    SH = lambda n_, sh, dt: nc.dram_tensor(n_, sh, dt, addr_space="Shared").ap()
    sh_k = SH("sh_k", [2, D, HTS], BF16)
    sh_v = SH("sh_v", [2, HTS, D], BF16)
    sh_ln2 = SH("sh_ln2", [2, D, HTS], BF16)
    sh_xd = SH("sh_xd", [2, TF, 96], F32)
    sh_y = SH("sh_y", [2, 1024, HTS], BF16)
    flags = nc.dram_tensor("flags", [2, 32], mybir.dt.int32,
                           addr_space="Shared").ap()

    es = contextlib.ExitStack()
    with es:
        gsb = lambda n_, sh, dt=F32: es.enter_context(nc.sbuf_tensor(n_, sh, dt))
        p = Plan(nc, es)
        sident = gsb("sident", [128, 128], BF16)
        par_sb = gsb("par_sb", [1, 1], mybir.dt.int32)
        nonce_sb = gsb("nonce_sb", [1, 1], mybir.dt.int32)
        fl_sb = gsb("fl_sb", [1, 1], mybir.dt.int32)
        p.dma("sync", sident[:, :], ident)
        p.dma("sync", par_sb[:, :], par_in, inc="ld")
        p.dma("sync", nonce_sb[:, :], nonce_in, inc="ld")
        regs = {"sync": {}, "scalar": {}}

        def setup_regs(e, eng):
            e.wait_ge(p.sems["ld"], 48)
            r = regs[eng]
            r["par"] = e.alloc_register(f"parr_{eng}")
            e.reg_load(r["par"], par_sb[0:1, 0:1])
            if eng == "sync":
                r["nd"] = e.alloc_register("ndr")
                r["fv"] = e.alloc_register("fvr")
                r["cnt"] = e.alloc_register("cntr")
                r["non"] = e.alloc_register("nonr")
                e.reg_load(r["non"], nonce_sb[0:1, 0:1])
                e.reg_mov(r["cnt"], 0)
        for _eng in ("sync", "scalar"):
            p.steps[_eng].append(
                (lambda e, _eng=_eng: (setup_regs(e, _eng), None)[1], [], None))

        _bar = [0]

        def barrier(waits=()):
            k = _bar[0]
            _bar[0] += 1
            psem = p.sem("bpoll")
            bfs = p.sem("bflag")
            bard = p.sem("bard")
            p.counts["bflag"] += 16
            p.counts["bard"] += 16

            def fn(e, k=k):
                r = regs["sync"]
                with e.If_eq(r["par"], 0):
                    e.dma_start(out=flags[0:1, k:k + 1],
                                in_=nonce_sb[0:1, 0:1]).then_inc(bfs, 16)
                with e.Else():
                    e.dma_start(out=flags[1:2, k:k + 1],
                                in_=nonce_sb[0:1, 0:1]).then_inc(bfs, 16)
                e.wait_ge(bfs, 16 * (k + 1))
                e.reg_mov(r["nd"], POLL_MAX)
                with e.While(r["nd"]):
                    e.reg_add(r["cnt"], r["cnt"], 16)
                    with e.If_eq(r["par"], 0):
                        e.dma_start(out=fl_sb[:, :],
                                    in_=flags[1:2, k:k + 1]).then_inc(psem, 16)
                    with e.Else():
                        e.dma_start(out=fl_sb[:, :],
                                    in_=flags[0:1, k:k + 1]).then_inc(psem, 16)
                    e.wait_ge(psem, r["cnt"])
                    e.reg_load(r["fv"], fl_sb[0:1, 0:1])
                    with e.If_cmp(r["fv"], r["non"], comp_op="IS_EQ"):
                        e.reg_mov(r["nd"], 0)
                    with e.Else():
                        e.reg_sub(r["nd"], r["nd"], 1)
                e.dma_start(out=fl_sb[:, :],
                            in_=flags[0:1, k:k + 1]).then_inc(bard, 16)
                return None
            p.steps["sync"].append((fn, [(p.sem(s), v) for (s, v) in waits], None))
            return 16 * (k + 1)

        def pardma(eng, out0, in0, out1, in1, waits=(), inc=None):
            sem = p.sem(inc)
            p.counts[inc] += 16
            tgt = p.sems[inc]

            def fn(e, eng=eng):
                with e.If_eq(regs[eng]["par"], 0):
                    e.dma_start(out=out0, in_=in0).then_inc(tgt, 16)
                with e.Else():
                    e.dma_start(out=out1, in_=in1).then_inc(tgt, 16)
                return None
            pf = p.pending.pop(eng, None)
            w = (pf + list(waits)) if pf else list(waits)
            p.steps[eng].append((fn, [(p.sem(s), v) for (s, v) in w], None))

        emit_fused_body(nc, p, es, locals())
        p.emit()
    return nc


def emit_fused_body(nc, p, es, C):
    barrier = C["barrier"]; pardma = C["pardma"]
    sident = C["sident"]
    gsb = C["gsb"]
    x_in, x_a, x_b, xo = C["x_in"], C["x_a"], C["x_b"], C["xo"]
    qf_s, kf_s, vt_s, af_s = C["qf_s"], C["kf_s"], C["vt_s"], C["af_s"]
    sh_k, sh_v, sh_ln2, sh_xd, sh_y = (C["sh_k"], C["sh_v"], C["sh_ln2"],
                                       C["sh_xd"], C["sh_y"])

    epsb = gsb("epsb_g", [128, 1])
    ones_bf = gsb("ones_g", [1, 128], BF16)
    ones_512 = gsb("ones512_g", [1, 512], BF16)
    onecol = gsb("onecol_g", [128, 1], BF16)
    p.step("vector", lambda e: e.memset(epsb[:, :], EPS))
    p.step("vector", lambda e: e.memset(ones_bf[:, :], 1.0))
    p.step("vector", lambda e: e.memset(onecol[:, :], 1.0))
    p.step("vector", lambda e: e.memset(ones_512[:, :], 1.0), inc="ones")

    NT = HTS // 128

    def ph_l0proj(x_src, call, prev_bv=None):
        with contextlib.ExitStack() as pes:
            sb = lambda n_, sh, dt=F32: pes.enter_context(
                nc.sbuf_tensor(f"{call}{n_}", sh, dt))
            psum = lambda n_, sh, dt=F32: pes.enter_context(
                nc.psum_tensor(f"{call}{n_}", sh, dt))
            wcol = sb("wcol", [128, 8])
            nbcol = sb("nbcol", [128, 8], BF16)
            nbf = sb("nbf", [128, 8])
            for j in range(8):
                p.dma("sync", wcol[:, j:j + 1],
                      C["nw"][0:1, j * 128:(j + 1) * 128].rearrange("one b -> b one"),
                      inc="wtld")
                p.dma("sync", nbf[:, j:j + 1],
                      C["nb"][0:1, j * 128:(j + 1) * 128].rearrange("one b -> b one"),
                      inc="wtld")
            p.step("vector", lambda e: e.tensor_copy(nbcol[:, :], nbf[:, :]),
                   waits=[("wtld", p.val("wtld"))], inc="wtcast")
            xt = [sb(f"xt{i}", [128, D]) for i in range(2)]
            z1 = sb("z1", [128, D]); scr = sb("scr", [128, D])
            xn_bf = sb("xn_bf", [128, D], BF16)
            xn_fm = [sb(f"xnfm{j}", [128, HTS], BF16) for j in range(8)]
            mu = sb("mu", [128, 2]); sumsq = sb("sumsq", [128, 2])
            rstd = sb("rstd", [128, 2])
            mu2 = sb("mu2", [128, 2]); sumsq2 = sb("sumsq2", [128, 2])
            rstd2 = sb("rstd2", [128, 2])
            tr_ps = [psum(f"trp{i}", [128, 1024], BF16) for i in range(2)]
            lnd0 = p.val("lnd")
            for it in range(NT):
                buf = it % 2
                p.dma("sync", xt[buf][:, :], x_src[it * 128:(it + 1) * 128, :],
                      waits=[("lnd", max(lnd0, p.val("lnd") - 1))], inc=f"ldx{buf}")
                emit_ln_tile(p, xt[buf][:, :], z1[:, :], scr, mu, sumsq, rstd, None,
                             epsb, waits=[(f"ldx{buf}", p.val(f"ldx{buf}"))],
                             inc="lnd")
                emit_ln_tile(p, z1[:, :], xn_bf[:, :], scr, mu2, sumsq2, rstd2, None,
                             epsb, waits=[("lnd", p.val("lnd"))], inc="lnd")
                for k in range(8):
                    p.step("tensor",
                           lambda e, k=k: e.transpose(
                               tr_ps[k % 2][:, 0:128], xn_bf[:, k * 128:(k + 1) * 128],
                               sident[:, :]),
                           waits=([("lnd", p.val("lnd")), ("ld", 16)] if k == 0 else
                                  [("trev", p.val("trev") - 1)]),
                           inc="trmm")
                    p.step("vector",
                           lambda e, k=k, it=it: e.tensor_copy(
                               xn_fm[k][:, it * 128:(it + 1) * 128],
                               tr_ps[k % 2][:, 0:128]),
                           waits=[("trmm", p.val("trmm"))], inc="trev")
            be_ps = psum("be_ps", [128, 512])
            brow = sb("brow", [1, D])
            mm_ps = [psum(f"mmp{i}", [128, 512]) for i in range(3)]
            ev = sb("ev", [128, 512], BF16)
            evv = sb("evv", [128, 512], BF16)
            wt_ps = [psum(f"wtps{i}", [128, 1024], BF16) for i in range(2)]
            for s in range(2):
                weff = {}
                be_sb = {}
                prev = None
                wes = pes.enter_context(contextlib.ExitStack())
                wsb = lambda n_, sh, dt=F32: wes.enter_context(
                    nc.sbuf_tensor(f"{call}s{s}{n_}", sh, dt))
                wnm = {}
                for nm, wd in (("q", C["wq2"][s]), ("k", C["wk2"][s]),
                               ("v", C["wv2"][s])):
                    weff[nm] = [wsb(f"weff_{nm}{j}", [128, D], BF16) for j in range(8)]
                    wnm[nm] = f"{call}s{s}w{nm}"
                    ew = [("wtev", p.val("wtev"))] if prev else \
                         ([("pev", p.val("pev"))] if (s == 1 or call != "A_") else [])
                    emit_wT(p, nc, wes, wnm[nm], wd, weff[nm], sident, ps=wt_ps,
                            extra_waits=ew, sp="wt")
                    prev = nm
                for nm, bd in (("q", C["bq"]), ("k", C["bk"]), ("v", C["bv"])):
                    be_sb[nm] = wsb(f"be_{nm}", [1, D], BF16)
                    p.dma("scalar", brow[0:1, :], bd,
                          waits=[("bsg", p.val("bsg"))], inc="ldb")
                    for half in range(2):
                        for kk in range(8):
                            p.step("tensor",
                                   lambda e, kk=kk, half=half, nm=nm, weff=weff:
                                   e.matmul(
                                       be_ps[0:1, :], nbcol[:, kk:kk + 1],
                                       weff[nm][kk][:, half * 512:(half + 1) * 512],
                                       start=(kk == 0), stop=(kk == 7)),
                                   waits=([("wtcast", p.val("wtcast")),
                                           ("wtev", p.val("wtev")),
                                           ("bsg", p.val("bsg"))] if kk == 0 else []),
                                   inc=("mmd" if kk == 7 else None))
                        p.step("vector",
                               lambda e, half=half, nm=nm, be_sb=be_sb:
                               e.tensor_add(
                                   be_sb[nm][0:1, half * 512:(half + 1) * 512],
                                   be_ps[0:1, :],
                                   brow[0:1, half * 512:(half + 1) * 512]),
                               waits=[("mmd", p.val("mmd")), ("ldb", p.val("ldb"))],
                               inc="bsg")
                    p.step("vector",
                           lambda e: e.tensor_copy(brow[0:1, 0:1], brow[0:1, 0:1]),
                           waits=[("bsg", p.val("bsg"))], inc="bsg")
                    for j in range(8):
                        p.step("vector",
                               lambda e, j=j, nm=nm, weff=weff: e.tensor_scalar_mul(
                                   weff[nm][j][:, :], weff[nm][j][:, :],
                                   wcol[:, j:j + 1]),
                               waits=[("mmd", p.val("mmd")),
                                      ("wtld", p.val("wtld"))], inc="wtsc")
                prq = [("bsg", p.val("bsg")), ("trev", p.val("trev")), ("ones", 1),
                       ("wtsc", p.val("wtsc"))]
                for nm, out, tm in (("q", qf_s, False), ("k", kf_s, False),
                                    ("v", vt_s, True)):
                    if tm:
                        mlist = range(s * 16, (s + 1) * 16)
                        nlist = range(2)
                    else:
                        mlist = range(8)
                        nlist = range(s * 4, (s + 1) * 4)
                    for mtile in mlist:
                        for nchunk in nlist:
                            pb = mm_ps[p.val("pmm") % 3]
                            for kk in range(8):
                                if tm:
                                    fn = (lambda e, pb=pb, kk=kk, mtile=mtile,
                                          nchunk=nchunk, nm=nm, weff=weff: e.matmul(
                                              pb[:, :],
                                              xn_fm[kk][:, mtile * 128:(mtile + 1) * 128],
                                              weff[nm][kk][:, nchunk * 512:(nchunk + 1) * 512],
                                              start=(kk == 0), stop=False))
                                else:
                                    fn = (lambda e, pb=pb, kk=kk, mtile=mtile,
                                          nchunk=nchunk, nm=nm, weff=weff: e.matmul(
                                              pb[:, :],
                                              weff[nm][kk][:, mtile * 128:(mtile + 1) * 128],
                                              xn_fm[kk][:, nchunk * 512:(nchunk + 1) * 512],
                                              start=(kk == 0), stop=False))
                                p.step("tensor", fn,
                                       waits=(prq if kk == 0 else []) +
                                             ([("pev", max(0, p.val("pev") - 1))]
                                              if kk == 0 else []),
                                       inc=None)
                            if tm:
                                bfn = (lambda e, pb=pb, nchunk=nchunk, nm=nm,
                                       be_sb=be_sb: e.matmul(
                                           pb[:, :], ones_bf[0:1, :],
                                           be_sb[nm][0:1, nchunk * 512:(nchunk + 1) * 512],
                                           start=False, stop=True))
                            else:
                                bfn = (lambda e, pb=pb, mtile=mtile, nm=nm,
                                       be_sb=be_sb: e.matmul(
                                           pb[:, :],
                                           be_sb[nm][0:1, mtile * 128:(mtile + 1) * 128],
                                           ones_512[0:1, :], start=False, stop=True))
                            p.step("tensor", bfn, inc="pmm")
                            g = p.val("pev")
                            slot = g % 2
                            buf_ev = ev if slot else evv
                            p.step("vector",
                                   lambda e, pb=pb, buf_ev=buf_ev: e.tensor_copy(
                                       buf_ev[:, :], pb[:, :]),
                                   waits=[("pmm", p.val("pmm")),
                                          (f"pst{slot}", p.val(f"pst{slot}"))],
                                   inc="pev")
                            dst = (out[mtile * 128:(mtile + 1) * 128,
                                       nchunk * 512:(nchunk + 1) * 512])
                            p.dma("sync", dst, buf_ev[:, :],
                                  waits=[("pev", p.val("pev"))], inc=f"pst{slot}")
                wes.close()
        pardma("sync", sh_k[0], kf_s, sh_k[1], kf_s,
               waits=[("pst0", p.val("pst0")), ("pst1", p.val("pst1"))] +
                     ([("bard", prev_bv)] if prev_bv else []), inc="shst")
        pardma("sync", sh_v[0], vt_s, sh_v[1], vt_s,
               waits=[("shst", p.val("shst"))], inc="shst")
        return barrier(waits=[("shst", p.val("shst"))])

    def ph_l0attn(bv_, call):
        NQB, NKC = 4, 8
        with contextlib.ExitStack() as pes:
            sb = lambda n_, sh, dt=F32: pes.enter_context(
                nc.sbuf_tensor(f"{call}a{n_}", sh, dt))
            psum = lambda n_, sh, dt=F32: pes.enter_context(
                nc.psum_tensor(f"{call}a{n_}", sh, dt))
            qh = [sb(f"qh{i}", [128, HTS], BF16) for i in range(2)]
            kf_sb = [sb(f"kfs{i}", [128, 1024], BF16) for i in range(2)]
            vt_sb = [[sb(f"vts{i}_{kt}", [128, 132], BF16) for kt in range(8)]
                     for i in range(2)]
            for i in range(2):
                for kt in range(8):
                    p.step("vector",
                           lambda e, i=i, kt=kt: e.memset(vt_sb[i][kt][:, 128:132],
                                                          1.0), inc="vones")
            st_sb2 = [[sb(f"st{s_}_{kt}", [128, 1024], BF16) for kt in range(8)]
                      for s_ in range(3)]
            ob_sb = sb("ob_sb", [128, 132], BF16)
            af_sb = [sb(f"afs{i}", [128, 1024], BF16) for i in range(2)]
            rec_sb = sb("rec_sb", [128, 16])
            st_ps = [psum(f"stps{i}", [128, 512]) for i in range(3)]
            av_big = psum("av_big", [128, 1536])
            av_ps = [av_big[:, (q2 // 3) * 512 + (q2 % 3) * 160:
                            (q2 // 3) * 512 + (q2 % 3) * 160 + 132] for q2 in range(8)]
            tr_ps = [psum(f"trp2{i}", [128, 1024], BF16) for i in range(2)]
            qk0 = p.val("qkv"); avv0 = p.val("avv"); ach0 = p.val("avch")
            for h in range(8):
                hs = h % 2
                p.dma("sync", qh[hs][:, :], qf_s[h * 128:(h + 1) * 128, :],
                      waits=[("bard", bv_)] +
                            ([("qkv", p.val("qkv") - 16 * NKC * NQB)] if h >= 2 else []),
                      inc=f"ldq{hs}")
                for qb in range(NQB):
                    for kc in range(NKC):
                        g = (h * NQB + qb) * NKC + kc
                        ks = g % 2
                        sts = g % 3
                        p.dma("sync", kf_sb[ks][:, :],
                              sh_k[kc // 4][h * 128:(h + 1) * 128,
                                            (kc % 4) * 1024:(kc % 4 + 1) * 1024],
                              waits=[("bard", bv_)] +
                                    ([("qkv", p.val("qkv") - 16)] if g >= 2 else []),
                              inc=f"ldk{ks}")
                        for kt in range(8):
                            r0 = kc * 1024 + kt * 128
                            p.dma("scalar", vt_sb[ks][kt][:, 0:128],
                                  sh_v[kc // 4][(kc % 4) * 1024 + kt * 128:
                                                (kc % 4) * 1024 + (kt + 1) * 128,
                                                h * 128:(h + 1) * 128],
                                  waits=[("bard", bv_)] +
                                        ([("avv", p.val("avv") - 16)] if g >= 2 else
                                         [("vones", p.val("vones"))]),
                                  inc=f"ldv{ks}")
                        for kt in range(8):
                            for qn in range(2):
                                pb = st_ps[p.val("qkv") % 3]
                                p.step("tensor",
                                       lambda e, pb=pb, kt=kt, qn=qn, ks=ks, hs=hs,
                                              qb=qb: e.matmul(
                                           pb[:, :],
                                           kf_sb[ks][:, kt * 128:(kt + 1) * 128],
                                           qh[hs][:, qb * 1024 + qn * 512:
                                                  qb * 1024 + (qn + 1) * 512],
                                           start=True, stop=True),
                                       waits=[(f"ldk{ks}", p.val(f"ldk{ks}")),
                                              (f"ldq{hs}", p.val(f"ldq{hs}")),
                                              ("expd", max(qk0, p.val("qkv") - 2))],
                                       inc="qkv")
                                p.step("scalar",
                                       lambda e, pb=pb, kt=kt, qn=qn, sts=sts:
                                       e.activation(
                                           out=st_sb2[sts][kt][:, qn * 512:
                                                               (qn + 1) * 512],
                                           in_=pb[:, :], func=AF.Exp, scale=1.0 / 32.0),
                                       waits=[("qkv", p.val("qkv"))] +
                                             ([("avv", max(avv0,
                                                           p.val("avv") - 32))]
                                              if (kt == 0 and qn == 0) else []),
                                       inc="expd")
                        for q2 in range(8):
                            for kt in range(8):
                                w = []
                                if kt == 0:
                                    w = [(f"ldv{ks}", p.val(f"ldv{ks}"))]
                                    if q2 == 0:
                                        w.append(("expd", p.val("expd")))
                                        if kc == 0:
                                            w.append(("recq", p.val("recq")))
                                p.step("tensor",
                                       lambda e, q2=q2, kt=kt, ks=ks, kc=kc, sts=sts:
                                       e.matmul(
                                           av_ps[q2],
                                           st_sb2[sts][kt][:, q2 * 128:(q2 + 1) * 128],
                                           vt_sb[ks][kt][:, :],
                                           start=(kc == 0 and kt == 0 and q2 % 3 == 0),
                                           stop=(kc == NKC - 1 and kt == 7),
                                           skip_group_check=True),
                                       waits=w,
                                       inc=("avv" if kt == 7 and q2 == 7 else None),
                                       inc_n=16)
                    for q2 in range(8):
                        p.step("vector",
                               lambda e, q2=q2: e.reciprocal(rec_sb[:, q2:q2 + 1],
                                                             av_ps[q2][:, 128:129]),
                               waits=[("avv", p.val("avv")),
                                      ("trd", max(0, p.val("trd") - 1))], inc="recq")
                        p.wait("vector", [("recq", p.val("recq"))])
                        p.step("vector",
                               lambda e, q2=q2: e.tensor_scalar_mul(
                                   ob_sb[:, 0:128], av_ps[q2][:, 0:128],
                                   rec_sb[:, q2:q2 + 1]),
                               waits=[("trq", p.val("trq"))], inc="nrmv")
                        p.step("tensor",
                               lambda e, q2=q2: e.transpose(
                                   tr_ps[q2 % 2][:, 0:128], ob_sb[:, 0:128],
                                   sident[:, :]),
                               waits=[("nrmv", p.val("nrmv")), ("ld", 16),
                                      ("trd", max(0, p.val("trd") - 1))], inc="trq")
                        gq = h * NQB + qb
                        p.step("vector",
                               lambda e, q2=q2, qb=qb: e.tensor_copy(
                                   af_sb[qb % 2][:, q2 * 128:(q2 + 1) * 128],
                                   tr_ps[q2 % 2][:, 0:128]),
                               waits=[("trq", p.val("trq"))] +
                                     ([(f"staf{qb % 2}", p.val(f"staf{qb % 2}"))]
                                      if q2 == 0 else []),
                               inc="trd")
                    p.step("vector", lambda e: e.tensor_copy(rec_sb[:, 0:1],
                                                             rec_sb[:, 0:1]),
                           waits=[("trd", p.val("trd"))], inc="recq")
                    p.dma("sync", af_s[h * 128:(h + 1) * 128, qb * 1024:(qb + 1) * 1024],
                          af_sb[qb % 2][:, :],
                          waits=[("recq", p.val("recq"))], inc=f"staf{qb % 2}")
        return barrier(waits=[("staf0", p.val("staf0")), ("staf1", p.val("staf1"))])

    def ph_l0oproj(x_src, x_dst, call):
        with contextlib.ExitStack() as pes:
            sb = lambda n_, sh, dt=F32: pes.enter_context(
                nc.sbuf_tensor(f"{call}o{n_}", sh, dt))
            psum = lambda n_, sh, dt=F32: pes.enter_context(
                nc.psum_tensor(f"{call}o{n_}", sh, dt))
            wt_ps = [psum(f"wtp{i}", [128, 1024], BF16) for i in range(2)]
            owt = {}
            obb = {}
            obf = sb("obf", [1, D])
            prev_ev = None
            for s in range(2):
                owt[s] = [sb(f"owt{s}_{j}", [128, D], BF16) for j in range(8)]
                emit_wT(p, nc, pes, f"{call}wo{s}", C["ow2"][s], owt[s], sident,
                        ps=wt_ps, sp="wt",
                        extra_waits=([("wtev", p.val("wtev"))] if prev_ev else []))
                prev_ev = "wtev"
                obb[s] = sb(f"ob_bf{s}", [1, D], BF16)
                p.dma("scalar", obf[0:1, :], C["ob2"][s],
                      waits=[("wtcast", p.val("wtcast"))], inc="wtld")
                p.step("vector",
                       lambda e, s=s: e.tensor_copy(obb[s][0:1, :], obf[0:1, :]),
                       waits=[("wtld", p.val("wtld"))], inc="wtcast")
            af_sb = [sb(f"afsb{k}", [128, HTS], BF16) for k in range(8)]
            for k in range(8):
                p.dma("sync", af_sb[k][:, :], af_s[k * 128:(k + 1) * 128, :],
                      inc="gld")
            xt = [sb(f"xt{i}", [128, D]) for i in range(2)]
            z1 = sb("z1", [128, D]); scr = sb("scr", [128, D])
            base = [sb(f"base{i}", [128, D]) for i in range(2)]
            onew = [sb(f"onew{i}", [128, D]) for i in range(2)]
            mu = sb("mu", [128, 2]); sumsq = sb("sumsq", [128, 2])
            rstd = sb("rstd", [128, 2])
            mm_ps = [psum(f"mmp{i}", [128, 512]) for i in range(4)]
            mmrdy = [("wtev", p.val("wtev")),
                     ("wtcast", p.val("wtcast")), ("ones", 1), ("gld", p.val("gld"))]
            lnd0 = p.val("lnd")
            opmm0 = p.val("opmm")
            for it in range(NT):
                buf = it % 2
                s = it // 16
                p.dma("sync", xt[buf][:, :], x_src[it * 128:(it + 1) * 128, :],
                      waits=[("bsg", max(0, p.val("bsg") - 1))], inc=f"lqx{buf}")
                emit_ln_tile(p, xt[buf][:, :], z1[:, :], scr, mu, sumsq, rstd, None,
                             epsb, waits=[(f"lqx{buf}", p.val(f"lqx{buf}"))],
                             inc="lnd")
                p.step("vector",
                       lambda e, buf=buf: e.scalar_tensor_tensor(
                           out=base[buf][:, :], in0=z1[:, :], scalar=0.5,
                           in1=xt[buf][:, :], op0=ALU.mult, op1=ALU.add),
                       waits=[("lnd", p.val("lnd"))], inc="bsg")
                for hh in range(2):
                    pb = mm_ps[p.val("opmm") % 4]
                    for kk in range(8):
                        p.step("tensor",
                               lambda e, pb=pb, kk=kk, it=it, hh=hh, s=s: e.matmul(
                                   pb[:, :], af_sb[kk][:, it * 128:(it + 1) * 128],
                                   owt[s][kk][:, hh * 512:(hh + 1) * 512],
                                   start=(kk == 0), stop=False),
                               waits=(mmrdy if p.val("opmm") == opmm0 and kk == 0 else
                                      ([("opev", max(0, p.val("opev") - 2))]
                                       if kk == 0 else [])),
                               inc=None)
                    p.step("tensor",
                           lambda e, pb=pb, hh=hh, s=s: e.matmul(
                               pb[:, :], ones_bf[0:1, :],
                               obb[s][0:1, hh * 512:(hh + 1) * 512],
                               start=False, stop=True),
                           inc="opmm")
                    p.step("vector",
                           lambda e, pb=pb, buf=buf, hh=hh: e.scalar_tensor_tensor(
                               out=onew[buf][:, hh * 512:(hh + 1) * 512], in0=pb[:, :],
                               scalar=0.5, in1=base[buf][:, hh * 512:(hh + 1) * 512],
                               op0=ALU.mult, op1=ALU.add),
                           waits=[("opmm", p.val("opmm")), ("bsg", p.val("bsg"))] +
                                 ([(f"osx{buf}", p.val(f"osx{buf}"))]
                                  if hh == 0 else []),
                           inc="opev")
                p.dma("scalar", x_dst[it * 128:(it + 1) * 128, :], onew[buf][:, :],
                      waits=[("opev", p.val("opev"))], inc=f"osx{buf}")
            p.wait("sync", [("osx0", p.val("osx0")), ("osx1", p.val("osx1"))])

    def ph_mamba(x_src, x_dst):
        with contextlib.ExitStack() as pes:
            sb = lambda n_, sh, dt=F32: pes.enter_context(
                nc.sbuf_tensor(f"M{n_}", sh, dt))
            psum = lambda n_, sh, dt=F32: pes.enter_context(
                nc.psum_tensor(f"M{n_}", sh, dt))
            xt = [sb(f"xt{i}", [128, D]) for i in range(2)]
            z1b = sb("z1b", [128, D], BF16)
            scr = sb("scr", [128, D])
            mu = sb("mu", [128, 2]); sumsq = sb("sumsq", [128, 2])
            rstd = sb("rstd", [128, 2])
            ln_fm = sb("ln_fm", [128, 1024], BF16)
            tr_ps = [psum(f"trp{i}", [128, 1024], BF16) for i in range(2)]
            lnd0 = p.val("lnd")
            for it in range(NT):
                buf = it % 2
                p.dma("sync", xt[buf][:, :], x_src[it * 128:(it + 1) * 128, :],
                      waits=[("lnd", max(lnd0, p.val("lnd") - 1))], inc=f"ldx{buf}")
                emit_ln_tile(p, xt[buf][:, :], z1b[:, :], scr, mu, sumsq, rstd, None,
                             epsb, waits=[(f"ldx{buf}", p.val(f"ldx{buf}"))],
                             inc="lnd")
                for k in range(8):
                    p.step("tensor",
                           lambda e, k=k: e.transpose(
                               tr_ps[k % 2][:, 0:128], z1b[:, k * 128:(k + 1) * 128],
                               sident[:, :]),
                           waits=([("lnd", p.val("lnd")), ("ld", 16)] if k == 0 else
                                  [("trev", p.val("trev") - 1)]),
                           inc="trmm")
                    p.step("vector",
                           lambda e, k=k: e.tensor_copy(
                               ln_fm[:, k * 128:(k + 1) * 128], tr_ps[k % 2][:, 0:128]),
                           waits=[("trmm", p.val("trmm")),
                                  ("pst0", p.val("pst0"))], inc="trev")
                for k in range(8):
                    pardma("scalar", sh_ln2[0][k * 128:(k + 1) * 128,
                                               it * 128:(it + 1) * 128],
                           ln_fm[:, k * 128:(k + 1) * 128],
                           sh_ln2[1][k * 128:(k + 1) * 128, it * 128:(it + 1) * 128],
                           ln_fm[:, k * 128:(k + 1) * 128],
                           waits=([("trev", p.val("trev"))] if k == 0 else []),
                           inc="pst0")
            bv1 = barrier(waits=[("pst0", p.val("pst0"))])

        with contextlib.ExitStack() as pes:
            sb = lambda n_, sh, dt=F32: pes.enter_context(
                nc.sbuf_tensor(f"Mb{n_}", sh, dt))
            psum = lambda n_, sh, dt=F32: pes.enter_context(
                nc.psum_tensor(f"Mb{n_}", sh, dt))
            winb = [sb(f"winb{k}", [128, 1024], BF16) for k in range(8)]
            wrsb = [sb(f"wrsb{k}", [128, 1024], BF16) for k in range(8)]
            wld = sb("wld", [128, 1024])
            for k in range(8):
                for src_w, tiles in ((C["win_u"], winb), (C["win_r"], wrsb)):
                    p.dma("scalar", wld[:, :], src_w[k * 128:(k + 1) * 128, :],
                          waits=[("wtcast", p.val("wtcast"))], inc="wtld")
                    p.step("vector",
                           lambda e, t=tiles[k], wld=wld: e.tensor_copy(t[:, :],
                                                                        wld[:, :]),
                           waits=[("wtld", p.val("wtld"))], inc="wtcast")
            wrdy = ("wtcast", p.val("wtcast"))
            rhs = [sb(f"rhs{k}", [128, 512], BF16) for k in range(8)]
            uev = [sb(f"uev{i}", [128, 512], BF16) for i in range(2)]
            mm_ps = [psum(f"mmp{i}", [128, 512]) for i in range(4)]
            for nch in range(TF // 512):
                half = nch // 8
                c0 = (nch % 8) * 512
                for kk in range(8):
                    p.dma("sync", rhs[kk][:, :],
                          sh_ln2[half][kk * 128:(kk + 1) * 128, c0:c0 + 512],
                          waits=[("bard", bv1)] +
                                ([("pmm", p.val("pmm"))] if kk == 0 else []),
                          inc="gld")
                for dt_ in range(16):
                    isu = dt_ < 8
                    dti = dt_ % 8
                    pb = mm_ps[p.val("pmm") % 4]
                    wt = winb if isu else wrsb
                    for kk in range(8):
                        p.step("tensor",
                               lambda e, pb=pb, kk=kk, dti=dti, wt=wt: e.matmul(
                                   pb[:, :], wt[kk][:, dti * 128:(dti + 1) * 128],
                                   rhs[kk][:, :], start=(kk == 0), stop=(kk == 7)),
                               waits=([("gld", p.val("gld")), wrdy,
                                       ("pev", max(0, p.val("pev") - 2))]
                                      if kk == 0 else []),
                               inc=("pmm" if kk == 7 else None))
                    ub = p.val("pev") % 2
                    p.step("vector",
                           lambda e, pb=pb, ub=ub: e.tensor_copy(uev[ub][:, :],
                                                                 pb[:, :]),
                           waits=[("pmm", p.val("pmm")),
                                  (f"pst{ub}", p.val(f"pst{ub}"))], inc="pev")
                    dst = (C["ur_s"] if isu else C["res_s"])[
                        dti * 128:(dti + 1) * 128, nch * 512:(nch + 1) * 512]
                    p.dma("sync", dst, uev[ub][:, :], waits=[("pev", p.val("pev"))],
                          inc=f"pst{ub}")
            ccol = sb("ccol", [128, 3])
            urow = [sb(f"urow{i}", [128, TF], BF16) for i in range(2)]
            ucv = sb("ucv", [128, TF])
            uout = [sb(f"uout{i}", [128, TF], BF16) for i in range(2)]
            for dti in range(8):
                ub = dti % 2
                p.dma("sync", urow[ub][:, :], C["ur_s"][dti * 128:(dti + 1) * 128, :],
                      waits=[("pst0", p.val("pst0")), ("pst1", p.val("pst1")),
                             ("cv", max(0, p.val("cv") - 3))], inc=f"ldx{ub}")
                p.dma("scalar", ccol[:, :], C["wcv_h"][dti * 128:(dti + 1) * 128, :],
                      waits=[("cv", p.val("cv"))], inc="sld")
                p.step("vector",
                       lambda e, ub=ub: e.tensor_scalar_mul(
                           ucv[:, :], urow[ub][:, :], ccol[:, 2:3]),
                       waits=[(f"ldx{ub}", p.val(f"ldx{ub}")), ("sld", p.val("sld")),
                              ("sild", p.val("sild"))], inc="cv")
                p.wait("vector", [("cv", p.val("cv"))])
                p.step("vector",
                       lambda e, ub=ub: e.scalar_tensor_tensor(
                           out=ucv[:, 1:TF], in0=urow[ub][:, 0:TF - 1],
                           scalar=ccol[:, 1:2], in1=ucv[:, 1:TF],
                           op0=ALU.mult, op1=ALU.add), inc="cv")
                p.wait("vector", [("cv", p.val("cv"))])
                p.step("vector",
                       lambda e, ub=ub: e.scalar_tensor_tensor(
                           out=ucv[:, 2:TF], in0=urow[ub][:, 0:TF - 2],
                           scalar=ccol[:, 0:1], in1=ucv[:, 2:TF],
                           op0=ALU.mult, op1=ALU.add), inc="cv")
                p.step("scalar",
                       lambda e, ub=ub: e.activation(out=uout[ub][:, :], in_=ucv[:, :],
                                                     func=AF.Silu),
                       waits=[("cv", p.val("cv")),
                              (f"stu{ub}", p.val(f"stu{ub}"))], inc="sild")
                p.dma("sync", C["u_s"][dti * 128:(dti + 1) * 128, :], uout[ub][:, :],
                      waits=[("sild", p.val("sild"))], inc=f"stu{ub}")
            wxb = [sb(f"wxb{k}", [128, 96], BF16) for k in range(8)]
            wxl = sb("wxl", [128, 96])
            for k in range(8):
                p.dma("scalar", wxl[:, :], C["wx_h"][k * 128:(k + 1) * 128, :],
                      waits=[("wtcast", p.val("wtcast"))], inc="wtld")
                p.step("vector", lambda e, k=k: e.tensor_copy(wxb[k][:, :], wxl[:, :]),
                       waits=[("wtld", p.val("wtld"))], inc="wtcast")
            wxrdy = ("wtcast", p.val("wtcast"))
            uk = [sb(f"uk{i}", [128, 128], BF16) for i in range(4)]
            xde = [sb(f"xde{i}", [128, 96]) for i in range(2)]
            xd_ps = [psum(f"xdp{i}", [128, 512]) for i in range(2)]
            for it in range(TF // 128):
                pb = xd_ps[it % 2]
                for kk in range(8):
                    ub = (it * 8 + kk) % 4
                    p.dma("sync", uk[ub][:, :],
                          C["u_s"][kk * 128:(kk + 1) * 128, it * 128:(it + 1) * 128],
                          waits=[("stu0", p.val("stu0")), ("stu1", p.val("stu1")),
                                 ("xmm", max(0, p.val("xmm") - 3))], inc=f"lduk{ub}")
                    p.step("tensor",
                           lambda e, pb=pb, kk=kk, ub=ub: e.matmul(
                               pb[:, 0:96], uk[ub][:, :], wxb[kk][:, :],
                               start=(kk == 0), stop=(kk == 7)),
                           waits=[(f"lduk{ub}", p.val(f"lduk{ub}"))] +
                                 ([wxrdy, ("xde_", max(0, p.val("xde_") - 1))]
                                  if kk == 0 else []),
                           inc="xmm")
                p.step("vector",
                       lambda e, pb=pb, it=it: e.tensor_copy(xde[it % 2][:, 0:96],
                                                             pb[:, 0:96]),
                       waits=[("xmm", p.val("xmm")),
                              (f"stxd{it % 2}", p.val(f"stxd{it % 2}"))],
                       inc="xde_")
                pardma("scalar", sh_xd[0][it * 128:(it + 1) * 128, :],
                       xde[it % 2][:, 0:96],
                       sh_xd[1][it * 128:(it + 1) * 128, :], xde[it % 2][:, 0:96],
                       waits=[("xde_", p.val("xde_"))], inc=f"stxd{it % 2}")
            bv2 = barrier(waits=[("stxd0", p.val("stxd0")),
                                 ("stxd1", p.val("stxd1"))])

        with contextlib.ExitStack() as pes:
            sb = lambda n_, sh, dt=F32: pes.enter_context(
                nc.sbuf_tensor(f"Ms{n_}", sh, dt))
            psum = lambda n_, sh, dt=F32: pes.enter_context(
                nc.psum_tensor(f"Ms{n_}", sh, dt))
            A_sb = [sb(f"asb{i}", [128, DST]) for i in range(8)]
            ald = sb("ald", [128, DST])
            bdtc = sb("bdtc", [128, 8])
            dpc = sb("dpc", [128, 8])
            wdtl = sb("wdtl", [DTR, 1024])
            wdtb = sb("wdtb", [DTR, 1024], BF16)
            for i in range(8):
                p.dma("scalar", ald[:, :], C["alog_h"][i * 128:(i + 1) * 128, :],
                      waits=[("wtcast", p.val("wtcast"))], inc="wtld")
                p.step("scalar",
                       lambda e, i=i: e.activation(out=A_sb[i][:, :], in_=ald[:, :],
                                                   func=AF.Exp),
                       waits=[("wtld", p.val("wtld"))], inc="sild")
                p.step("vector",
                       lambda e, i=i: e.tensor_scalar_mul(A_sb[i][:, :], A_sb[i][:, :],
                                                          -1.0),
                       waits=[("sild", p.val("sild"))], inc="wtcast")
                p.dma("scalar", bdtc[:, i:i + 1],
                      C["bdt_h"][0:1, i * 128:(i + 1) * 128].rearrange("one b -> b one"),
                      inc="sld")
                p.dma("scalar", dpc[:, i:i + 1],
                      C["dpar_h"][0:1, i * 128:(i + 1) * 128].rearrange("one b -> b one"),
                      inc="sld")
            asb_rdy = ("wtcast", p.val("wtcast"))
            ldc_rdy = ("sld", p.val("sld"))
            p.dma("scalar", wdtl[:, :], C["wdt_h"], inc="wtld")
            p.step("vector", lambda e: e.tensor_copy(wdtb[:, :], wdtl[:, :]),
                   waits=[("wtld", p.val("wtld"))], inc="wtcast")
            wdt_rdy = ("wtcast", p.val("wtcast"))
            dtbc = sb("dtbc", [96, TF], BF16)
            dt_fm = dtbc[0:64, :]
            bc_sb = dtbc[64:96, :]
            xdt = [sb(f"xdt{i}", [128, 96]) for i in range(2)]
            xdt2 = [sb(f"xdt2_{i}", [128, 96]) for i in range(2)]
            xdb = sb("xdb", [128, 96], BF16)
            tp = [psum(f"tp{i}", [128, 1024], BF16) for i in range(2)]
            for blk in range(TF // 128):
                bb = blk % 2
                p.dma("sync", xdt[bb][:, :], sh_xd[0][blk * 128:(blk + 1) * 128, :],
                      waits=[("bard", bv2), ("lnm", p.val("lnm"))],
                      inc=f"ldx{bb}")
                p.dma("sync", xdt2[bb][:, :], sh_xd[1][blk * 128:(blk + 1) * 128, :],
                      inc=f"ldx{bb}")
                p.step("vector",
                       lambda e, bb=bb: e.tensor_add(xdt[bb][:, :], xdt[bb][:, :],
                                                     xdt2[bb][:, :]),
                       waits=[(f"ldx{bb}", p.val(f"ldx{bb}")),
                              ("trq", p.val("trq"))], inc="lnm")
                p.wait("vector", [("lnm", p.val("lnm"))])
                p.step("vector",
                       lambda e, bb=bb: e.tensor_copy(xdb[:, :], xdt[bb][:, :]),
                       inc="xde_")
                p.step("tensor",
                       lambda e: e.transpose(tp[0][0:64, 0:128], xdb[:, 0:64],
                                             sident[:, :]),
                       waits=[("xde_", p.val("xde_")), ("ld", 16),
                              ("trd", max(0, p.val("trd") - 1))], inc="trq")
                p.step("tensor",
                       lambda e: e.transpose(tp[1][64:96, 0:128], xdb[:, 64:96],
                                             sident[:, :], tile_position=(0, 64)),
                       inc="trq")
                p.step("vector",
                       lambda e, blk=blk: e.tensor_copy(
                           dt_fm[:, blk * 128:(blk + 1) * 128], tp[0][0:64, 0:128]),
                       waits=[("trq", p.val("trq"))], inc="trd2")
                p.step("vector",
                       lambda e, blk=blk: e.tensor_copy(
                           bc_sb[:, blk * 128:(blk + 1) * 128], tp[1][64:96, 0:128]),
                       inc="trd")
            p.dma("sync", C["bc_scr"], bc_sb[:, :], waits=[("trd", p.val("trd"))],
                  inc="stbc")
            stbc_rdy = ("stbc", p.val("stbc"))
            L2 = 4096
            delta = sb("delta", [128, L2])
            epsb1 = sb("epsb1", [128, 1])
            p.step("vector", lambda e: e.memset(epsb1[:, :], 1.0), inc="ones")
            zb = [sb(f"zb{i}", [128, 512]) for i in range(2)]
            na = sb("na", [128, 512]); ab = sb("ab", [128, 512])
            e1 = sb("e1", [128, 512])
            u_sl = [sb(f"usl{i}", [128, L2], BF16) for i in range(2)]
            du = sb("du", [128, L2], BF16)
            dA = sb("dA", [128, L2])
            bbc = sb("bbc", [128, L2], BF16)
            dbu = sb("dbu", [128, L2], BF16)
            h_sb = sb("h_sb", [128, L2], BF16)
            cbc = sb("cbc", [128, L2], BF16)
            ytmp = sb("ytmp", [128, L2], BF16)
            yacc = sb("yacc", [128, L2])
            ressl = sb("ressl", [128, L2], BF16)
            yout = sb("yout", [128, L2], BF16)
            stcol = sb("stcol", [128, DST])
            dl_ps = [psum(f"dlp{i}", [128, 512]) for i in range(2)]
            it2 = 0
            for dt2 in range(8):
                for hf in range(2):
                    cols0 = hf * L2
                    ub2 = it2 % 2
                    p.dma("sync", u_sl[ub2][:, :],
                          C["u_s"][dt2 * 128:(dt2 + 1) * 128, cols0:cols0 + L2],
                          waits=[("dud", max(0, p.val("dud") - 1))], inc=f"ldq{ub2}")
                    for lch in range(8):
                        pb = dl_ps[lch % 2]
                        p.step("tensor",
                               lambda e, pb=pb, lch=lch, dt2=dt2, cols0=cols0:
                               e.matmul(pb[:, :],
                                        wdtb[:, dt2 * 128:(dt2 + 1) * 128],
                                        dt_fm[:, cols0 + lch * 512:
                                              cols0 + (lch + 1) * 512],
                                        start=True, stop=True),
                               waits=[wdt_rdy, ("trd2", p.val("trd2")),
                                      ("zbd", max(0, p.val("zbd") - 1))],
                               inc="dmm")
                        zbb = zb[lch % 2]
                        p.step("vector",
                               lambda e, pb=pb, zbb=zbb, dt2=dt2: e.tensor_scalar_add(
                                   zbb[:, :], pb[:, :], bdtc[:, dt2:dt2 + 1]),
                               waits=[("dmm", p.val("dmm")), ldc_rdy,
                                      ("sfp", max(0, p.val("sfp") - 1))], inc="zbd")
                        p.wait("vector", [("zbd", p.val("zbd"))])
                        p.step("vector",
                               lambda e, zbb=zbb: e.tensor_scalar_mul(na[:, :],
                                                                      zbb[:, :], -1.0),
                               inc="nad")
                        p.wait("vector", [("nad", p.val("nad"))])
                        p.step("vector",
                               lambda e, zbb=zbb: e.tensor_max(ab[:, :], zbb[:, :],
                                                               na[:, :]), inc="abd")
                        p.step("vector",
                               lambda e, zbb=zbb, lch=lch: e.tensor_scalar_max(
                                   delta[:, lch * 512:(lch + 1) * 512], zbb[:, :], 0.0),
                               inc="rld")
                        p.step("scalar",
                               lambda e: e.activation(out=e1[:, :], in_=ab[:, :],
                                                      func=AF.Exp, scale=-1.0),
                               waits=[("abd", p.val("abd"))], inc="lng")
                        p.wait("scalar", [("lng", p.val("lng"))])
                        p.step("scalar",
                               lambda e: e.activation(out=e1[:, :], in_=e1[:, :],
                                                      func=AF.Ln, bias=epsb1[:, 0:1]),
                               waits=[("ones", p.val("ones"))], inc="sfp")
                        p.step("vector",
                               lambda e, lch=lch: e.tensor_add(
                                   delta[:, lch * 512:(lch + 1) * 512],
                                   delta[:, lch * 512:(lch + 1) * 512], e1[:, :]),
                               waits=[("sfp", p.val("sfp")), ("rld", p.val("rld"))],
                               inc="deld")
                    p.step("vector",
                           lambda e, ub2=ub2: e.tensor_mul(du[:, :], delta[:, :],
                                                           u_sl[ub2][:, :]),
                           waits=[("deld", p.val("deld")),
                                  (f"ldq{ub2}", p.val(f"ldq{ub2}")),
                                  ("dbud", p.val("dbud"))], inc="dud")
                    for n_ in range(DST):
                        p.step("scalar",
                               lambda e, n_=n_, dt2=dt2: e.activation(
                                   out=dA[:, :], in_=delta[:, :], func=AF.Exp,
                                   scale=A_sb[dt2][:, n_:n_ + 1]),
                               waits=[("deld", p.val("deld")), asb_rdy,
                                      ("scand", p.val("scand"))], inc="dad")
                        p.dma("sync", bbc[:, :],
                              C["bc_scr"][n_:n_ + 1, cols0:cols0 + L2]
                              .partition_broadcast(128),
                              waits=[stbc_rdy, ("dbud", p.val("dbud"))],
                              inc="ldbb")
                        p.step("vector",
                               lambda e: e.tensor_mul(dbu[:, :], du[:, :], bbc[:, :]),
                               waits=[("dud", p.val("dud")), ("ldbb", p.val("ldbb")),
                                      ("scand", p.val("scand"))], inc="dbud")
                        p.wait("vector", [("dbud", p.val("dbud"))])
                        init = 0.0 if hf == 0 else stcol[:, n_:n_ + 1]
                        p.step("vector",
                               lambda e, init=init: e.tensor_tensor_scan(
                                   out=h_sb[:, :], data0=dA[:, :], data1=dbu[:, :],
                                   initial=init, op0=ALU.mult, op1=ALU.add),
                               waits=[("dad", p.val("dad")),
                                      ("ytd", p.val("ytd"))], inc="scand")
                        p.wait("vector", [("scand", p.val("scand"))])
                        if hf == 0:
                            p.step("vector",
                                   lambda e, n_=n_: e.tensor_copy(
                                       stcol[:, n_:n_ + 1], h_sb[:, L2 - 1:L2]),
                                   inc=None)
                        p.dma("scalar", cbc[:, :],
                              C["bc_scr"][16 + n_:17 + n_, cols0:cols0 + L2]
                              .partition_broadcast(128),
                              waits=[stbc_rdy, ("ytd", p.val("ytd"))],
                              inc="ldcb")
                        p.step("vector",
                               lambda e: e.tensor_mul(ytmp[:, :], h_sb[:, :],
                                                      cbc[:, :]),
                               waits=[("ldcb", p.val("ldcb"))], inc="ytd")
                        p.wait("vector", [("ytd", p.val("ytd"))])
                        if n_ == 0:
                            p.step("vector",
                                   lambda e: e.tensor_copy(yacc[:, :], ytmp[:, :]),
                                   inc="yad")
                        else:
                            p.step("vector",
                                   lambda e: e.tensor_add(yacc[:, :], yacc[:, :],
                                                          ytmp[:, :]), inc="yad")
                    p.dma("sync", ressl[:, :],
                          C["res_s"][dt2 * 128:(dt2 + 1) * 128, cols0:cols0 + L2],
                          waits=[("sild", p.val("sild"))], inc="ldrs")
                    p.step("scalar",
                           lambda e: e.activation(out=ressl[:, :], in_=ressl[:, :],
                                                  func=AF.Silu),
                           waits=[("ldrs", p.val("ldrs"))], inc="sild")
                    p.step("vector",
                           lambda e, ub2=ub2, dt2=dt2: e.tensor_scalar(
                               out=du[:, :], in0=u_sl[ub2][:, :],
                               scalar1=dpc[:, dt2:dt2 + 1], scalar2=None,
                               op0=ALU.mult),
                           waits=[("yad", p.val("yad"))], inc="yad")
                    p.wait("vector", [("yad", p.val("yad"))])
                    p.step("vector",
                           lambda e: e.tensor_add(yacc[:, :], yacc[:, :], du[:, :]),
                           inc="yad")
                    p.wait("vector", [("yad", p.val("yad")),
                                      ("sild", p.val("sild"))])
                    p.step("vector",
                           lambda e: e.tensor_mul(yout[:, :], yacc[:, :], ressl[:, :]),
                           waits=[("sty0", p.val("sty0"))], inc="yod")
                    mine0 = (hf == 0)
                    pardma("sync",
                           (C["y_s"] if mine0 else sh_y[0])[
                               dt2 * 128:(dt2 + 1) * 128, :],
                           yout[:, :],
                           (sh_y[1] if mine0 else C["y_s"])[
                               dt2 * 128:(dt2 + 1) * 128, :],
                           yout[:, :],
                           waits=[("yod", p.val("yod"))], inc="sty0")
                    it2 += 1
            bv3 = barrier(waits=[("sty0", p.val("sty0"))])

        with contextlib.ExitStack() as pes:
            sb = lambda n_, sh, dt=F32: pes.enter_context(
                nc.sbuf_tensor(f"Mo{n_}", sh, dt))
            psum = lambda n_, sh, dt=F32: pes.enter_context(
                nc.psum_tensor(f"Mo{n_}", sh, dt))
            pardma("sync", C["y2_s"], sh_y[1], C["y2_s"], sh_y[0],
                   waits=[("bard", bv3)], inc="gld")
            y2rdy = ("gld", p.val("gld"))
            wob = [sb(f"wob{k}", [128, D], BF16) for k in range(16)]
            wld = sb("wld", [128, D])
            for k in range(16):
                src_w = C["wout_m"] if k < 8 else C["wout_t"]
                p.dma("scalar", wld[:, :], src_w[(k % 8) * 128:(k % 8 + 1) * 128, :],
                      waits=[("wtcast", p.val("wtcast"))], inc="wtld")
                p.step("vector",
                       lambda e, k=k, wld=wld: e.tensor_copy(wob[k][:, :],
                                                             wld[:, :]),
                       waits=[("wtld", p.val("wtld"))], inc="wtcast")
            wordy = ("wtcast", p.val("wtcast"))
            yk = [sb(f"yk{i}", [128, 128], BF16) for i in range(16)]
            xt = [sb(f"xt{i}", [128, D]) for i in range(2)]
            onew = [sb(f"onew{i}", [128, D]) for i in range(2)]
            mm_ps = [psum(f"mmp{i}", [128, 512]) for i in range(4)]
            for it in range(NT):
                buf = it % 2
                p.dma("scalar", xt[buf][:, :], x_src[it * 128:(it + 1) * 128, :],
                      waits=[("opev", max(0, p.val("opev") - 2))], inc=f"lqx{buf}")
                for kk in range(16):
                    ub = kk % 4
                    src_y = C["y_s"] if kk < 8 else C["y2_s"]
                    p.dma("sync", yk[kk][:, :],
                          src_y[(kk % 8) * 128:(kk % 8 + 1) * 128,
                                it * 128:(it + 1) * 128],
                          waits=([("opmm", p.val("opmm")), y2rdy]
                                 if kk == 0 else []),
                          inc=f"lduk{ub}")
                for hh in range(2):
                    pb = mm_ps[p.val("opmm") % 4]
                    for kk in range(16):
                        p.step("tensor",
                               lambda e, pb=pb, kk=kk, hh=hh: e.matmul(
                                   pb[:, :], yk[kk][:, :],
                                   wob[kk][:, hh * 512:(hh + 1) * 512],
                                   start=(kk == 0), stop=(kk == 15)),
                               waits=([(f"lduk{u2}", p.val(f"lduk{u2}"))
                                       for u2 in range(4)] +
                                      [wordy, ("opev", max(0, p.val("opev") - 2))]
                                      if (kk == 0 and hh == 0) else []),
                               inc=("opmm" if kk == 15 else None))
                    p.step("vector",
                           lambda e, pb=pb, buf=buf, hh=hh: e.tensor_add(
                               onew[buf][:, hh * 512:(hh + 1) * 512], pb[:, :],
                               xt[buf][:, hh * 512:(hh + 1) * 512]),
                           waits=[("opmm", p.val("opmm")),
                                  (f"lqx{buf}", p.val(f"lqx{buf}"))] +
                                 ([(f"osx{buf}", p.val(f"osx{buf}"))]
                                  if hh == 0 else []),
                           inc="opev")
                p.dma("scalar", x_dst[it * 128:(it + 1) * 128, :], onew[buf][:, :],
                      waits=[("opev", p.val("opev"))], inc=f"osx{buf}")
            p.wait("sync", [("osx0", p.val("osx0")), ("osx1", p.val("osx1"))])

    def ph_l1ffn(x_src, x_dst, call):
        wtmp, w4 = C["wtmp"], C["w4"]
        with contextlib.ExitStack() as pes:
            sb = lambda n_, sh, dt=F32: pes.enter_context(
                nc.sbuf_tensor(f"{call}f{n_}", sh, dt))
            psum = lambda n_, sh, dt=F32: pes.enter_context(
                nc.psum_tensor(f"{call}f{n_}", sh, dt))
            sg1 = sb("sg1", [4, 1024])
            sg2 = sb("sg2", [4, 256])
            sg1b = sb("sg1b", [4, 1024], BF16)
            sg2b = sb("sg2b", [4, 256], BF16)
            wt_ps_full = psum("wt_ps", [128, 512])
            wt_ps = wt_ps_full[:, 0:256]
            wt_sb = sb("wt_sb", [128, 256])
            wc = [sb(f"wc{k}", [128, 1024], BF16) for k in range(8)]
            bias_bc = sb("bias_bc", [128, 1024])
            for i, (g1, g2) in enumerate(((C["g1r"], C["g2r"]),
                                          (C["g1i"], C["g2i"]))):
                p.dma("sync", sg1.rearrange("c (b d) -> c b d", b=32),
                      g1.rearrange("one b c d -> c b d"),
                      waits=[("wtcast", p.val("wtcast"))], inc="gld")
                p.dma("sync", sg2[:, :], g2.rearrange("c e one g -> c (e g)"),
                      inc="gld")
                p.step("vector", lambda e: e.tensor_copy(sg1b[:, :], sg1[:, :]),
                       waits=[("gld", p.val("gld")), ("wttr", p.val("wttr"))])
                p.step("vector", lambda e: e.tensor_copy(sg2b[:, :], sg2[:, :]),
                       inc="wtcast")
                for m in range(8):
                    p.step("tensor",
                           lambda e, m=m: e.matmul(wt_ps,
                                                   sg1b[:, m * 128:(m + 1) * 128],
                                                   sg2b[:, :], start=True, stop=True),
                           waits=[("wtcast", p.val("wtcast")),
                                  ("wtev", p.val("wtev"))],
                           inc="wttr")
                    p.step("vector",
                           lambda e, m=m: e.tensor_copy(
                               wt_sb.rearrange("p (g e) -> p g e", g=16),
                               wt_ps.rearrange("p (e g) -> p g e", g=16)),
                           waits=[("wttr", p.val("wttr")), ("pst0", p.val("pst0"))],
                           inc="wtev")
                    p.dma("scalar", wtmp[i][m * 128:(m + 1) * 128, :], wt_sb[:, :],
                          waits=[("wtev", p.val("wtev"))], inc="pst0")
                w4src = wtmp[i].rearrange("(b d) (g e) -> d g b e", b=32, g=16)
                for q in range(32):
                    eng = ("scalar", "sync")[q % 2]
                    p.dma(eng, w4[i][q:q + 1], w4src[q:q + 1],
                          waits=[("pst0", p.val("pst0"))], inc="gld")
            wcf = sb("wcf", [128, 512])
            for k in range(8):
                half = k // 4
                dk = (k % 4) * 8
                srcs = (0, 1) if half == 0 else (1, 0)
                for j, si in enumerate(srcs):
                    src = w4[si].rearrange("d g b e -> (d g) (b e)")[
                        dk * 16:(dk + 8) * 16, :]
                    p.dma("sync", wcf[:, :], src,
                          waits=[("gld", p.val("gld")),
                                 ("wtcast", p.val("wtcast"))], inc="gld")
                    p.step("vector",
                           lambda e, k=k, j=j: e.tensor_copy(
                               wc[k][:, j * 512:(j + 1) * 512], wcf[:, :]),
                           waits=[("gld", p.val("gld"))], inc="wtcast")
            for k in range(4, 8):
                p.step("vector",
                       lambda e, k=k: e.tensor_scalar_mul(wc[k][:, 0:512],
                                                          wc[k][:, 0:512], -1.0),
                       waits=[("wtcast", p.val("wtcast"))], inc="wtsc")
            p.dma("scalar", bias_bc[:, 0:512], C["brl"].partition_broadcast(128),
                  inc="ldb")
            p.dma("scalar", bias_bc[:, 512:1024], C["bim"].partition_broadcast(128),
                  inc="ldb")
            p.step("vector", lambda e: e.tensor_scalar_mul(bias_bc[:, :],
                                                           bias_bc[:, :], 0.5),
                   waits=[("ldb", p.val("ldb"))], inc="wtcast")
            xt = [sb(f"xt{i}", [128, D]) for i in range(2)]
            scr = sb("scr", [128, D])
            xn_bf = sb("xn_bf", [128, D], BF16)
            xn_fm = sb("xn_fm", [128, D], BF16)
            mu = sb("mu", [128, 2]); sumsq = sb("sumsq", [128, 2])
            rstd = sb("rstd", [128, 2])
            tr_ps_full = [psum(f"tr_ps{i}", [128, 1024], BF16) for i in range(2)]
            tr_ps = [t[:, 0:128] for t in tr_ps_full]
            mm_ps = [psum(f"mm_ps{i}", [128, 512]) for i in range(2)]
            out_t = [sb(f"out_t{i}", [128, D]) for i in range(2)]
            wc_ready = [("wtsc", p.val("wtsc")), ("wtcast", p.val("wtcast")),
                        ("wtcast", p.val("wtcast")), ("ld", 16)]
            lnd0 = p.val("lnd")
            oev0 = p.val("opev")
            for it in range(NT):
                buf = it % 2
                p.dma("sync", xt[buf][:, :], x_src[it * 128:(it + 1) * 128, :],
                      waits=[("opev", max(oev0, p.val("opev") - 1))],
                      inc=f"ldx{buf}")
                emit_ln_tile(p, xt[buf][:, :], xn_bf[:, :], scr, mu, sumsq, rstd,
                             None, epsb,
                             waits=[(f"ldx{buf}", p.val(f"ldx{buf}"))], inc="lnd")
                for k in range(8):
                    p.step("tensor",
                           lambda e, k=k: e.transpose(
                               tr_ps[k % 2], xn_bf[:, k * 128:(k + 1) * 128],
                               sident[:, :]),
                           waits=([("lnd", p.val("lnd"))] if k == 0 else
                                  [("trev", p.val("trev") - 1)]) +
                                 (wc_ready if it == 0 and k == 0 else []),
                           inc="trmm")
                    p.step("vector",
                           lambda e, k=k: e.tensor_copy(
                               xn_fm[:, k * 128:(k + 1) * 128], tr_ps[k % 2]),
                           waits=[("trmm", p.val("trmm"))], inc="trev")
                for h in range(2):
                    for k in range(8):
                        p.step("tensor",
                               lambda e, h=h, k=k: e.matmul(
                                   mm_ps[h][:, :], xn_fm[:, k * 128:(k + 1) * 128],
                                   wc[k][:, h * 512:(h + 1) * 512],
                                   start=(k == 0), stop=(k == 7)),
                               waits=([("trev", p.val("trev"))] if k == 0 else []) +
                                     ([("opev", p.val("opev"))]
                                      if (k == 0 and it > 0) else []),
                               inc=("mmd" if k == 7 else None))
                for h in range(2):
                    p.step("vector",
                           lambda e, h=h, buf=buf: e.scalar_tensor_tensor(
                               out=out_t[buf][:, h * 512:(h + 1) * 512],
                               in0=mm_ps[h][:, :], scalar=0.5,
                               in1=xt[buf][:, h * 512:(h + 1) * 512],
                               op0=ALU.mult, op1=ALU.add),
                           waits=[("mmd", p.val("mmd"))] +
                                 ([(f"osx{buf}", p.val(f"osx{buf}"))]
                                  if h == 0 else []),
                           inc=("bsg" if h == 1 else None))
                p.wait("vector", [("bsg", p.val("bsg"))])
                p.step("vector",
                       lambda e, buf=buf: e.tensor_add(out_t[buf][:, :],
                                                       out_t[buf][:, :],
                                                       bias_bc[:, :]),
                       inc="opev")
                p.dma("scalar", x_dst[it * 128:(it + 1) * 128, :], out_t[buf][:, :],
                      waits=[("opev", p.val("opev"))], inc=f"osx{buf}")
            p.wait("sync", [("osx0", p.val("osx0")), ("osx1", p.val("osx1"))])

    def ph_l1proj(x_src, prev_bv):
        with contextlib.ExitStack() as pes:
            sb = lambda n_, sh, dt=F32: pes.enter_context(
                nc.sbuf_tensor(f"J{n_}", sh, dt))
            psum = lambda n_, sh, dt=F32: pes.enter_context(
                nc.psum_tensor(f"J{n_}", sh, dt))
            wq_sb = [sb(f"wq{i}", [128, D], BF16) for i in range(8)]
            wk_sb = [sb(f"wk{i}", [128, D], BF16) for i in range(8)]
            wv_sb = [sb(f"wv{i}", [128, D], BF16) for i in range(8)]
            wld = sb("wld", [128, D])
            for dram, tiles in ((C["l1wq"], wq_sb), (C["l1wk"], wk_sb),
                                (C["l1wv"], wv_sb)):
                for i in range(8):
                    p.dma("scalar", wld[:, :], dram[i * 128:(i + 1) * 128, :],
                          waits=[("wtcast", p.val("wtcast"))], inc="wtld")
                    p.step("vector",
                           lambda e, t=tiles[i]: e.tensor_copy(t[:, :], wld[:, :]),
                           waits=[("wtld", p.val("wtld"))], inc="wtcast")
            xt = [sb(f"xt{i}", [128, D]) for i in range(2)]
            scr = sb("scr", [128, D])
            xn_bf = sb("xn_bf", [128, D], BF16)
            xn_fm = [sb(f"xnfm{j}", [128, HTS], BF16) for j in range(8)]
            mu = sb("mu", [128, 2]); sumsq = sb("sumsq", [128, 2])
            rstd = sb("rstd", [128, 2])
            tr_ps = [psum(f"trp{i}", [128, 1024], BF16) for i in range(2)]
            lnd0 = p.val("lnd")
            for it in range(NT):
                buf = it % 2
                p.dma("sync", xt[buf][:, :], x_src[it * 128:(it + 1) * 128, :],
                      waits=[("lnd", max(lnd0, p.val("lnd") - 1))], inc=f"ldx{buf}")
                emit_ln_tile(p, xt[buf][:, :], xn_bf[:, :], scr, mu, sumsq, rstd,
                             None, epsb,
                             waits=[(f"ldx{buf}", p.val(f"ldx{buf}"))], inc="lnd")
                for k in range(8):
                    p.step("tensor",
                           lambda e, k=k: e.transpose(
                               tr_ps[k % 2][:, 0:128], xn_bf[:, k * 128:(k + 1) * 128],
                               sident[:, :]),
                           waits=([("lnd", p.val("lnd")), ("ld", 16)] if k == 0 else
                                  [("trev", p.val("trev") - 1)]),
                           inc="trmm")
                    p.step("vector",
                           lambda e, k=k, it=it: e.tensor_copy(
                               xn_fm[k][:, it * 128:(it + 1) * 128],
                               tr_ps[k % 2][:, 0:128]),
                           waits=[("trmm", p.val("trmm"))], inc="trev")
            prq = [("wtcast", p.val("wtcast")), ("trev", p.val("trev"))]
            pmm0 = p.val("pmm")
            mm_ps = [psum(f"mmp{i}", [128, 512]) for i in range(4)]
            ev = sb("ev", [128, 512], BF16)
            evv = sb("evv", [128, 512], BF16)
            for tiles, out in ((wq_sb, qf_s), (wk_sb, kf_s)):
                for mtile in range(8):
                    for nchunk in range(HTS // 512):
                        pb = mm_ps[p.val("pmm") % 4]
                        for kk in range(8):
                            p.step("tensor",
                                   lambda e, pb=pb, kk=kk, mtile=mtile,
                                          nchunk=nchunk, tiles=tiles: e.matmul(
                                       pb[:, :],
                                       tiles[kk][:, mtile * 128:(mtile + 1) * 128],
                                       xn_fm[kk][:, nchunk * 512:(nchunk + 1) * 512],
                                       start=(kk == 0), stop=(kk == 7)),
                                   waits=(prq if p.val("pmm") == pmm0 and kk == 0
                                          else ([("pev", max(0, p.val("pev") - 2))]
                                                if kk == 0 else [])),
                                   inc=("pmm" if kk == 7 else None))
                        g = p.val("pev")
                        slot = g % 2
                        buf_ev = ev if slot else evv
                        p.step("vector",
                               lambda e, pb=pb, buf_ev=buf_ev: e.tensor_copy(
                                   buf_ev[:, :], pb[:, :]),
                               waits=[("pmm", p.val("pmm")),
                                      (f"pst{slot}", p.val(f"pst{slot}"))],
                               inc="pev")
                        p.dma("sync",
                              out[mtile * 128:(mtile + 1) * 128,
                                  nchunk * 512:(nchunk + 1) * 512],
                              buf_ev[:, :], waits=[("pev", p.val("pev"))],
                              inc=f"pst{slot}")
            for ttile in range(NT):
                for half in range(2):
                    pb = mm_ps[p.val("pmm") % 4]
                    for kk in range(8):
                        p.step("tensor",
                               lambda e, pb=pb, kk=kk, ttile=ttile, half=half:
                               e.matmul(
                                   pb[:, :],
                                   xn_fm[kk][:, ttile * 128:(ttile + 1) * 128],
                                   wv_sb[kk][:, half * 512:(half + 1) * 512],
                                   start=(kk == 0), stop=(kk == 7)),
                               waits=([("pev", max(0, p.val("pev") - 2))]
                                      if kk == 0 else []),
                               inc=("pmm" if kk == 7 else None))
                    g = p.val("pev")
                    slot = g % 2
                    buf_ev = ev if slot else evv
                    p.step("vector",
                           lambda e, pb=pb, buf_ev=buf_ev: e.tensor_copy(
                               buf_ev[:, :], pb[:, :]),
                           waits=[("pmm", p.val("pmm")),
                                  (f"pst{slot}", p.val(f"pst{slot}"))], inc="pev")
                    p.dma("sync",
                          vt_s[ttile * 128:(ttile + 1) * 128,
                               half * 512:(half + 1) * 512],
                          buf_ev[:, :], waits=[("pev", p.val("pev"))],
                          inc=f"pst{slot}")
        pardma("sync", sh_k[0], kf_s, sh_k[1], kf_s,
               waits=[("pst0", p.val("pst0")), ("pst1", p.val("pst1")),
                      ("bard", prev_bv)], inc="shst")
        pardma("sync", sh_v[0], vt_s, sh_v[1], vt_s,
               waits=[("shst", p.val("shst"))], inc="shst")
        return barrier(waits=[("shst", p.val("shst"))])

    def ph_l1attn(x_src, x_dst, bv_):
        NQB, NKC = 4, 8
        with contextlib.ExitStack() as pes:
            sb = lambda n_, sh, dt=F32: pes.enter_context(
                nc.sbuf_tensor(f"K{n_}", sh, dt))
            psum = lambda n_, sh, dt=F32: pes.enter_context(
                nc.psum_tensor(f"K{n_}", sh, dt))
            ones1 = sb("ones1", [128, 1], BF16)
            p.step("vector", lambda e: e.memset(ones1[:, :], 1.0), inc="vones")
            qf_sb = [[sb(f"qf{s}_{dk}", [128, 1024], BF16) for dk in range(8)]
                     for s in range(2)]
            kf_sb = [[sb(f"kf{s}_{dk}", [128, 1024], BF16) for dk in range(8)]
                     for s in range(2)]
            vt_sb = [[sb(f"vt{s}_{kt}", [128, 1024], BF16) for kt in range(8)]
                     for s in range(2)]
            st_sb = [sb(f"st{kt}", [128, 1024], BF16) for kt in range(8)]
            o_sb = [sb(f"osb{qt}", [128, 1024]) for qt in range(8)]
            xt_sb = [sb(f"xts{qt}", [128, D]) for qt in range(8)]
            rec = sb("rec", [128, 8])
            outb = [sb(f"outb{i}", [128, D]) for i in range(2)]
            st_ps = [psum(f"stps{i}", [128, 512]) for i in range(2)]
            av_ps = [psum(f"avps{i}", [128, 512]) for i in range(4)]
            dn_ps = psum("dnps", [128, 8])
            qk0 = p.val("qkv")
            avv0 = p.val("avv")
            for qb in range(NQB):
                qs = qb % 2
                for dk in range(8):
                    p.dma("sync", qf_sb[qs][dk][:, :],
                          qf_s[dk * 128:(dk + 1) * 128, qb * 1024:(qb + 1) * 1024],
                          waits=([("qkv", max(qk0, p.val("qkv") - 16 * NKC))]
                                 if qb >= 2 else []),
                          inc=f"ldq{qs}")
                for qt in range(8):
                    p.dma("scalar", xt_sb[qt][:, :],
                          x_src[qb * 1024 + qt * 128:qb * 1024 + (qt + 1) * 128, :],
                          waits=([("nrmv", p.val("nrmv"))] if qb >= 1 else []),
                          inc="lqx0")
                for kc in range(NKC):
                    g = qb * NKC + kc
                    ks = g % 2
                    for dk in range(8):
                        p.dma("sync", kf_sb[ks][dk][:, :],
                              sh_k[kc // 4][dk * 128:(dk + 1) * 128,
                                            (kc % 4) * 1024:(kc % 4 + 1) * 1024],
                              waits=[("bard", bv_)] +
                                    ([("qkv", max(qk0, p.val("qkv") - 16))]
                                     if g >= 2 else []),
                              inc=f"ldk{ks}")
                    for kt in range(8):
                        p.dma("scalar", vt_sb[ks][kt][:, :],
                              sh_v[kc // 4][(kc % 4) * 1024 + kt * 128:
                                            (kc % 4) * 1024 + (kt + 1) * 128, :],
                              waits=[("bard", bv_)] +
                                    ([("avv", max(avv0, p.val("avv") - 16))]
                                     if g >= 2 else []),
                              inc=f"ldv{ks}")
                    for kt in range(8):
                        for qn in range(2):
                            pb = st_ps[p.val("qkv") % 2]
                            for dk in range(8):
                                w = []
                                if dk == 0:
                                    w = [(f"ldk{ks}", p.val(f"ldk{ks}")),
                                         (f"ldq{qs}", p.val(f"ldq{qs}")),
                                         ("expd", max(qk0, p.val("qkv") - 1))]
                                p.step("tensor",
                                       lambda e, pb=pb, dk=dk, kt=kt, qn=qn, ks=ks,
                                              qs=qs: e.matmul(
                                           pb[:, :],
                                           kf_sb[ks][dk][:, kt * 128:(kt + 1) * 128],
                                           qf_sb[qs][dk][:, qn * 512:(qn + 1) * 512],
                                           start=(dk == 0), stop=(dk == 7)),
                                       waits=w, inc=("qkv" if dk == 7 else None))
                            p.step("scalar",
                                   lambda e, pb=pb, kt=kt, qn=qn: e.activation(
                                       out=st_sb[kt][:, qn * 512:(qn + 1) * 512],
                                       in_=pb[:, :], func=AF.Exp, scale=1.0 / 32.0),
                                   waits=[("qkv", p.val("qkv"))] +
                                         ([("avch", p.val("avch"))]
                                          if g >= 1 and kt == 0 and qn == 0 else []),
                                   inc="expd")
                    for qt in range(8):
                        for kt in range(8):
                            first = (kc == 0 and qt == 0 and kt == 0)
                            last = (kc == NKC - 1 and qt == 7 and kt == 7)
                            w = []
                            if qt == 0 and kt == 0:
                                w = [("expd", p.val("expd")),
                                     ("vones", p.val("vones"))]
                                if first and qb >= 1:
                                    w.append(("recq", p.val("recq")))
                            p.step("tensor",
                                   lambda e, qt=qt, kt=kt, first=first, last=last:
                                   e.matmul(dn_ps[:, qt:qt + 1],
                                            st_sb[kt][:, qt * 128:(qt + 1) * 128],
                                            ones1[:, :], start=first, stop=last,
                                            skip_group_check=True),
                                   waits=w, inc=None)
                    for qt in range(8):
                        for dv in range(2):
                            pb = av_ps[p.val("avv") % 4]
                            for kt in range(8):
                                w = []
                                if kt == 0:
                                    w = [(f"ldv{ks}", p.val(f"ldv{ks}")),
                                         ("pev", max(0, p.val("pev") - 3))]
                                    if qt == 0 and dv == 0:
                                        w.append(("expd", p.val("expd")))
                                        if kc == 0 and qb >= 1:
                                            w.append(("nrmv", p.val("nrmv")))
                                p.step("tensor",
                                       lambda e, pb=pb, kt=kt, qt=qt, dv=dv, ks=ks:
                                       e.matmul(pb[:, :],
                                                st_sb[kt][:, qt * 128:(qt + 1) * 128],
                                                vt_sb[ks][kt][:, dv * 512:(dv + 1) * 512],
                                                start=(kt == 0), stop=(kt == 7)),
                                       waits=w, inc=("avv" if kt == 7 else None))
                            if kc == 0:
                                p.step("vector",
                                       lambda e, pb=pb, qt=qt, dv=dv: e.tensor_copy(
                                           o_sb[qt][:, dv * 512:(dv + 1) * 512],
                                           pb[:, :]),
                                       waits=[("avv", p.val("avv"))], inc="pev")
                            else:
                                p.step("vector",
                                       lambda e, pb=pb, qt=qt, dv=dv: e.tensor_add(
                                           o_sb[qt][:, dv * 512:(dv + 1) * 512],
                                           o_sb[qt][:, dv * 512:(dv + 1) * 512],
                                           pb[:, :]),
                                       waits=[("avv", p.val("avv"))], inc="pev")
                    p.step("vector", lambda e: e.tensor_copy(ones1[:, :], ones1[:, :]),
                           waits=[], inc="avch")
                p.step("vector", lambda e: e.reciprocal(rec[:, :], dn_ps[:, :]),
                       waits=[("qkv", p.val("qkv")), ("expd", p.val("expd")),
                              ("pev", p.val("pev"))], inc="recq")
                for qt in range(8):
                    idx = qb * 8 + qt
                    slot = idx % 2
                    ob = outb[slot]
                    p.step("vector",
                           lambda e, qt=qt, ob=ob: e.scalar_tensor_tensor(
                               out=ob[:, :], in0=o_sb[qt][:, :],
                               scalar=rec[:, qt:qt + 1],
                               in1=xt_sb[qt][:, :], op0=ALU.mult, op1=ALU.add),
                           waits=[("recq", p.val("recq")), ("lqx0", p.val("lqx0")),
                                  (f"staf{slot}", p.val(f"staf{slot}"))],
                           inc="nrmv")
                    p.dma("sync",
                          x_dst[qb * 1024 + qt * 128:qb * 1024 + (qt + 1) * 128, :],
                          ob[:, :], waits=[("nrmv", p.val("nrmv"))],
                          inc=f"staf{slot}")
            p.wait("sync", [("staf0", p.val("staf0")), ("staf1", p.val("staf1"))])

    st = {"bv": None, "bva": None, "bvb": None}
    seq = [
        ("l0projA", lambda: st.__setitem__("bv", ph_l0proj(x_in, "A_")), x_in),
        ("l0attnA", lambda: st.__setitem__("bva", ph_l0attn(st["bv"], "A_")),
         x_in),
        ("l0oprojA", lambda: ph_l0oproj(x_in, x_a, "A_"), x_a),
        ("mamba", lambda: ph_mamba(x_a, x_b), x_b),
        ("l0projB",
         lambda: st.__setitem__("bv", ph_l0proj(x_b, "B_", prev_bv=st["bva"])),
         x_b),
        ("l0attnB", lambda: st.__setitem__("bvb", ph_l0attn(st["bv"], "B_")),
         x_b),
        ("l0oprojB", lambda: ph_l0oproj(x_b, x_a, "B_"), x_a),
        ("l1ffnF", lambda: ph_l1ffn(x_a, x_b, "F_"), x_b),
        ("l1proj", lambda: st.__setitem__("bv", ph_l1proj(x_b, prev_bv=st["bvb"])),
         x_b),
        ("l1attn", lambda: ph_l1attn(x_b, x_a, st["bv"]), x_a),
        ("l1ffnG", lambda: ph_l1ffn(x_a, xo, "G_"), xo),
    ]
    stop_after = C.get("stop_after")
    n = len(seq) if stop_after is None else stop_after
    cur = x_in
    for name, fn, xaft in seq[:n]:
        fn()
        p.fence()
        cur = xaft
    if n < len(seq):
        p.dma("sync", xo, cur, inc="gld")
        p.wait("sync", [("gld", p.val("gld"))])


_FUSED_CACHE = {}


def _fused_in_maps(inp, nonce, cores=range(NCORES)):
    x = inp["L0_attn_x"]
    DI_ = 2048
    win = inp["L0_attn_W_in"]
    in_maps = []
    for c in cores:
        par = c % 2
        b = c // 2
        ch = slice(par * 1024, (par + 1) * 1024)
        ch2 = slice((1 - par) * 1024, (2 - par) * 1024)
        im = dict(
            x_in=np.ascontiguousarray(x[b, par * HTS:(par + 1) * HTS]),
            par_in=np.array([[par]], np.int32),
            nonce_in=nonce,
            ident=_IDENT,
            nw=inp["L0_ffn_norm_w"].reshape(1, -1),
            nb=inp["L0_ffn_norm_b"].reshape(1, -1),
            bq=inp["L0_ffn_q_b"].reshape(1, -1),
            bk=inp["L0_ffn_k_b"].reshape(1, -1),
            bv=inp["L0_ffn_v_b"].reshape(1, -1),
            win_u=np.ascontiguousarray(win[:, par * 1024:(par + 1) * 1024]),
            win_r=np.ascontiguousarray(win[:, DI_ + par * 1024:
                                           DI_ + (par + 1) * 1024]),
            wcv_h=np.ascontiguousarray(inp["L0_attn_W_conv"][ch]),
            wx_h=np.ascontiguousarray(inp["L0_attn_W_x"][ch]),
            wdt_h=np.ascontiguousarray(inp["L0_attn_W_dt"][:, ch]),
            bdt_h=np.ascontiguousarray(inp["L0_attn_b_dt"][ch]).reshape(1, -1),
            alog_h=np.ascontiguousarray(inp["L0_attn_A_log"][ch]),
            dpar_h=np.ascontiguousarray(inp["L0_attn_D_param"][ch]).reshape(1, -1),
            wout_m=np.ascontiguousarray(inp["L0_attn_W_out"][ch]),
            wout_t=np.ascontiguousarray(inp["L0_attn_W_out"][ch2]),
            g1r=inp["L1_ffn_G1_R"], g2r=inp["L1_ffn_G2_R"],
            g1i=inp["L1_ffn_G1_I"], g2i=inp["L1_ffn_G2_I"],
            brl=inp["L1_ffn_bias_real"].reshape(1, 512),
            bim=inp["L1_ffn_bias_imag"].reshape(1, 512),
            l1wq=inp["L1_attn_W_Q"], l1wk=inp["L1_attn_W_K"],
            l1wv=inp["L1_attn_W_V"],
        )
        for s in range(2):
            m = 1024 >> (2 * par + s)
            im[f"wq{s}"] = _pad_w(inp["L0_ffn_q_w"], m)
            im[f"wk{s}"] = _pad_w(inp["L0_ffn_k_w"], m)
            im[f"wv{s}"] = _pad_w(inp["L0_ffn_v_w"], m)
            im[f"ow{s}"] = _pad_w(inp["L0_ffn_o_w"], m, rows_too=True)
            im[f"ob{s}"] = np.concatenate(
                [inp["L0_ffn_o_b"][:m], np.zeros(D - m, np.float32)]).reshape(1, -1)
        in_maps.append(im)
    return in_maps


def kernel_fused(inputs):
    inp = {k: np.ascontiguousarray(np.asarray(v, dtype=np.float32))
           for k, v in inputs.items()}
    _ = inp["L0_ffn_router_prob"].shape
    if "nc" not in _FUSED_CACHE:
        _FUSED_CACHE["nc"] = build_fused()
    nc = _FUSED_CACHE["nc"]
    nonce = np.array([[np.random.randint(1, 2 ** 31 - 1)]], np.int32)
    in_maps = _fused_in_maps(inp, nonce)
    res = run_bass_kernel_spmd(nc, in_maps, core_ids=list(range(NCORES)))
    out = np.empty((B, T, D), np.float32)
    for c in range(NCORES):
        par = c % 2
        b = c // 2
        out[b, par * HTS:(par + 1) * HTS] = res.results[c]["xo"]
    return out
